# version 10
# baseline (speedup 1.0000x reference)
import numpy as np
import ml_dtypes
import concourse.bass as bass
import concourse.mybir as mybir
from concourse.bass_utils import run_bass_kernel_spmd

F32 = mybir.dt.float32
BF16 = mybir.dt.bfloat16
AF = mybir.ActivationFunctionType
ALU = mybir.AluOpType
AX = mybir.AxisListType

N_CORES = 8
B_FULL, T_SEQ, D = 16, 2048, 1024
B_CORE = B_FULL // N_CORES
NTOK = B_CORE * T_SEQ
NT = 512
NTILES = NTOK // NT

ENGINES = ("tensor", "vector", "scalar", "gpsimd", "sync")
QUEUES = {"qs": "sync", "qa": "scalar", "qg": "gpsimd"}


class _Op:
    __slots__ = ("emit", "waits", "inc", "label")

    def __init__(self, emit, waits, inc, label):
        self.emit = emit
        self.waits = waits
        self.inc = inc
        self.label = label


class Buf:

    __slots__ = ("name", "t", "size", "writes", "reads", "ap", "is_psum")

    def __init__(self, name, tensor, size, is_psum=False):
        self.name = name
        self.t = tensor
        self.size = size
        self.is_psum = is_psum
        self.writes = []
        self.reads = []


def _overlap(lo, hi, rec):
    return not (hi <= rec[0] or lo >= rec[1])


class Prog:
    def __init__(self, nc):
        self.nc = nc
        self.ops = {e: [] for e in ENGINES}
        self.sem_counts = {}
        self.sem_handles = {}
        for e in ENGINES:
            self.sem_counts["m_" + e] = 0
        for q in QUEUES:
            self.sem_counts[q] = 0
        self.known = {e: {} for e in ENGINES}

    def alloc_sems(self, stack):
        for name in self.sem_counts:
            self.sem_handles[name] = stack.enter_context(
                self.nc.semaphore(name))

    def _gather(self, eng, reads, writes):
        waits = {}

        def need(tkt):
            if tkt is None:
                return
            sem, val = tkt
            if eng == "tensor" and sem == "m_tensor":
                return
            if waits.get(sem, 0) < val:
                waits[sem] = val

        myeng_sem = "m_" + eng
        for (buf, lo, hi) in reads:
            for rec in buf.writes:
                if _overlap(lo, hi, rec):
                    need(rec[2])
            if buf.is_psum:
                for rec in buf.reads:
                    if rec[2] is not None and rec[2][0] != myeng_sem and                             _overlap(lo, hi, rec):
                        need(rec[2])
        for (buf, lo, hi) in writes:
            for rec in buf.writes:
                if _overlap(lo, hi, rec):
                    need(rec[2])
            for rec in buf.reads:
                if _overlap(lo, hi, rec):
                    need(rec[2])
        kn = self.known[eng]
        return [(s, v) for s, v in sorted(waits.items()) if kn.get(s, 0) < v]

    def _commit(self, eng, reads, writes, tkt, waits):
        kn = self.known[eng]
        for sem, val in waits:
            if kn.get(sem, 0) < val:
                kn[sem] = val
        for (buf, lo, hi) in writes:
            buf.writes = [r for r in buf.writes
                          if not (lo <= r[0] and r[1] <= hi)]
            buf.writes.append([lo, hi, tkt])
            buf.reads = [r for r in buf.reads
                         if not (lo <= r[0] and r[1] <= hi)]
        for (buf, lo, hi) in reads:
            rl = buf.reads
            if rl and rl[-1][0] == lo and rl[-1][1] == hi:
                rl[-1][2] = tkt
            else:
                rl.append([lo, hi, tkt])

    def op(self, eng, emit, reads=(), writes=(), label=""):
        waits = self._gather(eng, reads, writes)
        sem = "m_" + eng
        self.sem_counts[sem] += 1
        tkt = (sem, self.sem_counts[sem])
        self.ops[eng].append(_Op(emit, waits, (sem, 1), label))
        self._commit(eng, reads, writes, tkt, waits)
        return tkt

    def mm_group(self, psum_buf, parts, label=""):
        eng = "tensor"
        allreads = []
        for (_, rds) in parts:
            allreads.extend(rds)
        wr = [(psum_buf, 0, psum_buf.size)]
        waits = self._gather(eng, allreads, wr)
        self.sem_counts["m_tensor"] += 1
        tkt = ("m_tensor", self.sem_counts["m_tensor"])
        n = len(parts)
        for i, (emit, _) in enumerate(parts):
            self.ops[eng].append(_Op(
                emit, waits if i == 0 else [],
                ("m_tensor", 1) if i == n - 1 else None, label))
        self._commit(eng, allreads, wr, tkt, waits)
        return tkt

    def pe_op(self, emit, reads=(), writes=(), label=""):
        waits = self._gather("tensor", reads, writes)
        self.sem_counts["m_tensor"] += 1
        tkt = ("m_tensor", self.sem_counts["m_tensor"])
        self.ops["tensor"].append(_Op(emit, waits, ("m_tensor", 1), label))
        self._commit("tensor", reads, writes, tkt, waits)
        return tkt

    def dma_batch(self, q, transfers, label=""):
        eng = QUEUES[q]
        prev = self.sem_counts[q]
        waits = {}
        if prev and self.known[eng].get(q, 0) < prev:
            waits[q] = prev
        allreads, allwrites = [], []
        norm = []
        for t in transfers:
            out_ap, in_ap, rds, wrs = t[0], t[1], t[2], t[3]
            kw = t[4] if len(t) > 4 else {}
            norm.append((out_ap, in_ap, kw))
            allreads.extend(rds)
            allwrites.extend(wrs)
        for sem, val in self._gather(eng, allreads, allwrites):
            if waits.get(sem, 0) < val:
                waits[sem] = val
        waits = sorted(waits.items())
        total = prev + 16 * len(norm)
        self.sem_counts[q] = total
        tkt = (q, total)
        for i, (out_ap, in_ap, kw) in enumerate(norm):
            def emit(e, out_ap=out_ap, in_ap=in_ap, kw=kw):
                return e.dma_start(out=out_ap, in_=in_ap, **kw)
            self.ops[eng].append(
                _Op(emit, waits if i == 0 else [], (q, 16), label))
        self._commit(eng, allreads, allwrites, tkt, waits)
        kn = self.known[eng]
        if kn.get(q, 0) < prev:
            kn[q] = prev
        return tkt

    def barrier(self, label="bar"):
        targets = {}
        for e in ENGINES:
            sem = "m_" + e
            cur = self.sem_counts[sem]
            kn = self.known[e]
            waits = []
            if cur and kn.get(sem, 0) < cur:
                waits.append((sem, cur))
                kn[sem] = cur
            self.sem_counts[sem] += 1
            targets[sem] = self.sem_counts[sem]
            self.ops[e].append(
                _Op(lambda eng: eng.nop(), waits, (sem, 1), label + "-i"))
        for q in QUEUES:
            targets[q] = self.sem_counts[q]
        for e in ENGINES:
            waits = []
            kn = self.known[e]
            for sem, val in sorted(targets.items()):
                if val and kn.get(sem, 0) < val:
                    waits.append((sem, val))
                    kn[sem] = val
            self.ops[e].append(
                _Op(lambda eng: eng.nop(), waits, None, label + "-w"))

    def finalize_and_emit(self, block):
        waits = []
        kn = self.known["sync"]
        for sem, tot in sorted(self.sem_counts.items()):
            if sem == "m_sync":
                continue
            if tot and kn.get(sem, 0) < tot:
                waits.append((sem, tot))
        self.ops["sync"].append(
            _Op(lambda eng: eng.nop(), waits, None, "final-drain"))
        handles = self.sem_handles

        def run(eng_name):
            def go(engine):
                for op in self.ops[eng_name]:
                    for sem, val in op.waits:
                        engine.wait_ge(handles[sem], val)
                    ins = op.emit(engine)
                    if op.inc is not None:
                        ins.then_inc(handles[op.inc[0]], op.inc[1])
            return go

        block.tensor(run("tensor"))
        block.vector(run("vector"))
        block.scalar(run("scalar"))
        block.gpsimd(run("gpsimd"))
        block.sync(run("sync"))

    def stats(self):
        return {e: len(self.ops[e]) for e in ENGINES}





LN_EPS = 1e-5


class Env:

    def __init__(self, prog, stack):
        self.prog = prog
        self.stack = stack
        self.bufs = {}

    def sbuf(self, name, shape, dtype):
        t = self.stack.enter_context(self.prog.nc.sbuf_tensor(name, list(shape), dtype))
        b = Buf(name, t, shape[1])
        self.bufs[name] = b
        return b

    def psum(self, name, shape, dtype):
        t = self.stack.enter_context(self.prog.nc.psum_tensor(name, list(shape), dtype))
        b = Buf(name, t, shape[1], is_psum=True)
        self.bufs[name] = b
        return b

    def dram(self, name, shape, dtype, kind):
        t = self.prog.nc.dram_tensor(name, list(shape), dtype, kind=kind)
        b = Buf(name, t, int(np.prod(shape)))
        b.ap = t.ap()
        self.bufs[name] = b
        return b


def host_consts():
    c = {}
    c["c_ident"] = np.eye(128, dtype=np.float32)
    c["c_lncol"] = np.stack(
        [np.full(128, -1.0 / D, np.float32), np.full(128, 1.0 / D, np.float32)],
        axis=1)
    c["c_ones_col"] = np.ones((128, 1), np.float32)
    c["c_ones_row"] = np.ones((1, 512), np.float32)
    misc = np.zeros((128, 4), np.float32)
    misc[:, 0] = 1e-5
    misc[:, 1] = 1e-6
    misc[:, 2] = 1e-8
    c["c_misc"] = misc
    return c


def load_consts(prog, env):
    nc = prog.nc
    d_id = env.dram("c_ident", (128, 128), F32, "ExternalInput")
    d_ln = env.dram("c_lncol", (128, 2), F32, "ExternalInput")
    d_oc = env.dram("c_ones_col", (128, 1), F32, "ExternalInput")
    d_or = env.dram("c_ones_row", (1, 512), F32, "ExternalInput")
    d_mi = env.dram("c_misc", (128, 4), F32, "ExternalInput")
    ident = env.sbuf("identb", (128, 128), BF16)
    identf = env.sbuf("identf", (128, 128), F32)
    lncol = env.sbuf("lncol", (128, 2), BF16)
    ones_col = env.sbuf("ones_col", (128, 1), BF16)
    ones_row = env.sbuf("ones_row", (1, 512), BF16)
    misc = env.sbuf("misc", (128, 4), F32)
    prog.dma_batch("qg", [
        (ident.t[:], d_id.ap, [], [(ident, 0, 128)]),
        (identf.t[:], d_id.ap, [], [(identf, 0, 128)]),
        (lncol.t[:], d_ln.ap, [], [(lncol, 0, 2)]),
        (ones_col.t[:], d_oc.ap, [], [(ones_col, 0, 1)]),
        (ones_row.t[:], d_or.ap, [], [(ones_row, 0, 512)]),
        (misc.t[:], d_mi.ap, [], [(misc, 0, 4)]),
    ], label="consts")
    return dict(ident=ident, identf=identf, lncol=lncol, ones_col=ones_col,
                ones_row=ones_row, misc=misc)


def x_tile_ap(xdram_buf, t0, nt=NT, f0=0, nf=D):
    ap = xdram_buf.t.ap()[f0:f0 + nf, t0:t0 + nt]
    return ap.rearrange("(c p) t -> p c t", p=128)


def sb3(buf, nchunks, nt=NT, col0=0):
    return buf.t[:, col0:col0 + nchunks * nt].rearrange(
        "p (c t) -> p c t", t=nt)


def emit_ln_tile(prog, cst, xdram, t0, xbuf, xcol, xcol2, xln, lnscr, ps, label="ln"):
    nc = prog.nc
    prog.dma_batch("qs", [
        (sb3(xbuf, 8), x_tile_ap(xdram, t0), [(xdram, t0, t0 + NT)],
         [(xbuf, 0, 8 * NT)]),
    ], label=label + "-ld")
    prog.op("vector",
            lambda e: e.tensor_copy(xln.t[:, 0:8 * NT], xbuf.t[:, 0:8 * NT]),
            reads=[(xbuf, 0, 8 * NT)], writes=[(xln, 0, 8 * NT)],
            label=label + "-cast")
    prog.mm_group(ps["stat1"], [
        (lambda e, fc=fc: e.matmul(ps["stat1"].t[0:1, 0:NT],
                                   cst["lncol"].t[:, 0:1],
                                   xln.t[:, fc * NT:(fc + 1) * NT],
                                   start=(fc == 0), stop=(fc == 7)),
         [(xln, fc * NT, (fc + 1) * NT), (cst["lncol"], 0, 1)])
        for fc in range(8)], label=label + "-mmu")
    prog.op("vector",
            lambda e: e.tensor_mul(lnscr.t[:, 0:8 * NT],
                                   xln.t[:, 0:8 * NT], xln.t[:, 0:8 * NT]),
            reads=[(xln, 0, 8 * NT)], writes=[(lnscr, 0, 8 * NT)],
            label=label + "-sq")
    prog.mm_group(ps["stat2"], [
        (lambda e, fc=fc: e.matmul(ps["stat2"].t[0:1, 0:NT],
                                   cst["lncol"].t[:, 1:2],
                                   lnscr.t[:, fc * NT:(fc + 1) * NT],
                                   start=(fc == 0), stop=(fc == 7)),
         [(lnscr, fc * NT, (fc + 1) * NT), (cst["lncol"], 0, 2)])
        for fc in range(8)], label=label + "-mmsq")
    prog.op("vector", lambda e: e.tensor_copy(xcol.t[0:1, 0:NT],
                                              ps["stat1"].t[0:1, :]),
            reads=[(ps["stat1"], 0, NT)], writes=[(xcol, 0, NT)],
            label=label + "-negmu")
    prog.op("vector",
            lambda e: e.tensor_mul(xcol.t[0:1, NT:2 * NT],
                                   xcol.t[0:1, 0:NT], xcol.t[0:1, 0:NT]),
            reads=[(xcol, 0, NT)], writes=[(xcol, NT, 2 * NT)],
            label=label + "-mu2")
    prog.op("vector",
            lambda e: e.tensor_tensor(xcol.t[0:1, NT:2 * NT],
                                      ps["stat2"].t[0:1, :],
                                      xcol.t[0:1, NT:2 * NT],
                                      op=ALU.subtract),
            reads=[(ps["stat2"], 0, NT), (xcol, NT, 2 * NT)],
            writes=[(xcol, NT, 2 * NT)], label=label + "-var")
    prog.op("scalar",
            lambda e: e.activation(out=xcol.t[0:1, NT:2 * NT],
                                   in_=xcol.t[0:1, NT:2 * NT],
                                   func=AF.Ln, bias=cst["misc"].t[0:1, 0:1]),
            reads=[(xcol, NT, 2 * NT), (cst["misc"], 0, 1)],
            writes=[(xcol, NT, 2 * NT)],
            label=label + "-lnv")
    prog.op("scalar",
            lambda e: e.activation(out=xcol.t[0:1, NT:2 * NT],
                                   in_=xcol.t[0:1, NT:2 * NT],
                                   func=AF.Exp, scale=-0.5),
            reads=[(xcol, NT, 2 * NT)], writes=[(xcol, NT, 2 * NT)],
            label=label + "-rsq")
    prog.op("vector",
            lambda e: e.tensor_mul(xcol.t[0:1, 0:NT], xcol.t[0:1, 0:NT],
                                   xcol.t[0:1, NT:2 * NT]),
            reads=[(xcol, 0, 2 * NT)], writes=[(xcol, 0, NT)],
            label=label + "-nmr")
    prog.op("vector", lambda e: e.tensor_copy(xcol2.t[0:1, 0:NT],
                                              xcol.t[0:1, NT:2 * NT]),
            reads=[(xcol, NT, 2 * NT)], writes=[(xcol2, 0, NT)],
            label=label + "-rb16")
    prog.op("vector", lambda e: e.tensor_copy(xcol2.t[0:1, NT:2 * NT],
                                              xcol.t[0:1, 0:NT]),
            reads=[(xcol, 0, NT)], writes=[(xcol2, NT, 2 * NT)],
            label=label + "-nm16")
    prog.mm_group(ps["bc1"], [
        (lambda e: e.matmul(ps["bc1"].t[:], cst["ones_row"].t[0:1, 0:128],
                            xcol2.t[0:1, 0:NT], start=True, stop=True),
         [(xcol2, 0, NT), (cst["ones_row"], 0, 128)])], label=label + "-bcr")
    prog.mm_group(ps["bc2"], [
        (lambda e: e.matmul(ps["bc2"].t[:], cst["ones_row"].t[0:1, 0:128],
                            xcol2.t[0:1, NT:2 * NT], start=True, stop=True),
         [(xcol2, NT, 2 * NT), (cst["ones_row"], 0, 128)])], label=label + "-bcm")
    for fc in range(8):
        prog.op("vector",
                lambda e, fc=fc: e.tensor_mul(
                    lnscr.t[:, fc * NT:(fc + 1) * NT],
                    xbuf.t[:, fc * NT:(fc + 1) * NT], ps["bc1"].t[:]),
                reads=[(xbuf, fc * NT, (fc + 1) * NT), (ps["bc1"], 0, NT)],
                writes=[(lnscr, fc * NT, (fc + 1) * NT)],
                label=label + "-ap1")
        prog.op("vector",
                lambda e, fc=fc: e.tensor_tensor(
                    xln.t[:, fc * NT:(fc + 1) * NT],
                    lnscr.t[:, fc * NT:(fc + 1) * NT], ps["bc2"].t[:],
                    op=ALU.add),
                reads=[(lnscr, fc * NT, (fc + 1) * NT), (ps["bc2"], 0, NT)],
                writes=[(xln, fc * NT, (fc + 1) * NT)],
                label=label + "-ap2")



def declare_l1ffn_drams(env):
    return dict(
        d_u1=env.dram("w_U1", (1024, 256), F32, "ExternalInput"),
        d_v1=env.dram("w_V1", (256, 4096), F32, "ExternalInput"),
        d_u2=env.dram("w_U2", (4096, 256), F32, "ExternalInput"),
        d_v2=env.dram("w_V2", (256, 1024), F32, "ExternalInput"),
        d_b1=env.dram("w_l1b1", (128, 32), F32, "ExternalInput"),
        d_b2=env.dram("w_l1b2", (1, 1024), F32, "ExternalInput"),
    )


def stage_l1ffn_weights(prog, env, dr=None, tag=""):
    if dr is None:
        dr = declare_l1ffn_drams(env)
    d_u1, d_v1, d_u2, d_v2 = dr["d_u1"], dr["d_v1"], dr["d_u2"], dr["d_v2"]
    d_b1, d_b2 = dr["d_b1"], dr["d_b2"]

    u1 = env.sbuf("u1" + tag, (128, 8 * 256), BF16)
    v1 = env.sbuf("v1" + tag, (128, 2 * 4096), BF16)
    u2 = env.sbuf("u2" + tag, (128, 32 * 256), BF16)
    v2 = env.sbuf("v2" + tag, (128, 2 * 1024), BF16)
    b1c = env.sbuf("l1b1c" + tag, (128, 32), F32)
    b2r = env.sbuf("l1b2r" + tag, (1, 1024), BF16)

    prog.dma_batch("qg", [
        (sb3(u1, 8, 256), d_u1.ap.rearrange("(c p) r -> p c r", p=128),
         [], [(u1, 0, 8 * 256)]),
        (sb3(v1, 2, 4096), d_v1.ap.rearrange("(c p) f -> p c f", p=128),
         [], [(v1, 0, 2 * 4096)]),
        (sb3(u2, 32, 256), d_u2.ap.rearrange("(c p) r -> p c r", p=128),
         [], [(u2, 0, 32 * 256)]),
        (sb3(v2, 2, 1024), d_v2.ap.rearrange("(c p) d -> p c d", p=128),
         [], [(v2, 0, 2 * 1024)]),
        (b1c.t[:], d_b1.ap, [], [(b1c, 0, 32)]),
        (b2r.t[:], d_b2.ap, [], [(b2r, 0, 1024)]),
    ], label="l1ffn-w")
    return dict(u1=u1, v1=v1, u2=u2, v2=v2, b1c=b1c, b2r=b2r)


def emit_l1ffn_sublayer(prog, cst, w, bufs, x_in, x_out, label="l1f"):
    ps = bufs["ps"]

    def B(name, par):
        return bufs[name + ("B" if par else "")]

    def stage_P(lb, par, xln):
        rot = (ps["pa"], ps["pb"]) if par == 0 else (ps["pc"], ps["pd"])
        pbuf = B("pbuf", par)
        for rc in range(2):
            bk = rot[rc % 2]
            prog.mm_group(bk, [
                (lambda e, fc=fc, rc=rc, bk=bk, xln=xln: e.matmul(
                    bk.t[:], w["u1"].t[:, fc * 256 + rc * 128:fc * 256 + rc * 128 + 128],
                    xln.t[:, fc * NT:(fc + 1) * NT],
                    start=(fc == 0), stop=(fc == 7)),
                 [(w["u1"], fc * 256, (fc + 1) * 256), (xln, fc * NT, (fc + 1) * NT)])
                for fc in range(8)], label=lb + "-P")
            prog.op("vector",
                    lambda e, rc=rc, bk=bk, pbuf=pbuf: e.tensor_copy(
                        pbuf.t[:, rc * NT:(rc + 1) * NT], bk.t[:]),
                    reads=[(bk, 0, NT)], writes=[(pbuf, rc * NT, (rc + 1) * NT)],
                    label=lb + "-Pev")

    def stage_H(lb, par):
        rot = (ps["pa"], ps["pb"]) if par == 0 else (ps["pc"], ps["pd"])
        pbuf, hbuf = B("pbuf", par), B("hbuf", par)
        for hc in range(32):
            bk = rot[hc % 2]
            prog.mm_group(bk, [
                (lambda e, rc=rc, hc=hc, bk=bk, pbuf=pbuf: e.matmul(
                    bk.t[:], w["v1"].t[:, rc * 4096 + hc * 128:rc * 4096 + hc * 128 + 128],
                    pbuf.t[:, rc * NT:(rc + 1) * NT],
                    start=(rc == 0), stop=(rc == 1)),
                 [(w["v1"], rc * 4096, (rc + 1) * 4096), (pbuf, rc * NT, (rc + 1) * NT)])
                for rc in range(2)], label=lb + "-H")
            prog.op("scalar",
                    lambda e, hc=hc, bk=bk, hbuf=hbuf: e.activation(
                        out=hbuf.t[:, hc * NT:(hc + 1) * NT], in_=bk.t[:],
                        func=AF.Gelu_apprx_tanh, bias=w["b1c"].t[:, hc:hc + 1]),
                    reads=[(bk, 0, NT), (w["b1c"], hc, hc + 1)],
                    writes=[(hbuf, hc * NT, (hc + 1) * NT)], label=lb + "-g")

    def stage_Q(lb, par):
        rot = (ps["pa"], ps["pb"]) if par == 0 else (ps["pc"], ps["pd"])
        hbuf, qbuf = B("hbuf", par), B("qbuf", par)
        for rc in range(2):
            bk = rot[rc % 2]
            prog.mm_group(bk, [
                (lambda e, hc=hc, rc=rc, bk=bk, hbuf=hbuf: e.matmul(
                    bk.t[:], w["u2"].t[:, hc * 256 + rc * 128:hc * 256 + rc * 128 + 128],
                    hbuf.t[:, hc * NT:(hc + 1) * NT],
                    start=(hc == 0), stop=(hc == 31)),
                 [(w["u2"], hc * 256, (hc + 1) * 256), (hbuf, hc * NT, (hc + 1) * NT)])
                for hc in range(32)], label=lb + "-Q")
            prog.op("vector",
                    lambda e, rc=rc, bk=bk, qbuf=qbuf: e.tensor_copy(
                        qbuf.t[:, rc * NT:(rc + 1) * NT], bk.t[:]),
                    reads=[(bk, 0, NT)], writes=[(qbuf, rc * NT, (rc + 1) * NT)],
                    label=lb + "-Qev")

    def stage_O(lb, par, t0, xbuf):
        rot = (ps["pa"], ps["pb"]) if par == 0 else (ps["pc"], ps["pd"])
        qbuf = B("qbuf", par)
        for dc in range(8):
            bk = rot[dc % 2]
            parts = [
                (lambda e, rc=rc, dc=dc, bk=bk, qbuf=qbuf: e.matmul(
                    bk.t[:], w["v2"].t[:, rc * 1024 + dc * 128:rc * 1024 + dc * 128 + 128],
                    qbuf.t[:, rc * NT:(rc + 1) * NT],
                    start=(rc == 0), stop=False),
                 [(w["v2"], rc * 1024, (rc + 1) * 1024), (qbuf, rc * NT, (rc + 1) * NT)])
                for rc in range(2)]
            parts.append(
                (lambda e, dc=dc, bk=bk: e.matmul(
                    bk.t[:], w["b2r"].t[0:1, dc * 128:(dc + 1) * 128],
                    cst["ones_row"].t[0:1, 0:NT], start=False, stop=True),
                 [(w["b2r"], dc * 128, (dc + 1) * 128), (cst["ones_row"], 0, NT)]))
            prog.mm_group(bk, parts, label=lb + "-O")
            prog.op("vector",
                    lambda e, dc=dc, bk=bk, xbuf=xbuf: e.scalar_tensor_tensor(
                        xbuf.t[:, dc * NT:(dc + 1) * NT], bk.t[:], 0.5,
                        xbuf.t[:, dc * NT:(dc + 1) * NT],
                        op0=ALU.mult, op1=ALU.add),
                    reads=[(bk, 0, NT), (xbuf, dc * NT, (dc + 1) * NT)],
                    writes=[(xbuf, dc * NT, (dc + 1) * NT)], label=lb + "-res")
        prog.dma_batch("qa", [
            (x_tile_ap(x_out, t0), sb3(xbuf, 8), [(xbuf, 0, 8 * NT)],
             [(x_out, t0, t0 + NT)]),
        ], label=lb + "-st")

    lnps = dict(stat1=ps["stat1"], stat2=ps["stat2"], bc1=ps["bc1"],
                bc2=ps["bc2"])
    for tp_ in range(NTILES // 2):
        tiles = (2 * tp_, 2 * tp_ + 1)
        lbs = [f"{label}{ti}" for ti in tiles]
        for j, ti in enumerate(tiles):
            emit_ln_tile(prog, cst, x_in, ti * NT, B("xbuf", j), bufs["xcol"],
                         bufs["xcol2"], B("xln", j), B("hbuf", j), lnps,
                         label=lbs[j])
        for j in range(2):
            stage_P(lbs[j], j, B("xln", j))
        for j in range(2):
            stage_H(lbs[j], j)
        for j in range(2):
            stage_Q(lbs[j], j)
        for j in range(2):
            stage_O(lbs[j], j, tiles[j] * NT, B("xbuf", j))


def make_l1ffn_bufs(env, tag=""):
    ps = dict(
        stat1=env.psum("ps_stat1" + tag, (1, NT), F32),
        stat2=env.psum("ps_stat2" + tag, (1, NT), F32),
        bc1=env.psum("ps_bc1" + tag, (128, NT), F32),
        bc2=env.psum("ps_bc2" + tag, (128, NT), F32),
        pa=env.psum("ps_pa" + tag, (128, NT), F32),
        pb=env.psum("ps_pb" + tag, (128, NT), F32),
        pc=env.psum("ps_pc" + tag, (128, NT), F32),
        pd=env.psum("ps_pd" + tag, (128, NT), F32),
    )
    return dict(
        xbuf=env.sbuf("xbuf" + tag, (128, 8 * NT), F32),
        xbufB=env.sbuf("xbufB" + tag, (128, 8 * NT), F32),
        xcol=env.sbuf("xcol" + tag, (1, 2 * NT), F32),
        xcol2=env.sbuf("xcol2" + tag, (1, 2 * NT), BF16),
        xln=env.sbuf("xln" + tag, (128, 8 * NT), BF16),
        xlnB=env.sbuf("xlnB" + tag, (128, 8 * NT), BF16),
        pbuf=env.sbuf("pbuf" + tag, (128, 2 * NT), BF16),
        pbufB=env.sbuf("pbufB" + tag, (128, 2 * NT), BF16),
        qbuf=env.sbuf("qbuf" + tag, (128, 2 * NT), BF16),
        qbufB=env.sbuf("qbufB" + tag, (128, 2 * NT), BF16),
        hbuf=env.sbuf("hbuf" + tag, (128, 32 * NT), BF16),
        hbufB=env.sbuf("hbufB" + tag, (128, 32 * NT), BF16),
        ps=ps,
    )



def host_l1ffn_weights(inputs):
    return {
        "w_U1": np.ascontiguousarray(inputs["L1_ffn_U1"]),
        "w_V1": np.ascontiguousarray(inputs["L1_ffn_V1"]),
        "w_U2": np.ascontiguousarray(inputs["L1_ffn_U2"]),
        "w_V2": np.ascontiguousarray(inputs["L1_ffn_V2"]),
        "w_l1b1": np.ascontiguousarray(
            inputs["L1_ffn_b1"].reshape(32, 128).T),
        "w_l1b2": np.ascontiguousarray(
            inputs["L1_ffn_b2"].reshape(1, 1024)),
    }


def host_x_shard(x_full, core):
    sh = x_full[core * B_CORE:(core + 1) * B_CORE]
    return np.ascontiguousarray(
        sh.transpose(2, 0, 1).reshape(D, NTOK).astype(np.float32))


def host_x_unshard(res_list):
    out = np.empty((B_FULL, T_SEQ, D), np.float32)
    for core, r in enumerate(res_list):
        xm = r.reshape(D, B_CORE, T_SEQ).transpose(1, 2, 0)
        out[core * B_CORE:(core + 1) * B_CORE] = xm
    return out



def declare_moe_drams(env):
    return dict(
        w1=env.dram("w_moe_w1", (8, 1024, 4096), F32, "ExternalInput"),
        w2=env.dram("w_moe_w2", (8, 4096, 1024), F32, "ExternalInput"),
        d_wg=env.dram("w_moe_wg", (128, 64), F32, "ExternalInput"),
        d_bg=env.dram("w_moe_bg", (1, 8), F32, "ExternalInput"),
        d_b1=env.dram("w_moe_b1", (128, 256), F32, "ExternalInput"),
        d_b2=env.dram("w_moe_b2", (128, 64), F32, "ExternalInput"),
    )


def stage_moe_weights(prog, env, dr=None, tag=""):
    if dr is None:
        dr = declare_moe_drams(env)
    d = dict(dr)
    wg = env.sbuf("wgsb" + tag, (128, 64), BF16)
    bg = env.sbuf("bgrow" + tag, (1, 8), BF16)
    b1 = env.sbuf("b1c" + tag, (128, 256), F32)
    b2 = env.sbuf("b2c" + tag, (128, 64), F32)
    prog.dma_batch("qg", [
        (wg.t[:], dr["d_wg"].ap, [], [(wg, 0, 64)]),
        (bg.t[:], dr["d_bg"].ap, [], [(bg, 0, 8)]),
        (b1.t[:], dr["d_b1"].ap, [], [(b1, 0, 256)]),
        (b2.t[:], dr["d_b2"].ap, [], [(b2, 0, 64)]),
    ], label="moe-smallw")
    d.update(wg=wg, bg=bg, b1=b1, b2=b2)
    return d


def make_moe_bufs(env, tag=""):
    ps = {f"p{i}": env.psum(f"{tag}mps_p{i}", (128, NT), F32) for i in range(8)}
    return dict(
        w1sb=env.sbuf("w1sb" + tag, (128, 8 * 4096), BF16),
        w2sb=env.sbuf("w2sb" + tag, (128, 32 * 1024), BF16),
        hbuf=env.sbuf("mhbuf" + tag, (128, 32 * NT), BF16),
        lnbuf=env.sbuf("mlnbuf" + tag, (128, 8 * NT), BF16),
        xresid=env.sbuf("mxresid" + tag, (128, 8 * NT), F32),
        wbsb=env.sbuf("wbsb" + tag, (128, NT), F32),
        tmpsb=env.sbuf("mtmp" + tag, (128, NT), F32),
        tmp2=env.sbuf("mtmp2" + tag, (128, NT), F32),
        tmp3=env.sbuf("mtmp3" + tag, (128, NT), F32),
        wstore=env.sbuf("wstore" + tag, (128, 32 * NTILES), F32),
        xcol=env.sbuf("mxcol" + tag, (1, 2 * NT), F32),
        xcol2=env.sbuf("mxcol2" + tag, (1, 2 * NT), BF16),
        rsc=env.sbuf("mrsc" + tag, (128, 64), F32),
        ps=ps,
    )


def emit_moe_cast_prepass(prog, w, wbf_scr, label="mc"):
    for e in range(8):
        prog.dma_batch("qg", [
            (wbf_scr.t.ap()[e, :, 0:32768].rearrange("p (c h) -> p c h", h=4096),
             w["w1"].t.ap()[e].rearrange("(c p) h -> p c h", p=128),
             [], [(wbf_scr, e * 65536, e * 65536 + 32768)]),
        ], label=label + f"-c1e{e}")
        prog.dma_batch("qg", [
            (wbf_scr.t.ap()[e, :, 32768:65536].rearrange("p (c d) -> p c d", d=1024),
             w["w2"].t.ap()[e].rearrange("(c p) d -> p c d", p=128),
             [], [(wbf_scr, e * 65536 + 32768, (e + 1) * 65536)]),
        ], label=label + f"-c2e{e}")


def emit_moe_router_tile(prog, cst, w, bufs, x_in, xs_ln, ti, label="mr"):
    ps = bufs["ps"]
    lnps = dict(stat1=ps["p0"], stat2=ps["p1"], bc1=ps["p2"], bc2=ps["p3"])
    rsc = bufs["rsc"]
    if True:
        t0 = ti * NT
        lb = f"{label}{ti}"
        emit_ln_tile(prog, cst, x_in, t0, bufs["xresid"], bufs["xcol"],
                     bufs["xcol2"], bufs["lnbuf"], bufs["hbuf"], lnps, label=lb)
        prog.dma_batch("qa", [
            (x_tile_ap(xs_ln, t0), sb3(bufs["lnbuf"], 8),
             [(bufs["lnbuf"], 0, 8 * NT)], [(xs_ln, t0, t0 + NT)]),
        ], label=lb + "-lnst")
        for g in range(4):
            gt = g * 128
            bk = ps["p4"] if g % 2 == 0 else ps["p5"]
            parts = [
                (lambda e, fc=fc, gt=gt, bk=bk: e.matmul(
                    bk.t[0:128, 0:8],
                    bufs["lnbuf"].t[:, fc * NT + gt:fc * NT + gt + 128],
                    w["wg"].t[:, fc * 8:(fc + 1) * 8],
                    start=(fc == 0), stop=False),
                 [(bufs["lnbuf"], fc * NT + gt, fc * NT + gt + 128),
                  (w["wg"], fc * 8, (fc + 1) * 8)])
                for fc in range(8)]
            parts.append(
                (lambda e, bk=bk: e.matmul(
                    bk.t[0:128, 0:8], cst["ones_row"].t[0:1, 0:128],
                    w["bg"].t[0:1, 0:8], start=False, stop=True),
                 [(cst["ones_row"], 0, 128), (w["bg"], 0, 8)]))
            prog.mm_group(bk, parts, label=lb + f"-lg{g}")
            c0 = 0
            prog.op("vector",
                    lambda e, bk=bk: e.reduce_max(rsc.t[:, 0:1], bk.t[0:128, 0:8],
                                                  axis=AX.X, negate=True),
                    reads=[(bk, 0, NT)], writes=[(rsc, 0, 1)], label=lb + "-mx")
            prog.op("scalar",
                    lambda e, bk=bk: e.activation(
                        out=rsc.t[:, 8:16], in_=bk.t[0:128, 0:8], func=AF.Exp,
                        bias=rsc.t[:, 0:1]),
                    reads=[(bk, 0, NT), (rsc, 0, 1)], writes=[(rsc, 8, 16)],
                    label=lb + "-exp")
            prog.op("vector",
                    lambda e: e.reduce_sum(rsc.t[:, 1:2], rsc.t[:, 8:16], axis=AX.X),
                    reads=[(rsc, 8, 16)], writes=[(rsc, 1, 2)], label=lb + "-s")
            prog.op("vector",
                    lambda e: e.reduce_max(rsc.t[:, 2:3], rsc.t[:, 8:16], axis=AX.X),
                    reads=[(rsc, 8, 16)], writes=[(rsc, 2, 3)], label=lb + "-m1")
            prog.op("vector",
                    lambda e: e.tensor_scalar(
                        out=rsc.t[:, 16:24], in0=rsc.t[:, 8:16],
                        scalar1=rsc.t[:, 2:3], scalar2=-1.0,
                        op0=ALU.is_equal, op1=ALU.mult),
                    reads=[(rsc, 8, 16), (rsc, 2, 3)], writes=[(rsc, 16, 24)],
                    label=lb + "-negeq")
            prog.op("vector",
                    lambda e: e.tensor_scalar(
                        out=rsc.t[:, 16:24], in0=rsc.t[:, 16:24],
                        scalar1=1.0, scalar2=None, op0=ALU.add),
                    reads=[(rsc, 16, 24)], writes=[(rsc, 16, 24)],
                    label=lb + "-inv")
            prog.op("vector",
                    lambda e: e.tensor_mul(rsc.t[:, 16:24], rsc.t[:, 16:24],
                                           rsc.t[:, 8:16]),
                    reads=[(rsc, 8, 24)], writes=[(rsc, 16, 24)],
                    label=lb + "-msk")
            prog.op("vector",
                    lambda e: e.reduce_max(rsc.t[:, 3:4], rsc.t[:, 16:24], axis=AX.X),
                    reads=[(rsc, 16, 24)], writes=[(rsc, 3, 4)], label=lb + "-m2")
            prog.op("vector",
                    lambda e: e.tensor_scalar(
                        out=rsc.t[:, 24:32], in0=rsc.t[:, 8:16],
                        scalar1=rsc.t[:, 3:4], scalar2=None, op0=ALU.is_ge),
                    reads=[(rsc, 8, 16), (rsc, 3, 4)], writes=[(rsc, 24, 32)],
                    label=lb + "-ge")
            prog.op("vector",
                    lambda e: e.tensor_mul(rsc.t[:, 24:32], rsc.t[:, 24:32],
                                           rsc.t[:, 8:16]),
                    reads=[(rsc, 8, 32)], writes=[(rsc, 24, 32)],
                    label=lb + "-wsel")
            prog.op("vector",
                    lambda e: e.reciprocal(rsc.t[:, 4:5], rsc.t[:, 1:2]),
                    reads=[(rsc, 1, 2)], writes=[(rsc, 4, 5)], label=lb + "-rs")
            wc = (ti * 4 + g) * 8
            prog.op("vector",
                    lambda e, wc=wc: e.tensor_scalar(
                        out=bufs["wstore"].t[:, wc:wc + 8], in0=rsc.t[:, 24:32],
                        scalar1=rsc.t[:, 4:5], scalar2=None, op0=ALU.mult),
                    reads=[(rsc, 24, 32), (rsc, 4, 5)],
                    writes=[(bufs["wstore"], wc, wc + 8)], label=lb + "-w")


def emit_moe_experts(prog, cst, w, bufs, x_in, x_out, xs_ln, xs_acc,
                     wbf_scr=None, scratch_mode="write", label="me",
                     router_cb=None):
    ps = bufs["ps"]
    w1sb, w2sb = bufs["w1sb"], bufs["w2sb"]
    hbuf, xlnt = bufs["hbuf"], bufs["lnbuf"]
    xresid, wbsb, tmpsb, tmp2 = bufs["xresid"], bufs["wbsb"], bufs["tmpsb"], bufs["tmp2"]
    for e in range(8):
        lbE = f"{label}e{e}"
        if scratch_mode == "read":
            for q4 in range(4):
                c0, c1 = q4 * 8192, (q4 + 1) * 8192
                prog.dma_batch("qs", [
                    (w1sb.t[:, c0:c1], wbf_scr.t.ap()[e, :, c0:c1],
                     [(wbf_scr, e * 65536 + c0, e * 65536 + c1)],
                     [(w1sb, c0, c1)]),
                ], label=lbE + f"-w1{q4}")
            for q4 in range(4):
                c0, c1 = q4 * 8192, (q4 + 1) * 8192
                prog.dma_batch("qs", [
                    (w2sb.t[:, c0:c1], wbf_scr.t.ap()[e, :, 32768 + c0:32768 + c1],
                     [(wbf_scr, e * 65536 + 32768 + c0, e * 65536 + 32768 + c1)],
                     [(w2sb, c0, c1)]),
                ], label=lbE + f"-w2{q4}")
        else:
            prog.dma_batch("qg", [
                (sb3(w1sb, 8, 4096),
                 w["w1"].t.ap()[e].rearrange("(c p) h -> p c h", p=128),
                 [], [(w1sb, 0, 32768)]),
            ], label=lbE + "-w1")
            prog.dma_batch("qg", [
                (sb3(w2sb, 32, 1024),
                 w["w2"].t.ap()[e].rearrange("(c p) d -> p c d", p=128),
                 [], [(w2sb, 0, 32768)]),
            ], label=lbE + "-w2")
            if scratch_mode == "write":
                prog.dma_batch("qa", [
                    (wbf_scr.t.ap()[e, :, 0:32768], w1sb.t[:],
                     [(w1sb, 0, 32768)],
                     [(wbf_scr, e * 65536, e * 65536 + 32768)]),
                ], label=lbE + "-sv1")
                prog.dma_batch("qa", [
                    (wbf_scr.t.ap()[e, :, 32768:65536], w2sb.t[:],
                     [(w2sb, 0, 32768)],
                     [(wbf_scr, e * 65536 + 32768, (e + 1) * 65536)]),
                ], label=lbE + "-sv2")
        for ti in range(NTILES):
            t0 = ti * NT
            lb = f"{lbE}t{ti}"
            if router_cb is not None and e == 0 and ti + 2 < NTILES:
                router_cb(ti + 2)
            prog.dma_batch("qs", [
                (sb3(xlnt, 8), x_tile_ap(xs_ln, t0), [(xs_ln, t0, t0 + NT)],
                 [(xlnt, 0, 8 * NT)]),
            ], label=lb + "-xln")
            if e == 7:
                prog.dma_batch("qs", [
                    (sb3(xresid, 8), x_tile_ap(x_in, t0),
                     [(x_in, t0, t0 + NT)], [(xresid, 0, 8 * NT)]),
                ], label=lb + "-xres")
            pw = ps["p4"]
            for g in range(4):
                wc = (ti * 4 + g) * 8 + e
                prog.pe_op(
                    lambda en, wc=wc, g=g, pw=pw: en.transpose(
                        pw.t[0:1, g * 128:(g + 1) * 128],
                        bufs["wstore"].t[:, wc:wc + 1], cst["identf"].t[:]),
                    reads=[(bufs["wstore"], wc, wc + 1), (cst["identf"], 0, 128)],
                    writes=[(pw, 0, NT)], label=lb + "-wt")
            prog.op("vector",
                    lambda e_, pw=pw: e_.tensor_copy(bufs["xcol2"].t[0:1, 0:NT],
                                                     pw.t[0:1, :]),
                    reads=[(pw, 0, NT)], writes=[(bufs["xcol2"], 0, NT)],
                    label=lb + "-wr16")
            prog.mm_group(ps["p5"], [
                (lambda e_: e_.matmul(ps["p5"].t[:], cst["ones_row"].t[0:1, 0:128],
                                      bufs["xcol2"].t[0:1, 0:NT],
                                      start=True, stop=True),
                 [(bufs["xcol2"], 0, NT), (cst["ones_row"], 0, 128)])],
                label=lb + "-wbc")
            prog.op("vector",
                    lambda e_: e_.tensor_copy(wbsb.t[:], ps["p5"].t[:]),
                    reads=[(ps["p5"], 0, NT)], writes=[(wbsb, 0, NT)],
                    label=lb + "-wbs")
            for hc in range(32):
                bk = ps["p0" if hc % 2 == 0 else "p1"]
                prog.mm_group(bk, [
                    (lambda e_, fc=fc, hc=hc, bk=bk: e_.matmul(
                        bk.t[:],
                        w1sb.t[:, fc * 4096 + hc * 128:fc * 4096 + hc * 128 + 128],
                        xlnt.t[:, fc * NT:(fc + 1) * NT],
                        start=(fc == 0), stop=(fc == 7)),
                     [(w1sb, fc * 4096 + hc * 128, fc * 4096 + hc * 128 + 128),
                      (xlnt, fc * NT, (fc + 1) * NT)])
                    for fc in range(8)], label=lb + "-h")
                prog.op("scalar",
                        lambda e_, hc=hc, bk=bk, e=e: e_.activation(
                            out=hbuf.t[:, hc * NT:(hc + 1) * NT], in_=bk.t[:],
                            func=AF.Gelu,
                            bias=w["b1"].t[:, e * 32 + hc:e * 32 + hc + 1]),
                        reads=[(bk, 0, NT), (w["b1"], e * 32 + hc, e * 32 + hc + 1)],
                        writes=[(hbuf, hc * NT, (hc + 1) * NT)], label=lb + "-g")
            for dc in range(8):
                bk = ps["p2" if dc % 2 == 0 else "p3"]
                tps_ = tmpsb if dc % 2 == 0 else bufs["tmp3"]
                if e == 7:
                    prog.dma_batch("qs", [
                        (tmp2.t[:],
                         x_tile_ap(xs_acc, t0, nt=NT, f0=dc * 128, nf=128)
                         .rearrange("p c t -> p (c t)"),
                         [(xs_acc, t0, t0 + NT)], [(tmp2, 0, NT)]),
                    ], label=lb + f"-ald{dc}")
                prog.mm_group(bk, [
                    (lambda e_, hc=hc, dc=dc, bk=bk: e_.matmul(
                        bk.t[:],
                        w2sb.t[:, hc * 1024 + dc * 128:hc * 1024 + dc * 128 + 128],
                        hbuf.t[:, hc * NT:(hc + 1) * NT],
                        start=(hc == 0), stop=(hc == 31)),
                     [(w2sb, hc * 1024 + dc * 128, hc * 1024 + dc * 128 + 128),
                      (hbuf, hc * NT, (hc + 1) * NT)])
                    for hc in range(32)], label=lb + "-eo")
                prog.op("vector",
                        lambda e_, dc=dc, bk=bk, e=e, tps_=tps_: e_.scalar_tensor_tensor(
                            tps_.t[:], bk.t[:],
                            w["b2"].t[:, e * 8 + dc:e * 8 + dc + 1], wbsb.t[:],
                            op0=ALU.add, op1=ALU.mult),
                        reads=[(bk, 0, NT), (w["b2"], e * 8 + dc, e * 8 + dc + 1),
                               (wbsb, 0, NT)],
                        writes=[(tps_, 0, NT)], label=lb + "-wt2")
                if e < 7:
                    kw = {} if e == 0 else {"accum_op": ALU.add}
                    prog.dma_batch("qg", [
                        (x_tile_ap(xs_acc, t0, nt=NT, f0=dc * 128, nf=128)
                         .rearrange("p c t -> p (c t)"), tps_.t[:],
                         [(tps_, 0, NT)], [(xs_acc, t0, t0 + NT)], kw),
                    ], label=lb + f"-acc{dc}")
                else:
                    prog.op("vector",
                            lambda e_, dc=dc, tps_=tps_: e_.tensor_tensor(
                                tps_.t[:], tps_.t[:], tmp2.t[:], op=ALU.add),
                            reads=[(tmp2, 0, NT), (tps_, 0, NT)],
                            writes=[(tps_, 0, NT)], label=lb + "-cmb")
                    prog.op("vector",
                            lambda e_, dc=dc, tps_=tps_: e_.scalar_tensor_tensor(
                                xresid.t[:, dc * NT:(dc + 1) * NT],
                                tps_.t[:], 0.5,
                                xresid.t[:, dc * NT:(dc + 1) * NT],
                                op0=ALU.mult, op1=ALU.add),
                            reads=[(tps_, 0, NT),
                                   (xresid, dc * NT, (dc + 1) * NT)],
                            writes=[(xresid, dc * NT, (dc + 1) * NT)],
                            label=lb + "-res")
            if e == 7:
                prog.dma_batch("qa", [
                    (x_tile_ap(x_out, t0), sb3(xresid, 8), [(xresid, 0, 8 * NT)],
                     [(x_out, t0, t0 + NT)]),
                ], label=lb + "-xst")


def host_moe_weights(inputs):
    w1 = np.ascontiguousarray(inputs["L0_ffn_W1"].astype(np.float32))
    w2 = np.ascontiguousarray(inputs["L0_ffn_W2"].astype(np.float32))
    wg = np.ascontiguousarray(
        inputs["L0_ffn_Wg"].reshape(8, 128, 8).transpose(1, 0, 2).reshape(128, 64))
    bg = np.ascontiguousarray(inputs["L0_ffn_bg"].reshape(1, 8))
    b1 = np.ascontiguousarray(
        inputs["L0_ffn_b1"].reshape(8, 32, 128).transpose(2, 0, 1).reshape(128, 256))
    b2 = np.ascontiguousarray(
        inputs["L0_ffn_b2"].reshape(8, 8, 128).transpose(2, 0, 1).reshape(128, 64))
    return {"w_moe_w1": w1, "w_moe_w2": w2, "w_moe_wg": wg, "w_moe_bg": bg,
            "w_moe_b1": b1, "w_moe_b2": b2}



WDN = 512
CDN = 128
NH = 8
DK = 128


def declare_dn_drams(env):
    return dict(
        d_qkv=env.dram("w_qkvT", (1024, 3072), F32, "ExternalInput"),
        d_zw=env.dram("w_zwT", (1024, 1024), F32, "ExternalInput"),
        d_ab=env.dram("w_abT", (1024, 16), F32, "ExternalInput"),
        d_ow=env.dram("w_owT", (1024, 1024), F32, "ExternalInput"),
        d_cv=env.dram("w_conv", (128, 96), F32, "ExternalInput"),
        d_dtb=env.dram("w_dtbrow", (1, 16), F32, "ExternalInput"),
        d_alog=env.dram("w_alog", (8, 1), F32, "ExternalInput"),
        d_nw=env.dram("w_normw", (128, 1), F32, "ExternalInput"),
        d_msT=env.dram("c_masksT", (128, 128), F32, "ExternalInput"),
        d_miT=env.dram("c_maskiT", (128, 128), F32, "ExternalInput"),
        d_onesf=env.dram("c_onesf", (1, 128), F32, "ExternalInput"),
    )


def stage_dn_weights(prog, env, dr=None):
    if dr is None:
        dr = declare_dn_drams(env)
    d_qkv, d_zw, d_ab, d_ow = dr["d_qkv"], dr["d_zw"], dr["d_ab"], dr["d_ow"]
    d_cv, d_dtb, d_alog, d_nw = dr["d_cv"], dr["d_dtb"], dr["d_alog"], dr["d_nw"]
    d_msT, d_miT, d_onesf = dr["d_msT"], dr["d_miT"], dr["d_onesf"]

    qkvT = env.sbuf("qkvT", (128, 8 * 3072), BF16)
    zwT = env.sbuf("zwT", (128, 8 * 1024), BF16)
    owT = env.sbuf("owT", (128, 8 * 1024), BF16)
    convw = env.sbuf("convw", (128, 96), F32)
    dtbrow = env.sbuf("dtbrow", (1, 16), BF16)
    alog = env.sbuf("alog", (8, 1), F32)
    normw = env.sbuf("normw", (128, 1), F32)
    msT = env.sbuf("msT", (128, 128), F32)
    miT = env.sbuf("miT", (128, 128), F32)
    onesf = env.sbuf("onesf", (1, 128), F32)
    nega = env.sbuf("nega", (8, 1), F32)

    prog.dma_batch("qg", [
        (sb3(qkvT, 8, 3072), d_qkv.ap.rearrange("(c p) o -> p c o", p=128),
         [], [(qkvT, 0, 8 * 3072)]),
        (sb3(zwT, 8, 1024), d_zw.ap.rearrange("(c p) o -> p c o", p=128),
         [], [(zwT, 0, 8 * 1024)]),
        (sb3(owT, 8, 1024), d_ow.ap.rearrange("(c p) o -> p c o", p=128),
         [], [(owT, 0, 8 * 1024)]),
        (convw.t[:], d_cv.ap, [], [(convw, 0, 96)]),
        (dtbrow.t[:], d_dtb.ap, [], [(dtbrow, 0, 16)]),
        (alog.t[:], d_alog.ap, [], [(alog, 0, 1)]),
        (normw.t[:], d_nw.ap, [], [(normw, 0, 1)]),
        (msT.t[:], d_msT.ap, [], [(msT, 0, 128)]),
        (miT.t[:], d_miT.ap, [], [(miT, 0, 128)]),
        (onesf.t[:], d_onesf.ap, [], [(onesf, 0, 128)]),
    ], label="dn-w")
    abTc = env.sbuf("abTc", (128, 8 * 16), BF16)
    prog.dma_batch("qg", [
        (sb3(abTc, 8, 16), d_ab.ap.rearrange("(c p) o -> p c o", p=128),
         [], [(abTc, 0, 8 * 16)]),
    ], label="dn-w2")
    prog.op("scalar", lambda e: e.activation(out=nega.t[:], in_=alog.t[:],
                                             func=AF.Exp),
            reads=[(alog, 0, 1)], writes=[(nega, 0, 1)], label="dn-nega")
    prog.op("vector", lambda e: e.tensor_scalar(
        out=nega.t[:], in0=nega.t[:], scalar1=-1.0, scalar2=None, op0=ALU.mult),
        reads=[(nega, 0, 1)], writes=[(nega, 0, 1)], label="dn-nega2")
    return dict(qkvT=qkvT, zwT=zwT, abT=abTc, owT=owT, convw=convw,
                dtbrow=dtbrow, nega=nega, normw=normw, msT=msT,
                miT=miT, onesf=onesf)


def make_dn_bufs(env):
    ps = {f"p{i}": env.psum(f"dps_p{i}", (128, NT), F32) for i in range(6)}
    ps["b0"] = env.psum("dps_b0", (128, 1024), BF16)
    ps["b1"] = env.psum("dps_b1", (128, 1024), BF16)
    return dict(
        mpre=env.sbuf("mpre", (128, 24 * 516), BF16),
        mpost=env.sbuf("mpost", (128, 2 * 24 * 128), BF16),
        lnbuf=env.sbuf("dlnbuf", (128, 8 * NT), BF16),
        xresid=env.sbuf("dxresid", (128, 8 * NT), F32),
        obuf=env.sbuf("obuf", (128, 2 * 1024), BF16),
        zsil=env.sbuf("zsil", (128, 2 * 1024), BF16),
        omf=env.sbuf("omf", (128, 8 * NT), BF16),
        S=env.sbuf("Sst", (128, 8 * 128), F32),
        Sb=env.sbuf("Sbst", (128, 8 * 128), BF16),
        brow=env.sbuf("brow", (8, WDN), F32),
        grow=env.sbuf("grow", (8, WDN), F32),
        zrow=env.sbuf("zrow", (8, CDN), F32),
        crow=env.sbuf("crow", (8, WDN), F32),
        crows=env.sbuf("crows", (1, 1024), F32),
        cols=env.sbuf("dncols", (128, 52), F32),
        powb=env.sbuf("powb", (128, 26 * 128), F32),
        wk=env.sbuf("dnwk", (128, 26 * 128), F32),
        xcol2=env.sbuf("dxcol2", (1, 2 * NT), BF16),
        ps=ps,
    )


def emit_dn_sublayer(prog, cst, w, bufs, x_in, x_out, label="dn"):
    ps = bufs["ps"]
    mpre, mpost, lnbuf = bufs["mpre"], bufs["mpost"], bufs["lnbuf"]
    xresid, obuf, zsil, omf = bufs["xresid"], bufs["obuf"], bufs["zsil"], bufs["omf"]
    S, Sb = bufs["S"], bufs["Sb"]
    brow, grow, zrow, crow = bufs["brow"], bufs["grow"], bufs["zrow"], bufs["crow"]
    crows, cols, powb, wk = bufs["crows"], bufs["cols"], bufs["powb"], bufs["wk"]
    lnps = dict(stat1=ps["p0"], stat2=ps["p1"], bc1=ps["p2"], bc2=ps["p3"])
    srow_ap = lambda: wk.t[0:8, 0:WDN]
    trow_ap = lambda: wk.t[0:8, WDN:2 * WDN]
    SR = (wk, 0, WDN)
    TR = (wk, WDN, 2 * WDN)

    prog.op("vector", lambda e: e.memset(zrow.t[:], 0.0), writes=[(zrow, 0, CDN)],
            label=label + "-z0")
    for vc in range(8):
        prog.op("vector",
                lambda e, vc=vc: e.tensor_scalar(
                    out=w["owT"].t[:, vc * 1024:(vc + 1) * 1024],
                    in0=w["owT"].t[:, vc * 1024:(vc + 1) * 1024],
                    scalar1=w["normw"].t[:, 0:1], scalar2=None, op0=ALU.mult),
                reads=[(w["owT"], vc * 1024, (vc + 1) * 1024), (w["normw"], 0, 1)],
                writes=[(w["owT"], vc * 1024, (vc + 1) * 1024)],
                label=label + "-nwfold")

    def head_chain(h, ch, hpar, cpar, lc):
        lh = lc + f"h{h}"
        par = hpar
        mb = cpar * 24
        pmain = ps["p3"] if par == 0 else ps["p5"]
        paux = ps["p2"] if par == 0 else ps["p4"]
        bnb = ps["b0"] if par == 0 else ps["b1"]
        wkb = 8 + par * 9
        pwb = par * 13

        def wa(i):
            return wk.t[:, (wkb + i) * 128:(wkb + i + 1) * 128]

        def wi(i):
            return (wk, (wkb + i) * 128, (wkb + i + 1) * 128)

        def pa(i):
            return powb.t[:, (pwb + i) * 128:(pwb + i + 1) * 128]

        def pi(i):
            return (powb, (pwb + i) * 128, (pwb + i + 1) * 128)

        qap = mpost.t[:, (mb + h) * 128:(mb + h + 1) * 128]
        qiv = (mpost, (mb + h) * 128, (mb + h + 1) * 128)
        kap = mpost.t[:, (mb + 8 + h) * 128:(mb + 8 + h + 1) * 128]
        kiv = (mpost, (mb + 8 + h) * 128, (mb + 8 + h + 1) * 128)
        vap = mpost.t[:, (mb + 16 + h) * 128:(mb + 16 + h + 1) * 128]
        viv = (mpost, (mb + 16 + h) * 128, (mb + 16 + h + 1) * 128)
        sap = Sb.t[:, h * 128:(h + 1) * 128]
        siv = (Sb, h * 128, (h + 1) * 128)
        prog.mm_group(paux, [
            (lambda e, h=h, paux=paux: e.matmul(
                paux.t[0:128, 0:128], w["onesf"].t[0:1, 0:128],
                crows.t[0:1, h * 128:(h + 1) * 128], start=True, stop=True),
             [(crows, h * 128, (h + 1) * 128), (w["onesf"], 0, 128)])],
            label=lh + "-bc")
        yield
        prog.op("vector",
                lambda e, h=h, paux=paux: e.tensor_scalar(
                    out=wa(0), in0=paux.t[0:128, 0:128],
                    scalar1=cols.t[:, 24 + h:25 + h], scalar2=0.0,
                    op0=ALU.add, op1=ALU.min),
                reads=[(paux, 0, NT), (cols, 24 + h, 25 + h)],
                writes=[wi(0)], label=lh + "-gd")
        yield
        prog.op("scalar",
                lambda e: e.activation(out=wa(0), in_=wa(0), func=AF.Exp),
                reads=[wi(0)], writes=[wi(0)], label=lh + "-gexp")
        yield
        prog.op("vector",
                lambda e: e.tensor_mul(wa(1), wa(0), w["miT"].t[:]),
                reads=[wi(0), (w["miT"], 0, 128)], writes=[wi(1)],
                label=lh + "-gi")
        yield
        prog.op("vector",
                lambda e: e.tensor_mul(wa(2), wa(0), w["msT"].t[:]),
                reads=[wi(0), (w["msT"], 0, 128)], writes=[wi(2)],
                label=lh + "-gs")
        yield
        prog.mm_group(pmain, [
            (lambda e, kap=kap, pmain=pmain: e.matmul(
                pmain.t[0:128, 0:128], kap, kap, start=True, stop=True),
             [kiv])], label=lh + "-akk")
        yield
        prog.op("vector",
                lambda e, pmain=pmain: e.tensor_tensor(
                    wa(2), wa(2), pmain.t[0:128, 0:128], op=ALU.mult),
                reads=[wi(2), (pmain, 0, NT)], writes=[wi(2)], label=lh + "-mt")
        yield
        prog.op("vector",
                lambda e, h=h: e.tensor_scalar(
                    out=pa(0), in0=wa(2), scalar1=cols.t[:, 8 + h:9 + h],
                    scalar2=None, op0=ALU.mult),
                reads=[wi(2), (cols, 8 + h, 9 + h)], writes=[pi(0)],
                label=lh + "-pt")
        yield
        prog.mm_group(pmain, [
            (lambda e, kap=kap, qap=qap, pmain=pmain: e.matmul(
                pmain.t[0:128, 0:128], kap, qap, start=True, stop=True),
             [kiv, qiv])], label=lh + "-aqk")
        yield
        prog.op("vector",
                lambda e, pmain=pmain: e.tensor_tensor(
                    wa(3), wa(1), pmain.t[0:128, 0:128], op=ALU.mult),
                reads=[wi(1), (pmain, 0, NT)], writes=[wi(3)], label=lh + "-wt")
        yield
        prog.pe_op(lambda e, paux=paux: e.transpose(
            paux.t[0:128, 0:128], pa(0), cst["identf"].t[:]),
            reads=[pi(0), (cst["identf"], 0, 128)], writes=[(paux, 0, NT)],
            label=lh + "-trp")
        yield
        prog.op("scalar",
                lambda e, paux=paux: e.activation(
                    out=pa(1), in_=paux.t[0:128, 0:128], func=AF.Identity),
                reads=[(paux, 0, NT)], writes=[pi(1)], label=lh + "-pcp")
        yield
        for (it, inn, ot, on) in [(0, 1, 2, 3), (2, 3, 4, 5), (4, 5, 6, 7),
                                  (6, 7, 8, 9), (8, 9, 10, 11)]:
            prog.mm_group(pmain, [
                (lambda e, it=it, inn=inn, pmain=pmain: e.matmul(
                    pmain.t[0:128, 0:128], pa(it), pa(inn),
                    start=True, stop=True), [pi(it), pi(inn)])],
                label=lh + "-sq")
            yield
            prog.op("scalar",
                    lambda e, on=on, pmain=pmain: e.activation(
                        out=pa(on), in_=pmain.t[0:128, 0:128],
                        func=AF.Identity),
                    reads=[(pmain, 0, NT)], writes=[pi(on)], label=lh + "-sqe")
            yield
            prog.mm_group(paux, [
                (lambda e, it=it, inn=inn, paux=paux: e.matmul(
                    paux.t[0:128, 0:128], pa(inn), pa(it),
                    start=True, stop=True), [pi(it), pi(inn)])],
                label=lh + "-sqt")
            yield
            prog.op("scalar",
                    lambda e, ot=ot, paux=paux: e.activation(
                        out=pa(ot), in_=paux.t[0:128, 0:128],
                        func=AF.Identity),
                    reads=[(paux, 0, NT)], writes=[pi(ot)], label=lh + "-sqte")
            yield
        prog.mm_group(pmain, [
            (lambda e, pmain=pmain: e.matmul(pmain.t[0:128, 0:128], pa(11),
                                             pa(10), start=True, stop=True),
             [pi(10), pi(11)])], label=lh + "-s64")
        yield
        prog.op("scalar",
                lambda e, pmain=pmain: e.activation(
                    out=pa(12), in_=pmain.t[0:128, 0:128], func=AF.Identity),
                reads=[(pmain, 0, NT)], writes=[pi(12)], label=lh + "-s64e")
        yield
        prog.mm_group(pmain, [
            (lambda e, kap=kap, sap=sap, pmain=pmain: e.matmul(
                pmain.t[0:128, 0:128], kap, sap, start=True, stop=True),
             [kiv, siv])], label=lh + "-ks")
        yield
        prog.pe_op(lambda e, vap=vap, bnb=bnb: e.transpose(
            bnb.t[:, 128:256], vap, cst["ident"].t[:]),
            reads=[viv, (cst["ident"], 0, 128)], writes=[(bnb, 0, 1024)],
            label=lh + "-vt")
        yield
        prog.op("scalar",
                lambda e, bnb=bnb: e.activation(
                    out=wa(4), in_=bnb.t[:, 128:256], func=AF.Identity),
                reads=[(bnb, 0, 1024)], writes=[wi(4)], label=lh + "-vte")
        yield
        prog.op("vector",
                lambda e, h=h, pmain=pmain: e.scalar_tensor_tensor(
                    wa(5), pmain.t[0:128, 0:128], cols.t[:, 32 + h:33 + h],
                    wa(4), op0=ALU.mult, op1=ALU.add),
                reads=[(pmain, 0, NT), (cols, 32 + h, 33 + h), wi(4)],
                writes=[wi(5)], label=lh + "-bt")
        yield
        tsrc = 5
        for pix in (12, 10, 8, 6, 4, 2):
            tdst = 6 if tsrc != 6 else 7
            prog.mm_group(pmain, [
                (lambda e, pix=pix, tsrc=tsrc, pmain=pmain: e.matmul(
                    pmain.t[0:128, 0:128], pa(pix), wa(tsrc),
                    start=True, stop=True), [pi(pix), wi(tsrc)])],
                label=lh + "-ap")
            yield
            prog.op("vector",
                    lambda e, tsrc=tsrc, tdst=tdst, pmain=pmain: e.tensor_tensor(
                        wa(tdst), wa(tsrc), pmain.t[0:128, 0:128], op=ALU.add),
                    reads=[wi(tsrc), (pmain, 0, NT)], writes=[wi(tdst)],
                    label=lh + "-ape")
            yield
            tsrc = tdst
        prog.mm_group(pmain, [
            (lambda e, tsrc=tsrc, pmain=pmain: e.matmul(
                pmain.t[0:128, 0:128], pa(0), wa(tsrc),
                start=True, stop=True), [pi(0), wi(tsrc)])], label=lh + "-apl")
        yield
        prog.op("vector",
                lambda e, tsrc=tsrc, pmain=pmain: e.tensor_tensor(
                    wa(8), wa(tsrc), pmain.t[0:128, 0:128], op=ALU.subtract),
                reads=[wi(tsrc), (pmain, 0, NT)], writes=[wi(8)],
                label=lh + "-ub")
        yield
        prog.op("vector",
                lambda e, h=h: e.tensor_scalar(
                    out=wa(8), in0=wa(8), scalar1=cols.t[:, 8 + h:9 + h],
                    scalar2=None, op0=ALU.mult),
                reads=[wi(8), (cols, 8 + h, 9 + h)], writes=[wi(8)],
                label=lh + "-u")
        yield
        prog.mm_group(paux, [
            (lambda e, qap=qap, sap=sap, paux=paux: e.matmul(
                paux.t[0:128, 0:128], qap, sap, start=True, stop=True),
             [qiv, siv])], label=lh + "-oint")
        yield
        prog.mm_group(pmain, [
            (lambda e, pmain=pmain: e.matmul(pmain.t[0:128, 0:128], wa(3),
                                             wa(8), start=True, stop=True),
             [wi(3), wi(8)])], label=lh + "-ointr")
        yield
        prog.op("vector",
                lambda e, h=h, paux=paux: e.tensor_scalar(
                    out=wa(2), in0=paux.t[0:128, 0:128],
                    scalar1=cols.t[:, 16 + h:17 + h], scalar2=None,
                    op0=ALU.mult),
                reads=[(paux, 0, NT), (cols, 16 + h, 17 + h)],
                writes=[wi(2)], label=lh + "-osc")
        yield
        prog.op("vector",
                lambda e, h=h, cpar=cpar, pmain=pmain: e.tensor_tensor(
                    obuf.t[:, cpar * 1024 + h * 128:cpar * 1024 + (h + 1) * 128],
                    wa(2), pmain.t[0:128, 0:128], op=ALU.add),
                reads=[wi(2), (pmain, 0, NT)],
                writes=[(obuf, cpar * 1024 + h * 128,
                         cpar * 1024 + (h + 1) * 128)],
                label=lh + "-o")
        yield
        prog.pe_op(lambda e, kap=kap, bnb=bnb: e.transpose(
            bnb.t[:, 256:384], kap, cst["ident"].t[:]),
            reads=[kiv, (cst["ident"], 0, 128)], writes=[(bnb, 0, 1024)],
            label=lh + "-kt")
        yield
        prog.op("vector",
                lambda e: e.tensor_copy(wa(0)[:, 0:1],
                                        wa(1)[:, 127:128]),
                reads=[wi(1)], writes=[wi(0)], label=lh + "-ghf")
        yield
        prog.op("vector",
                lambda e, bnb=bnb: e.tensor_scalar(
                    out=wa(4), in0=bnb.t[:, 256:384],
                    scalar1=wa(0)[:, 0:1], scalar2=None, op0=ALU.mult),
                reads=[(bnb, 0, 1024), wi(0)], writes=[wi(4)],
                label=lh + "-kh")
        yield
        prog.mm_group(pmain, [
            (lambda e, pmain=pmain: e.matmul(pmain.t[0:128, 0:128], wa(4),
                                             wa(8), start=True, stop=True),
             [wi(4), wi(8)])], label=lh + "-snew")
        yield
        prog.op("vector",
                lambda e, h=h, pmain=pmain: e.scalar_tensor_tensor(
                    S.t[:, h * 128:(h + 1) * 128],
                    S.t[:, h * 128:(h + 1) * 128],
                    cols.t[:, 40 + h:41 + h], pmain.t[0:128, 0:128],
                    op0=ALU.mult, op1=ALU.add),
                reads=[(S, h * 128, (h + 1) * 128), (cols, 40 + h, 41 + h),
                       (pmain, 0, NT)],
                writes=[(S, h * 128, (h + 1) * 128)], label=lh + "-S")
        yield
        prog.op("vector",
                lambda e, h=h: e.tensor_copy(
                    Sb.t[:, h * 128:(h + 1) * 128],
                    S.t[:, h * 128:(h + 1) * 128]),
                reads=[(S, h * 128, (h + 1) * 128)],
                writes=[(Sb, h * 128, (h + 1) * 128)], label=lh + "-Sb")
        yield
        prog.op("gpsimd",
                lambda e, h=h, cpar=cpar: e.tensor_tensor(
                    wa(0),
                    obuf.t[:, cpar * 1024 + h * 128:cpar * 1024 + (h + 1) * 128],
                    obuf.t[:, cpar * 1024 + h * 128:cpar * 1024 + (h + 1) * 128],
                    op=ALU.mult),
                reads=[(obuf, cpar * 1024 + h * 128,
                        cpar * 1024 + (h + 1) * 128)],
                writes=[wi(0)], label=lh + "-osq")
        yield
        prog.op("vector",
                lambda e, h=h, par=par: e.reduce_sum(
                    cols.t[:, 48 + par:49 + par], wa(0), axis=AX.X),
                reads=[wi(0)], writes=[(cols, 48 + par, 49 + par)],
                label=lh + "-ms")
        yield
        prog.op("scalar",
                lambda e, par=par: e.activation(
                    out=cols.t[:, 48 + par:49 + par],
                    in_=cols.t[:, 48 + par:49 + par], func=AF.Ln,
                    scale=1.0 / 128, bias=cst["misc"].t[:, 1:2]),
                reads=[(cols, 48 + par, 49 + par), (cst["misc"], 1, 2)],
                writes=[(cols, 48 + par, 49 + par)], label=lh + "-lnm")
        yield
        prog.op("scalar",
                lambda e, par=par: e.activation(
                    out=cols.t[:, 48 + par:49 + par],
                    in_=cols.t[:, 48 + par:49 + par], func=AF.Exp, scale=-0.5),
                reads=[(cols, 48 + par, 49 + par)],
                writes=[(cols, 48 + par, 49 + par)], label=lh + "-rsq")
        yield
        prog.op("vector",
                lambda e, h=h, par=par, cpar=cpar: e.scalar_tensor_tensor(
                    obuf.t[:, cpar * 1024 + h * 128:cpar * 1024 + (h + 1) * 128],
                    obuf.t[:, cpar * 1024 + h * 128:cpar * 1024 + (h + 1) * 128],
                    cols.t[:, 48 + par:49 + par],
                    zsil.t[:, cpar * 1024 + h * 128:cpar * 1024 + (h + 1) * 128],
                    op0=ALU.mult, op1=ALU.mult),
                reads=[(obuf, cpar * 1024 + h * 128,
                        cpar * 1024 + (h + 1) * 128),
                       (cols, 48 + par, 49 + par),
                       (zsil, cpar * 1024 + h * 128,
                        cpar * 1024 + (h + 1) * 128)],
                writes=[(obuf, cpar * 1024 + h * 128,
                         cpar * 1024 + (h + 1) * 128)],
                label=lh + "-om")
        yield

    for s in range(B_CORE):
        prog.op("vector", lambda e: e.memset(S.t[:], 0.0),
                writes=[(S, 0, 1024)], label=label + "-S0")
        prog.op("vector", lambda e: e.memset(Sb.t[:], 0.0),
                writes=[(Sb, 0, 1024)], label=label + "-Sb0")
        prog.op("vector", lambda e: e.memset(mpre.t[:, 0:24 * 516], 0.0),
                writes=[(mpre, 0, 24 * 516)], label=label + "-halo0")
        for wi_ in range(T_SEQ // WDN):
            t0 = s * T_SEQ + wi_ * WDN
            lb = f"{label}s{s}w{wi_}"
            emit_ln_tile(prog, cst, x_in, t0, xresid, crows,
                         bufs["xcol2"], lnbuf, omf, lnps, label=lb)
            if wi_ > 0:
                for c in range(24):
                    prog.op("vector",
                            lambda e, c=c: e.tensor_copy(
                                mpre.t[:, c * 516:c * 516 + 3],
                                mpre.t[:, c * 516 + 512:c * 516 + 515]),
                            reads=[(mpre, c * 516 + 512, c * 516 + 515)],
                            writes=[(mpre, c * 516, c * 516 + 3)],
                            label=lb + "-halo")
            for c in range(24):
                bk = ps["p0" if c % 2 == 0 else "p1"]
                prog.mm_group(bk, [
                    (lambda e, fc=fc, c=c, bk=bk: e.matmul(
                        bk.t[:],
                        w["qkvT"].t[:, fc * 3072 + c * 128:fc * 3072 + c * 128 + 128],
                        lnbuf.t[:, fc * NT:(fc + 1) * NT],
                        start=(fc == 0), stop=(fc == 7)),
                     [(w["qkvT"], fc * 3072 + c * 128, fc * 3072 + c * 128 + 128),
                      (lnbuf, fc * NT, (fc + 1) * NT)])
                    for fc in range(8)], label=lb + "-qkv")
                prog.op("scalar",
                        lambda e, c=c, bk=bk: e.activation(
                            out=mpre.t[:, c * 516 + 3:c * 516 + 3 + 512],
                            in_=bk.t[:], func=AF.Identity),
                        reads=[(bk, 0, NT)],
                        writes=[(mpre, c * 516 + 3, c * 516 + 516)],
                        label=lb + "-qkve")
            for which, rows0 in (("b", 0), ("a", 8)):
                bk = ps["p4"] if which == "b" else ps["p5"]
                parts = [
                    (lambda e, fc=fc, rows0=rows0, bk=bk: e.matmul(
                        bk.t[0:8, 0:WDN],
                        w["abT"].t[:, fc * 16 + rows0:fc * 16 + rows0 + 8],
                        lnbuf.t[:, fc * NT:(fc + 1) * NT],
                        start=(fc == 0), stop=(fc == 7 and rows0 == 0)),
                     [(w["abT"], fc * 16 + rows0, fc * 16 + rows0 + 8),
                      (lnbuf, fc * NT, (fc + 1) * NT)])
                    for fc in range(8)]
                if rows0 == 8:
                    parts.append(
                        (lambda e, bk=bk: e.matmul(
                            bk.t[0:8, 0:WDN], w["dtbrow"].t[0:1, 8:16],
                            cst["ones_row"].t[0:1, 0:WDN],
                            start=False, stop=True),
                         [(w["dtbrow"], 8, 16), (cst["ones_row"], 0, WDN)]))
                prog.mm_group(bk, parts, label=lb + "-ab" + which)
            prog.op("scalar",
                    lambda e: e.activation(out=brow.t[:],
                                           in_=ps["p4"].t[0:8, 0:WDN],
                                           func=AF.Sigmoid),
                    reads=[(ps["p4"], 0, NT)], writes=[(brow, 0, WDN)],
                    label=lb + "-sig")
            prog.op("scalar",
                    lambda e: e.activation(out=srow_ap(),
                                           in_=ps["p5"].t[0:8, 0:WDN],
                                           func=AF.Abs),
                    reads=[(ps["p5"], 0, NT)], writes=[SR], label=lb + "-abs")
            prog.op("scalar",
                    lambda e: e.activation(out=srow_ap(), in_=srow_ap(),
                                           func=AF.Exp, scale=-1.0),
                    reads=[SR], writes=[SR], label=lb + "-en")
            prog.op("scalar",
                    lambda e: e.activation(out=srow_ap(), in_=srow_ap(),
                                           func=AF.Ln, bias=1.0),
                    reads=[SR], writes=[SR], label=lb + "-l1p")
            prog.op("vector",
                    lambda e: e.tensor_scalar(out=trow_ap(),
                                              in0=ps["p5"].t[0:8, 0:WDN],
                                              scalar1=0.0, scalar2=None,
                                              op0=ALU.max),
                    reads=[(ps["p5"], 0, NT)], writes=[TR], label=lb + "-relu")
            prog.op("vector",
                    lambda e: e.tensor_add(grow.t[:], srow_ap(), trow_ap()),
                    reads=[SR, TR], writes=[(grow, 0, WDN)], label=lb + "-sp")
            prog.op("vector",
                    lambda e: e.tensor_scalar(out=grow.t[:], in0=grow.t[:],
                                              scalar1=w["nega"].t[:, 0:1],
                                              scalar2=None, op0=ALU.mult),
                    reads=[(grow, 0, WDN), (w["nega"], 0, 1)],
                    writes=[(grow, 0, WDN)], label=lb + "-g")
            for ch in range(4):
                prog.op("vector",
                        lambda e, ch=ch: e.tensor_tensor_scan(
                            out=crow.t[:, ch * CDN:(ch + 1) * CDN],
                            data0=grow.t[:, ch * CDN:(ch + 1) * CDN],
                            data1=zrow.t[:, 0:CDN],
                            initial=0.0, op0=ALU.add, op1=ALU.add),
                        reads=[(grow, ch * CDN, (ch + 1) * CDN), (zrow, 0, CDN)],
                        writes=[(crow, ch * CDN, (ch + 1) * CDN)],
                        label=lb + "-cs")
            for ch in range(4):
                lc = lb + f"c{ch}"
                par = ch % 2
                mb = par * 24
                for c in range(24):
                    prog.op("vector",
                            lambda e, c=c, ch=ch, mb=mb: e.tensor_scalar(
                                out=mpost.t[:, (mb + c) * 128:(mb + c + 1) * 128],
                                in0=mpre.t[:, c * 516 + ch * 128:c * 516 + ch * 128 + 128],
                                scalar1=w["convw"].t[:, c * 4:c * 4 + 1],
                                scalar2=None, op0=ALU.mult),
                            reads=[(mpre, c * 516 + ch * 128,
                                    c * 516 + ch * 128 + 128),
                                   (w["convw"], c * 4, c * 4 + 1)],
                            writes=[(mpost, (mb + c) * 128, (mb + c + 1) * 128)],
                            label=lc + "-cv0")
                    for j in (1, 2, 3):
                        prog.op("vector",
                                lambda e, c=c, j=j, ch=ch, mb=mb:
                                e.scalar_tensor_tensor(
                                    mpost.t[:, (mb + c) * 128:(mb + c + 1) * 128],
                                    mpre.t[:, c * 516 + ch * 128 + j:c * 516 + ch * 128 + j + 128],
                                    w["convw"].t[:, c * 4 + j:c * 4 + j + 1],
                                    mpost.t[:, (mb + c) * 128:(mb + c + 1) * 128],
                                    op0=ALU.mult, op1=ALU.add),
                                reads=[(mpre, c * 516 + ch * 128,
                                        c * 516 + ch * 128 + 131),
                                       (w["convw"], c * 4 + j, c * 4 + j + 1),
                                       (mpost, (mb + c) * 128, (mb + c + 1) * 128)],
                                writes=[(mpost, (mb + c) * 128,
                                         (mb + c + 1) * 128)],
                                label=lc + f"-cv{j}")
                prog.pe_op(lambda e, ch=ch: e.transpose(
                    ps["p0"].t[0:128, 0:8], crow.t[:, ch * CDN:(ch + 1) * CDN],
                    cst["identf"].t[0:8, 0:8]),
                    reads=[(crow, ch * CDN, (ch + 1) * CDN),
                           (cst["identf"], 0, 128)],
                    writes=[(ps["p0"], 0, NT)], label=lc + "-tc")
                prog.op("vector", lambda e: e.tensor_copy(cols.t[:, 0:8],
                                                          ps["p0"].t[0:128, 0:8]),
                        reads=[(ps["p0"], 0, NT)], writes=[(cols, 0, 8)],
                        label=lc + "-ccol")
                prog.pe_op(lambda e, ch=ch: e.transpose(
                    ps["p0"].t[0:128, 0:8], brow.t[:, ch * CDN:(ch + 1) * CDN],
                    cst["identf"].t[0:8, 0:8]),
                    reads=[(brow, ch * CDN, (ch + 1) * CDN),
                           (cst["identf"], 0, 128)],
                    writes=[(ps["p0"], 0, NT)], label=lc + "-tb")
                prog.op("vector", lambda e: e.tensor_copy(cols.t[:, 8:16],
                                                          ps["p0"].t[0:128, 0:8]),
                        reads=[(ps["p0"], 0, NT)], writes=[(cols, 8, 16)],
                        label=lc + "-bcol")
                for h in range(NH):
                    bko = ps["p0"] if h < 4 else ps["p1"]
                    prog.pe_op(lambda e, h=h, bko=bko: e.transpose(
                        bko.t[0:1, (h % 4) * 128:(h % 4 + 1) * 128],
                        cols.t[:, h:h + 1], cst["identf"].t[:]),
                        reads=[(cols, h, h + 1), (cst["identf"], 0, 128)],
                        writes=[(bko, 0, NT)], label=lc + "-tr")
                prog.op("vector", lambda e: e.tensor_copy(crows.t[0:1, 0:512],
                                                          ps["p0"].t[0:1, 0:512]),
                        reads=[(ps["p0"], 0, NT)], writes=[(crows, 0, 512)],
                        label=lc + "-crA")
                prog.op("vector", lambda e: e.tensor_copy(crows.t[0:1, 512:1024],
                                                          ps["p1"].t[0:1, 0:512]),
                        reads=[(ps["p1"], 0, NT)], writes=[(crows, 512, 1024)],
                        label=lc + "-crB")
                prog.op("scalar", lambda e: e.activation(
                    out=cols.t[:, 16:24], in_=cols.t[:, 0:8], func=AF.Exp),
                    reads=[(cols, 0, 8)], writes=[(cols, 16, 24)],
                    label=lc + "-ecol")
                prog.op("vector", lambda e: e.tensor_scalar(
                    out=cols.t[:, 24:32], in0=cols.t[:, 0:8], scalar1=-1.0,
                    scalar2=None, op0=ALU.mult),
                    reads=[(cols, 0, 8)], writes=[(cols, 24, 32)],
                    label=lc + "-ncc")
                prog.op("vector", lambda e: e.tensor_scalar(
                    out=cols.t[:, 32:40], in0=cols.t[:, 16:24], scalar1=-1.0,
                    scalar2=None, op0=ALU.mult),
                    reads=[(cols, 16, 24)], writes=[(cols, 32, 40)],
                    label=lc + "-nec")
                prog.mm_group(ps["p0"], [
                    (lambda e: e.matmul(
                        ps["p0"].t[0:128, 0:8], w["onesf"].t[0:1, 0:128],
                        crows.t[0:1, 127:1024:128], start=True, stop=True),
                     [(crows, 0, 1024), (w["onesf"], 0, 128)])], label=lc + "-ccb")
                prog.op("scalar", lambda e: e.activation(
                    out=cols.t[:, 40:48], in_=ps["p0"].t[0:128, 0:8],
                    func=AF.Exp),
                    reads=[(ps["p0"], 0, NT)], writes=[(cols, 40, 48)],
                    label=lc + "-eCC")
                for half in range(2):
                    bk = ps["p0"] if half == 0 else ps["p1"]
                    prog.mm_group(bk, [
                        (lambda e, fc=fc, ch=ch, half=half, bk=bk: e.matmul(
                            bk.t[:],
                            lnbuf.t[:, fc * NT + ch * CDN:fc * NT + ch * CDN + 128],
                            w["zwT"].t[:, fc * 1024 + half * 512:fc * 1024 + (half + 1) * 512],
                            start=(fc == 0), stop=(fc == 7)),
                         [(lnbuf, fc * NT + ch * CDN, fc * NT + ch * CDN + 128),
                          (w["zwT"], fc * 1024 + half * 512,
                           fc * 1024 + (half + 1) * 512)])
                        for fc in range(8)], label=lc + f"-z{half}")
                    prog.op("scalar",
                            lambda e, par=par, half=half, bk=bk: e.activation(
                                out=zsil.t[:, par * 1024 + half * 512:par * 1024 + (half + 1) * 512],
                                in_=bk.t[:], func=AF.Silu),
                            reads=[(bk, 0, NT)],
                            writes=[(zsil, par * 1024 + half * 512,
                                     par * 1024 + (half + 1) * 512)],
                            label=lc + f"-zsil{half}")
                for hp in range(NH // 2):
                    g0 = head_chain(2 * hp, ch, 0, par, lc)
                    g1 = head_chain(2 * hp + 1, ch, 1, par, lc)
                    a_done = b_done = False
                    while not (a_done and b_done):
                        if not a_done:
                            try:
                                next(g0)
                            except StopIteration:
                                a_done = True
                        if not b_done:
                            try:
                                next(g1)
                            except StopIteration:
                                b_done = True
                for vc in range(8):
                    prog.pe_op(lambda e, vc=vc, par=par: e.transpose(
                        ps["b0"].t[:, 384:512],
                        obuf.t[:, par * 1024 + vc * 128:par * 1024 + (vc + 1) * 128],
                        cst["ident"].t[:]),
                        reads=[(obuf, par * 1024 + vc * 128,
                                par * 1024 + (vc + 1) * 128),
                               (cst["ident"], 0, 128)],
                        writes=[(ps["b0"], 0, 1024)], label=lc + "-ot")
                    prog.op("vector",
                            lambda e, vc=vc, ch=ch: e.tensor_copy(
                                omf.t[:, vc * NT + ch * CDN:vc * NT + ch * CDN + 128],
                                ps["b0"].t[:, 384:512]),
                            reads=[(ps["b0"], 0, 1024)],
                            writes=[(omf, vc * NT + ch * CDN,
                                     vc * NT + ch * CDN + 128)],
                            label=lc + "-ote")
            for fc in range(8):
                bk = ps["p0" if fc % 2 == 0 else "p1"]
                prog.mm_group(bk, [
                    (lambda e, vc=vc, fc=fc, bk=bk: e.matmul(
                        bk.t[:],
                        w["owT"].t[:, vc * 1024 + fc * 128:vc * 1024 + fc * 128 + 128],
                        omf.t[:, vc * NT:(vc + 1) * NT],
                        start=(vc == 0), stop=(vc == 7)),
                     [(w["owT"], vc * 1024 + fc * 128, vc * 1024 + fc * 128 + 128),
                      (omf, vc * NT, (vc + 1) * NT)])
                    for vc in range(8)], label=lb + "-ow")
                prog.op("vector",
                        lambda e, fc=fc, bk=bk: e.tensor_tensor(
                            xresid.t[:, fc * NT:(fc + 1) * NT],
                            xresid.t[:, fc * NT:(fc + 1) * NT], bk.t[:],
                            op=ALU.add),
                        reads=[(bk, 0, NT), (xresid, fc * NT, (fc + 1) * NT)],
                        writes=[(xresid, fc * NT, (fc + 1) * NT)],
                        label=lb + "-res")
            prog.dma_batch("qa", [
                (x_tile_ap(x_out, t0), sb3(xresid, 8), [(xresid, 0, 8 * NT)],
                 [(x_out, t0, t0 + NT)]),
            ], label=lb + "-xst")


def host_dn_weights(inputs):
    qkv = inputs["L0_attn_qkv_w"]
    zw = inputs["L0_attn_z_w"]
    bw = inputs["L0_attn_b_w"]
    aw = inputs["L0_attn_a_w"]
    cw = inputs["L0_attn_conv_w"]
    dtb = inputs["L0_attn_dt_bias"]
    alog = inputs["L0_attn_A_log"]
    nw = inputs["L0_attn_norm_w"]
    ow = inputs["L0_attn_out_w"]
    i128 = np.arange(128)
    dtrow = np.zeros((1, 16), np.float32)
    dtrow[0, 8:16] = dtb
    return {
        "w_qkvT": np.ascontiguousarray(qkv.T.astype(np.float32)),
        "w_zwT": np.ascontiguousarray(zw.T.astype(np.float32)),
        "w_abT": np.ascontiguousarray(
            np.concatenate([bw, aw], 0).T.astype(np.float32)),
        "w_owT": np.ascontiguousarray(ow.T.astype(np.float32)),
        "w_conv": np.ascontiguousarray(
            cw.reshape(24, 128, 4).transpose(1, 0, 2).reshape(128, 96)
            .astype(np.float32)),
        "w_dtbrow": dtrow,
        "w_alog": np.ascontiguousarray(alog.reshape(8, 1).astype(np.float32)),
        "w_normw": np.ascontiguousarray(nw.reshape(128, 1).astype(np.float32)),
        "c_masksT": np.ascontiguousarray(
            (i128[:, None] < i128[None, :]).astype(np.float32)),
        "c_maskiT": np.ascontiguousarray(
            (i128[:, None] <= i128[None, :]).astype(np.float32)),
        "c_onesf": np.ones((1, 128), np.float32),
    }



ATT_SCALE = float(1.0 / np.sqrt(64 * 2.0))
ATT_TEMP = 1.5


def declare_att_drams(env):
    d = {}
    for nm in ("WqT", "WkT", "WvT", "WoT"):
        d[nm] = env.dram("w_" + nm, (1024, 1024), F32, "ExternalInput")
    d["d_bq"] = env.dram("w_bqc", (128, 8), F32, "ExternalInput")
    d["d_bk"] = env.dram("w_bkc", (128, 8), F32, "ExternalInput")
    d["d_bv"] = env.dram("w_bvr", (1, 1024), F32, "ExternalInput")
    d["d_bo"] = env.dram("w_bor", (1, 1024), F32, "ExternalInput")
    d["d_msk"] = env.dram("c_att", (128, 5 * 512), F32, "ExternalInput")
    return d


def stage_att_weights(prog, env, dr=None):
    if dr is None:
        dr = declare_att_drams(env)
    d = dict(dr)
    d_bq, d_bk, d_bv, d_bo, d_msk = (dr["d_bq"], dr["d_bk"], dr["d_bv"],
                                     dr["d_bo"], dr["d_msk"])
    bq = env.sbuf("bqc", (128, 8), F32)
    bk = env.sbuf("bkc", (128, 8), F32)
    bv = env.sbuf("bvr", (1, 1024), BF16)
    bo = env.sbuf("bor", (1, 1024), BF16)
    msk = env.sbuf("attmsk", (128, 5 * 512), F32)
    prog.dma_batch("qg", [
        (bq.t[:], d_bq.ap, [], [(bq, 0, 8)]),
        (bk.t[:], d_bk.ap, [], [(bk, 0, 8)]),
        (bv.t[:], d_bv.ap, [], [(bv, 0, 1024)]),
        (bo.t[:], d_bo.ap, [], [(bo, 0, 1024)]),
        (msk.t[:], d_msk.ap, [], [(msk, 0, 2560)]),
    ], label="att-w")
    d.update(bq=bq, bk=bk, bv=bv, bo=bo, msk=msk)
    return d


def make_att_bufs(env):
    ps = {f"p{i}": env.psum(f"aps_p{i}", (128, NT), F32) for i in range(8)}
    return dict(
        wA=env.sbuf("awA", (128, 8 * 1024), BF16),
        wB=env.sbuf("awB", (128, 8 * 1024), BF16),
        xlnS=env.sbuf("axln", (128, 8 * 2048), BF16),
        kbuf=env.sbuf("akbuf", (128, 8 * 2048), BF16),
        vbuf=env.sbuf("avbuf", (128, 16 * 1024), BF16),
        qbuf=env.sbuf("aqbuf", (128, 8 * NT), BF16),
        obuf=env.sbuf("aobuf", (128, 8 * NT), BF16),
        pbuf=env.sbuf("apbuf", (128, 4 * NT), BF16),
        sqb=env.sbuf("asqb", (128, NT), F32),
        bcb=env.sbuf("abcb", (128, NT), F32),
        bcb2=env.sbuf("abcb2", (128, NT), F32),
        bcb3=env.sbuf("abcb3", (128, NT), F32),
        bcb4=env.sbuf("abcb4", (128, NT), F32),
        xresid=env.sbuf("axresid", (128, 8 * NT), F32),
        xcol=env.sbuf("axcol", (1, 2 * NT), F32),
        xcol2=env.sbuf("axcol2", (1, 2 * NT), BF16),
        ps=ps,
    )


def _load_att_w(prog, wbuf, dram, label):
    prog.dma_batch("qg", [
        (sb3(wbuf, 8, 1024), dram.ap.rearrange("(c p) o -> p c o", p=128),
         [], [(wbuf, 0, 8192)]),
    ], label=label)


def emit_att_sublayer(prog, cst, w, bufs, x_in, x_out, label="at"):
    ps = bufs["ps"]
    wA, wB = bufs["wA"], bufs["wB"]
    xlnS, kbuf, vbuf, qbuf = bufs["xlnS"], bufs["kbuf"], bufs["vbuf"], bufs["qbuf"]
    obuf, pbuf, sqb, bcb = bufs["obuf"], bufs["pbuf"], bufs["sqb"], bufs["bcb"]
    xresid, xcol, xcol2 = bufs["xresid"], bufs["xcol"], bufs["xcol2"]
    lnps = dict(stat1=ps["p0"], stat2=ps["p1"], bc1=ps["p2"], bc2=ps["p3"])
    NTQ = T_SEQ // NT

    def proj_norm(lb, Wsb, bcol, src_off, dst_buf, dst_cols):
        for rc in range(8):
            bk_ = ps["p4" if rc % 2 == 0 else "p5"]
            prog.mm_group(bk_, [
                (lambda e, fc=fc, rc=rc, bk_=bk_: e.matmul(
                    bk_.t[:],
                    Wsb.t[:, fc * 1024 + rc * 128:fc * 1024 + rc * 128 + 128],
                    xlnS.t[:, fc * 2048 + src_off:fc * 2048 + src_off + NT],
                    start=(fc == 0), stop=(fc == 7)),
                 [(Wsb, fc * 1024 + rc * 128, fc * 1024 + rc * 128 + 128),
                  (xlnS, fc * 2048 + src_off, fc * 2048 + src_off + NT)])
                for fc in range(8)], label=lb + "-pj")
            prog.op("scalar",
                    lambda e, rc=rc, bk_=bk_: e.activation(
                        out=sqb.t[:], in_=bk_.t[:], func=AF.Identity,
                        bias=bcol.t[:, rc:rc + 1]),
                    reads=[(bk_, 0, NT), (bcol, rc, rc + 1)],
                    writes=[(sqb, 0, NT)], label=lb + "-ev")
            prog.op("vector",
                    lambda e: e.tensor_mul(pbuf.t[:, 0:NT], sqb.t[:], sqb.t[:]),
                    reads=[(sqb, 0, NT)], writes=[(pbuf, 0, NT)],
                    label=lb + "-sq")
            for sub in range(2):
                b0 = sub * 64
                bnk = ps["p6"] if sub == 0 else ps["p7"]
                prog.mm_group(bnk, [
                    (lambda e, b0=b0, bnk=bnk: e.matmul(
                        bnk.t[0:1, 0:NT], cst["ones_col"].t[b0:b0 + 64, 0:1],
                        pbuf.t[b0:b0 + 64, 0:NT], start=True, stop=True),
                     [(pbuf, 0, NT), (cst["ones_col"], 0, 1)])],
                    label=lb + "-ss")
                prog.op("scalar",
                        lambda e, bnk=bnk: e.activation(
                            out=xcol.t[0:1, 0:NT], in_=bnk.t[0:1, 0:NT],
                            func=AF.Ln, bias=cst["misc"].t[0:1, 3:4]),
                        reads=[(bnk, 0, NT), (cst["misc"], 3, 4)],
                        writes=[(xcol, 0, NT)], label=lb + "-ln")
                prog.op("scalar",
                        lambda e: e.activation(out=xcol.t[0:1, 0:NT],
                                               in_=xcol.t[0:1, 0:NT],
                                               func=AF.Exp, scale=-0.5),
                        reads=[(xcol, 0, NT)], writes=[(xcol, 0, NT)],
                        label=lb + "-ex")
                prog.op("vector",
                        lambda e: e.tensor_scalar(
                            out=xcol2.t[0:1, 0:NT], in0=xcol.t[0:1, 0:NT],
                            scalar1=1e8, scalar2=None, op0=ALU.min),
                        reads=[(xcol, 0, NT)], writes=[(xcol2, 0, NT)],
                        label=lb + "-mn")
                prog.mm_group(bnk, [
                    (lambda e, bnk=bnk: e.matmul(
                        bnk.t[0:64, 0:NT], cst["ones_row"].t[0:1, 0:64],
                        xcol2.t[0:1, 0:NT], start=True, stop=True),
                     [(xcol2, 0, NT), (cst["ones_row"], 0, 64)])],
                    label=lb + "-bc")
                prog.op("vector",
                        lambda e, rc=rc, b0=b0, bnk=bnk,
                        dc=dst_cols[rc]: e.tensor_tensor(
                            dst_buf.t[b0:b0 + 64, dc:dc + NT],
                            sqb.t[b0:b0 + 64, 0:NT], bnk.t[0:64, 0:NT],
                            op=ALU.mult),
                        reads=[(sqb, 0, NT), (bnk, 0, NT)],
                        writes=[(dst_buf, dst_cols[rc], dst_cols[rc] + NT)],
                        label=lb + "-nm")

    for s in range(B_CORE):
        sb_ = s * T_SEQ
        lbS = f"{label}s{s}"
        _load_att_w(prog, wA, w["WkT"], lbS + "-wk")
        _load_att_w(prog, wB, w["WvT"], lbS + "-wv")
        for ti in range(NTQ):
            t0 = sb_ + ti * NT
            lb = f"{lbS}l{ti}"
            emit_ln_tile(prog, cst, x_in, t0, xresid, xcol, xcol2, qbuf,
                         obuf, lnps, label=lb)
            for fc in range(8):
                prog.op("vector",
                        lambda e, fc=fc, ti=ti: e.tensor_copy(
                            xlnS.t[:, fc * 2048 + ti * NT:fc * 2048 + (ti + 1) * NT],
                            qbuf.t[:, fc * NT:(fc + 1) * NT]),
                        reads=[(qbuf, fc * NT, (fc + 1) * NT)],
                        writes=[(xlnS, fc * 2048 + ti * NT,
                                 fc * 2048 + (ti + 1) * NT)],
                        label=lb + "-plc")
        for ti in range(NTQ):
            proj_norm(f"{lbS}k{ti}", wA, w["bk"], ti * NT, kbuf,
                      [rc * 2048 + ti * NT for rc in range(8)])
        for ti in range(NTQ):
            lb = f"{lbS}v{ti}"
            for tc4 in range(4):
                tc = ti * 4 + tc4
                for half in range(2):
                    bk_ = ps["p4"] if half == 0 else ps["p5"]
                    parts = [
                        (lambda e, fc=fc, tc=tc, half=half, bk_=bk_: e.matmul(
                            bk_.t[:],
                            xlnS.t[:, fc * 2048 + tc * 128:fc * 2048 + tc * 128 + 128],
                            wB.t[:, fc * 1024 + half * 512:fc * 1024 + (half + 1) * 512],
                            start=(fc == 0), stop=False),
                         [(xlnS, fc * 2048 + tc * 128, fc * 2048 + tc * 128 + 128),
                          (wB, fc * 1024 + half * 512,
                           fc * 1024 + (half + 1) * 512)])
                        for fc in range(8)]
                    parts.append(
                        (lambda e, half=half, bk_=bk_: e.matmul(
                            bk_.t[:], cst["ones_row"].t[0:1, 0:128],
                            w["bv"].t[0:1, half * 512:(half + 1) * 512],
                            start=False, stop=True),
                         [(cst["ones_row"], 0, 128),
                          (w["bv"], half * 512, (half + 1) * 512)]))
                    prog.mm_group(bk_, parts, label=lb + f"-v{half}")
                    prog.op("vector",
                            lambda e, tc=tc, half=half, bk_=bk_: e.tensor_copy(
                                vbuf.t[:, tc * 1024 + half * 512:tc * 1024 + (half + 1) * 512],
                                bk_.t[:]),
                            reads=[(bk_, 0, NT)],
                            writes=[(vbuf, tc * 1024 + half * 512,
                                     tc * 1024 + (half + 1) * 512)],
                            label=lb + f"-ve{half}")
        _load_att_w(prog, wA, w["WqT"], lbS + "-wq")
        _load_att_w(prog, wB, w["WoT"], lbS + "-wo")
        for qt in range(NTQ):
            lbq = f"{lbS}q{qt}"
            proj_norm(lbq, wA, w["bq"], qt * NT, qbuf,
                      [rc * NT for rc in range(8)])
            nkc = (qt + 1) * 4

            def head_att(h, lbq=lbq, qt=qt, nkc=nkc):
                lbh = lbq + f"h{h}"
                rc, b0 = h // 2, (h % 2) * 64
                par = h % 2
                scs = (ps["p0"], ps["p3"]) if par == 0 else (ps["p4"], ps["p7"])
                av = ps["p1"] if par == 0 else ps["p5"]
                lsum = ps["p2"] if par == 0 else ps["p6"]
                pep = ps["p3"] if par == 0 else ps["p7"]
                cb = bufs["bcb"] if par == 0 else bufs["bcb2"]
                cb2 = bufs["bcb3"] if par == 0 else bufs["bcb4"]
                xo0 = NT if par == 0 else 0
                def emit_qk(kc):
                    sc = scs[kc % 2]
                    prog.mm_group(sc, [
                        (lambda e, rc=rc, b0=b0, kc=kc, sc=sc: e.matmul(
                            sc.t[:],
                            kbuf.t[b0:b0 + 64,
                                   rc * 2048 + kc * 128:rc * 2048 + kc * 128 + 128],
                            qbuf.t[b0:b0 + 64, rc * NT:(rc + 1) * NT],
                            start=True, stop=True),
                         [(kbuf, rc * 2048 + kc * 128,
                           rc * 2048 + kc * 128 + 128),
                          (qbuf, rc * NT, (rc + 1) * NT)])],
                        label=lbh + f"k{kc}-qk")

                def emit_clmk(kc):
                    sc = scs[kc % 2]
                    cb = (bufs["bcb"], bufs["bcb3"])[kc % 2] if par == 0 else \
                         (bufs["bcb2"], bufs["bcb4"])[kc % 2]
                    rel = kc * 128 - qt * NT
                    mi = rel // 128 if 0 <= rel < NT else 4
                    prog.op("vector",
                            lambda e, sc=sc, cb=cb: e.tensor_scalar(
                                out=cb.t[:], in0=sc.t[:], scalar1=ATT_SCALE,
                                scalar2=-5.0, op0=ALU.mult, op1=ALU.max),
                            reads=[(sc, 0, NT)], writes=[(cb, 0, NT)],
                            label=lbh + f"k{kc}-cl")
                    prog.op("vector",
                            lambda e, mi=mi, cb=cb: e.tensor_tensor(
                                cb.t[:], cb.t[:],
                                w["msk"].t[:, mi * 512:(mi + 1) * 512],
                                op=ALU.min),
                            reads=[(cb, 0, NT),
                                   (w["msk"], mi * 512, (mi + 1) * 512)],
                            writes=[(cb, 0, NT)], label=lbh + f"k{kc}-mk")

                def emit_ex(kc):
                    cb = (bufs["bcb"], bufs["bcb3"])[kc % 2] if par == 0 else \
                         (bufs["bcb2"], bufs["bcb4"])[kc % 2]
                    pslot = par * 2 + (kc % 2)
                    prog.op("scalar",
                            lambda e, pslot=pslot, cb=cb: e.activation(
                                out=pbuf.t[:, pslot * NT:(pslot + 1) * NT],
                                in_=cb.t[:], func=AF.Exp,
                                scale=1.0 / ATT_TEMP),
                            reads=[(cb, 0, NT)],
                            writes=[(pbuf, pslot * NT, (pslot + 1) * NT)],
                            label=lbh + f"k{kc}-ex")

                def emit_lsav(kc, first):
                    pslot = par * 2 + (kc % 2)
                    prog.mm_group(lsum, [
                        (lambda e, pslot=pslot, lsum=lsum, first=first,
                         last=(kc == nkc - 1): e.matmul(
                            lsum.t[0:1, 0:NT], cst["ones_col"].t[:, 0:1],
                            pbuf.t[:, pslot * NT:(pslot + 1) * NT],
                            start=first, stop=last),
                         [(pbuf, pslot * NT, (pslot + 1) * NT),
                          (cst["ones_col"], 0, 1)])], label=lbh + f"k{kc}-ls")
                    prog.mm_group(av, [
                        (lambda e, kc=kc, h=h, pslot=pslot, av=av, first=first,
                         last=(kc == nkc - 1): e.matmul(
                            av.t[0:64, 0:NT],
                            vbuf.t[:, kc * 1024 + h * 64:kc * 1024 + h * 64 + 64],
                            pbuf.t[:, pslot * NT:(pslot + 1) * NT],
                            start=first, stop=last),
                         [(vbuf, kc * 1024 + h * 64, kc * 1024 + h * 64 + 64),
                          (pbuf, pslot * NT, (pslot + 1) * NT)])],
                        label=lbh + f"k{kc}-av")

                emit_qk(0)
                yield
                emit_clmk(0)
                yield
                emit_ex(0)
                yield
                for kc in range(nkc):
                    if kc + 1 < nkc:
                        emit_qk(kc + 1)
                        yield
                        emit_clmk(kc + 1)
                        yield
                    emit_lsav(kc, kc == 0)
                    yield
                    if kc + 1 < nkc:
                        emit_ex(kc + 1)
                        yield
                prog.op("scalar",
                        lambda e, lsum=lsum, xo0=xo0: e.activation(
                            out=xcol.t[0:1, xo0:xo0 + NT], in_=lsum.t[0:1, 0:NT],
                            func=AF.Ln),
                        reads=[(lsum, 0, NT)], writes=[(xcol, xo0, xo0 + NT)],
                        label=lbh + "-rcl")
                yield
                prog.op("scalar",
                        lambda e, xo0=xo0: e.activation(
                            out=xcol.t[0:1, xo0:xo0 + NT],
                            in_=xcol.t[0:1, xo0:xo0 + NT],
                            func=AF.Exp, scale=-1.0),
                        reads=[(xcol, xo0, xo0 + NT)],
                        writes=[(xcol, xo0, xo0 + NT)],
                        label=lbh + "-rce")
                yield
                prog.op("vector",
                        lambda e, xo0=xo0: e.tensor_copy(
                            xcol2.t[0:1, xo0:xo0 + NT],
                            xcol.t[0:1, xo0:xo0 + NT]),
                        reads=[(xcol, xo0, xo0 + NT)],
                        writes=[(xcol2, xo0, xo0 + NT)], label=lbh + "-r16")
                yield
                prog.mm_group(pep, [
                    (lambda e, pep=pep, xo0=xo0: e.matmul(
                        pep.t[0:64, 0:NT], cst["ones_row"].t[0:1, 0:64],
                        xcol2.t[0:1, xo0:xo0 + NT], start=True, stop=True),
                     [(xcol2, xo0, xo0 + NT), (cst["ones_row"], 0, 64)])],
                    label=lbh + "-bc")
                yield
                prog.op("vector",
                        lambda e, pep=pep, cb=cb: e.tensor_copy(
                            cb.t[0:64, 0:NT], pep.t[0:64, 0:NT]),
                        reads=[(pep, 0, NT)], writes=[(cb, 0, NT)],
                        label=lbh + "-bce")
                yield
                prog.op("vector",
                        lambda e, rc=rc, b0=b0, av=av, cb=cb: e.tensor_tensor(
                            obuf.t[b0:b0 + 64, rc * NT:(rc + 1) * NT],
                            cb.t[0:64, 0:NT], av.t[0:64, 0:NT], op=ALU.mult),
                        reads=[(cb, 0, NT), (av, 0, NT)],
                        writes=[(obuf, rc * NT, (rc + 1) * NT)],
                        label=lbh + "-o")
                yield

            for hp in range(8):
                g0 = head_att(2 * hp)
                g1 = head_att(2 * hp + 1)
                d0 = d1 = False
                while not (d0 and d1):
                    if not d0:
                        try:
                            next(g0)
                        except StopIteration:
                            d0 = True
                    if not d1:
                        try:
                            next(g1)
                        except StopIteration:
                            d1 = True
            t0 = sb_ + qt * NT
            prog.dma_batch("qs", [
                (sb3(xresid, 8), x_tile_ap(x_in, t0), [(x_in, t0, t0 + NT)],
                 [(xresid, 0, 8 * NT)]),
            ], label=lbq + "-xr")
            for fc in range(8):
                bk_ = ps["p4" if fc % 2 == 0 else "p5"]
                parts = [
                    (lambda e, rc=rc, fc=fc, bk_=bk_: e.matmul(
                        bk_.t[:],
                        wB.t[:, rc * 1024 + fc * 128:rc * 1024 + fc * 128 + 128],
                        obuf.t[:, rc * NT:(rc + 1) * NT],
                        start=(rc == 0), stop=False),
                     [(wB, rc * 1024 + fc * 128, rc * 1024 + fc * 128 + 128),
                      (obuf, rc * NT, (rc + 1) * NT)])
                    for rc in range(8)]
                parts.append(
                    (lambda e, fc=fc, bk_=bk_: e.matmul(
                        bk_.t[:], w["bo"].t[0:1, fc * 128:(fc + 1) * 128],
                        cst["ones_row"].t[0:1, 0:NT], start=False, stop=True),
                     [(w["bo"], fc * 128, (fc + 1) * 128),
                      (cst["ones_row"], 0, NT)]))
                prog.mm_group(bk_, parts, label=lbq + "-ow")
                prog.op("vector",
                        lambda e, fc=fc, bk_=bk_: e.scalar_tensor_tensor(
                            xresid.t[:, fc * NT:(fc + 1) * NT], bk_.t[:], 0.5,
                            xresid.t[:, fc * NT:(fc + 1) * NT],
                            op0=ALU.mult, op1=ALU.add),
                        reads=[(bk_, 0, NT), (xresid, fc * NT, (fc + 1) * NT)],
                        writes=[(xresid, fc * NT, (fc + 1) * NT)],
                        label=lbq + "-rs")
            prog.dma_batch("qa", [
                (x_tile_ap(x_out, t0), sb3(xresid, 8), [(xresid, 0, 8 * NT)],
                 [(x_out, t0, t0 + NT)]),
            ], label=lbq + "-st")


def host_att_weights(inputs):
    return {
        "w_WqT": np.ascontiguousarray(inputs["L1_attn_Wq"].T.astype(np.float32)),
        "w_WkT": np.ascontiguousarray(inputs["L1_attn_Wk"].T.astype(np.float32)),
        "w_WvT": np.ascontiguousarray(inputs["L1_attn_Wv"].T.astype(np.float32)),
        "w_WoT": np.ascontiguousarray(inputs["L1_attn_Wo"].T.astype(np.float32)),
        "w_bqc": np.ascontiguousarray(
            inputs["L1_attn_bq"].reshape(8, 128).T.astype(np.float32)),
        "w_bkc": np.ascontiguousarray(
            inputs["L1_attn_bk"].reshape(8, 128).T.astype(np.float32)),
        "w_bvr": np.ascontiguousarray(
            inputs["L1_attn_bv"].reshape(1, 1024).astype(np.float32)),
        "w_bor": np.ascontiguousarray(
            inputs["L1_attn_bo"].reshape(1, 1024).astype(np.float32)),
        "c_att": _att_masks(),
    }


def _att_masks():
    m = np.empty((128, 5 * 512), np.float32)
    pk = np.arange(128)[:, None]
    fq = np.arange(512)[None, :]
    for r in range(4):
        rel = r * 128
        m[:, r * 512:(r + 1) * 512] = np.where(pk + rel > fq, -10000.0, 5.0)
    m[:, 4 * 512:] = 5.0
    return m



from contextlib import ExitStack

_PHASE_CACHE = {}


def build_phase(phase):
    if phase in _PHASE_CACHE:
        return _PHASE_CACHE[phase]
    nc = bass.Bass("TRN2", target_bir_lowering=False, debug=False)
    prog = Prog(nc)
    stack = ExitStack()
    with stack:
        env = Env(prog, stack)
        prog.alloc_sems(stack)
        x_in = env.dram("x_in", (D, NTOK), F32, "ExternalInput")
        x_out = env.dram("x_out", (D, NTOK), F32, "ExternalOutput")
        cst = load_consts(prog, env)
        if phase == "moe":
            xs_ln = env.dram("xs_ln", (D, NTOK), BF16, "Internal")
            xs_acc = env.dram("xs_acc", (D, NTOK), F32, "Internal")
            w = stage_moe_weights(prog, env)
            bufs = make_moe_bufs(env)
            for ti in range(NTILES):
                emit_moe_router_tile(prog, cst, w, bufs, x_in, xs_ln, ti)
            emit_moe_experts(prog, cst, w, bufs, x_in, x_out, xs_ln, xs_acc,
                             wbf_scr=None, scratch_mode="none")
        elif phase == "dn":
            w = stage_dn_weights(prog, env)
            bufs = make_dn_bufs(env)
            emit_dn_sublayer(prog, cst, w, bufs, x_in, x_out)
        elif phase == "att":
            w = stage_att_weights(prog, env)
            bufs = make_att_bufs(env)
            emit_att_sublayer(prog, cst, w, bufs, x_in, x_out)
        elif phase == "l1f":
            w = stage_l1ffn_weights(prog, env)
            bufs = make_l1ffn_bufs(env)
            emit_l1ffn_sublayer(prog, cst, w, bufs, x_in, x_out)
        block = stack.enter_context(nc.Block())
        prog.finalize_and_emit(block)
    _PHASE_CACHE[phase] = nc
    return nc


_FUSED_CACHE = []


def build_fused():
    if _FUSED_CACHE:
        return _FUSED_CACHE[0]
    nc = bass.Bass("TRN2", target_bir_lowering=False, debug=False)
    prog = Prog(nc)
    stack = ExitStack()
    with stack:
        env = Env(prog, stack)
        prog.alloc_sems(stack)
        x0 = env.dram("x_in", (D, NTOK), F32, "ExternalInput")
        xs_mid = [env.dram(f"xmid{i}", (D, NTOK), F32, "Internal")
                  for i in range(5)]
        x6 = env.dram("x_out", (D, NTOK), F32, "ExternalOutput")
        chain = [x0] + xs_mid + [x6]
        xs_ln = env.dram("xs_ln", (D, NTOK), BF16, "Internal")
        xs_acc = env.dram("xs_acc", (D, NTOK), F32, "Internal")
        wbf = env.dram("wbf", (8, 128, 65536), BF16, "Internal")
        cst = load_consts(prog, env)
        dr_moe = declare_moe_drams(env)
        dr_dn = declare_dn_drams(env)
        dr_att = declare_att_drams(env)
        dr_l1f = declare_l1ffn_drams(env)

        def ph_moe(xi, xo, mode, tag):
            with ExitStack() as ps_:
                penv = Env(prog, ps_)
                w = stage_moe_weights(prog, penv, dr_moe, tag=tag)
                bufs = make_moe_bufs(penv, tag=tag)

                for ti in range(NTILES):
                    emit_moe_router_tile(prog, cst, w, bufs, xi, xs_ln, ti,
                                         label="mr" + tag)
                emit_moe_experts(prog, cst, w, bufs, xi, xo, xs_ln, xs_acc,
                                 wbf_scr=wbf, scratch_mode=mode,
                                 label="me" + tag)

        def ph_dn(xi, xo):
            with ExitStack() as ps_:
                penv = Env(prog, ps_)
                w = stage_dn_weights(prog, penv, dr_dn)
                bufs = make_dn_bufs(penv)
                emit_dn_sublayer(prog, cst, w, bufs, xi, xo)

        def ph_att(xi, xo):
            with ExitStack() as ps_:
                penv = Env(prog, ps_)
                w = stage_att_weights(prog, penv, dr_att)
                bufs = make_att_bufs(penv)
                emit_att_sublayer(prog, cst, w, bufs, xi, xo)

        def ph_l1f(xi, xo, tag):
            with ExitStack() as ps_:
                penv = Env(prog, ps_)
                w = stage_l1ffn_weights(prog, penv, dr_l1f, tag=tag)
                bufs = make_l1ffn_bufs(penv, tag=tag)
                emit_l1ffn_sublayer(prog, cst, w, bufs, xi, xo,
                                    label="l1f" + tag)

        with ExitStack() as ps0:
            penv0 = Env(prog, ps0)
            w0 = dict(dr_moe)
            emit_moe_cast_prepass(prog, w0, wbf)
        ph_moe(chain[0], chain[1], "read", "A")
        prog.barrier("b1")
        ph_dn(chain[1], chain[2])
        prog.barrier("b2")
        ph_moe(chain[2], chain[3], "read", "B")
        prog.barrier("b3")
        ph_l1f(chain[3], chain[4], "A")
        prog.barrier("b4")
        ph_att(chain[4], chain[5])
        prog.barrier("b5")
        ph_l1f(chain[5], chain[6], "B")
        block = stack.enter_context(nc.Block())
        prog.finalize_and_emit(block)
    _FUSED_CACHE.append(nc)
    return nc


def kernel(**inputs):
    consts = host_consts()
    base = dict(consts)
    base.update(host_moe_weights(inputs))
    base.update(host_dn_weights(inputs))
    base.update(host_att_weights(inputs))
    base.update(host_l1ffn_weights(inputs))
    xs = [host_x_shard(np.asarray(inputs["L0_attn_hidden_states"]), c)
          for c in range(N_CORES)]
    nc = build_fused()
    in_maps = [dict(base, x_in=xs[c]) for c in range(N_CORES)]
    res = run_bass_kernel_spmd(nc, in_maps, core_ids=list(range(N_CORES)))
    xs = [np.asarray(res.results[c]["x_out"]) for c in range(N_CORES)]
    return host_x_unshard(xs)


def kernel_unfused(**inputs):
    consts = host_consts()
    wsets = {
        "moe": host_moe_weights(inputs),
        "dn": host_dn_weights(inputs),
        "att": host_att_weights(inputs),
        "l1f": host_l1ffn_weights(inputs),
    }
    xs = [host_x_shard(np.asarray(inputs["L0_attn_hidden_states"]), c)
          for c in range(N_CORES)]
    for phase in ("moe", "dn", "moe", "l1f", "att", "l1f"):
        nc = build_phase(phase)
        base = dict(consts)
        base.update(wsets[phase])
        in_maps = [dict(base, x_in=xs[c]) for c in range(N_CORES)]
        res = run_bass_kernel_spmd(nc, in_maps, core_ids=list(range(N_CORES)))
        xs = [np.asarray(res.results[c]["x_out"]) for c in range(N_CORES)]
    return host_x_unshard(xs)


# version 11 (final)
# speedup vs baseline: 1.0723x; 1.0723x over previous
import numpy as np
import ml_dtypes
import concourse.bass as bass
import concourse.mybir as mybir
from concourse.bass_utils import run_bass_kernel_spmd

F32 = mybir.dt.float32
BF16 = mybir.dt.bfloat16
AF = mybir.ActivationFunctionType
ALU = mybir.AluOpType
AX = mybir.AxisListType

N_CORES = 8
B_FULL, T_SEQ, D = 16, 2048, 1024
B_CORE = B_FULL // N_CORES
NTOK = B_CORE * T_SEQ
NT = 512
NTILES = NTOK // NT

ENGINES = ("tensor", "vector", "scalar", "gpsimd", "sync")
QUEUES = {"qs": "sync", "qa": "scalar", "qg": "gpsimd"}


class _Op:
    __slots__ = ("emit", "waits", "inc", "label")

    def __init__(self, emit, waits, inc, label):
        self.emit = emit
        self.waits = waits
        self.inc = inc
        self.label = label


class Buf:

    __slots__ = ("name", "t", "size", "writes", "reads", "ap", "is_psum")

    def __init__(self, name, tensor, size, is_psum=False):
        self.name = name
        self.t = tensor
        self.size = size
        self.is_psum = is_psum
        self.writes = []
        self.reads = []


def _overlap(lo, hi, rec):
    return not (hi <= rec[0] or lo >= rec[1])


class Prog:
    def __init__(self, nc):
        self.nc = nc
        self.ops = {e: [] for e in ENGINES}
        self.sem_counts = {}
        self.sem_handles = {}
        for e in ENGINES:
            self.sem_counts["m_" + e] = 0
        for q in QUEUES:
            self.sem_counts[q] = 0
        self.known = {e: {} for e in ENGINES}

    def alloc_sems(self, stack):
        for name in self.sem_counts:
            self.sem_handles[name] = stack.enter_context(
                self.nc.semaphore(name))

    def _gather(self, eng, reads, writes):
        waits = {}

        def need(tkt):
            if tkt is None:
                return
            sem, val = tkt
            if eng == "tensor" and sem == "m_tensor":
                return
            if waits.get(sem, 0) < val:
                waits[sem] = val

        myeng_sem = "m_" + eng
        for (buf, lo, hi) in reads:
            for rec in buf.writes:
                if _overlap(lo, hi, rec):
                    need(rec[2])
            if buf.is_psum:
                for rec in buf.reads:
                    if rec[2] is not None and rec[2][0] != myeng_sem and                             _overlap(lo, hi, rec):
                        need(rec[2])
        for (buf, lo, hi) in writes:
            for rec in buf.writes:
                if _overlap(lo, hi, rec):
                    need(rec[2])
            for rec in buf.reads:
                if _overlap(lo, hi, rec):
                    need(rec[2])
        kn = self.known[eng]
        return [(s, v) for s, v in sorted(waits.items()) if kn.get(s, 0) < v]

    def _commit(self, eng, reads, writes, tkt, waits):
        kn = self.known[eng]
        for sem, val in waits:
            if kn.get(sem, 0) < val:
                kn[sem] = val
        for (buf, lo, hi) in writes:
            buf.writes = [r for r in buf.writes
                          if not (lo <= r[0] and r[1] <= hi)]
            buf.writes.append([lo, hi, tkt])
            buf.reads = [r for r in buf.reads
                         if not (lo <= r[0] and r[1] <= hi)]
        for (buf, lo, hi) in reads:
            rl = buf.reads
            if rl and rl[-1][0] == lo and rl[-1][1] == hi:
                rl[-1][2] = tkt
            else:
                rl.append([lo, hi, tkt])

    def op(self, eng, emit, reads=(), writes=(), label=""):
        waits = self._gather(eng, reads, writes)
        sem = "m_" + eng
        self.sem_counts[sem] += 1
        tkt = (sem, self.sem_counts[sem])
        self.ops[eng].append(_Op(emit, waits, (sem, 1), label))
        self._commit(eng, reads, writes, tkt, waits)
        return tkt

    def mm_group(self, psum_buf, parts, label=""):
        eng = "tensor"
        allreads = []
        for (_, rds) in parts:
            allreads.extend(rds)
        wr = [(psum_buf, 0, psum_buf.size)]
        waits = self._gather(eng, allreads, wr)
        self.sem_counts["m_tensor"] += 1
        tkt = ("m_tensor", self.sem_counts["m_tensor"])
        n = len(parts)
        for i, (emit, _) in enumerate(parts):
            self.ops[eng].append(_Op(
                emit, waits if i == 0 else [],
                ("m_tensor", 1) if i == n - 1 else None, label))
        self._commit(eng, allreads, wr, tkt, waits)
        return tkt

    def pe_op(self, emit, reads=(), writes=(), label=""):
        waits = self._gather("tensor", reads, writes)
        self.sem_counts["m_tensor"] += 1
        tkt = ("m_tensor", self.sem_counts["m_tensor"])
        self.ops["tensor"].append(_Op(emit, waits, ("m_tensor", 1), label))
        self._commit("tensor", reads, writes, tkt, waits)
        return tkt

    def dma_batch(self, q, transfers, label=""):
        eng = QUEUES[q]
        prev = self.sem_counts[q]
        waits = {}
        if prev and self.known[eng].get(q, 0) < prev:
            waits[q] = prev
        allreads, allwrites = [], []
        norm = []
        for t in transfers:
            out_ap, in_ap, rds, wrs = t[0], t[1], t[2], t[3]
            kw = t[4] if len(t) > 4 else {}
            norm.append((out_ap, in_ap, kw))
            allreads.extend(rds)
            allwrites.extend(wrs)
        for sem, val in self._gather(eng, allreads, allwrites):
            if waits.get(sem, 0) < val:
                waits[sem] = val
        waits = sorted(waits.items())
        total = prev + 16 * len(norm)
        self.sem_counts[q] = total
        tkt = (q, total)
        for i, (out_ap, in_ap, kw) in enumerate(norm):
            def emit(e, out_ap=out_ap, in_ap=in_ap, kw=kw):
                return e.dma_start(out=out_ap, in_=in_ap, **kw)
            self.ops[eng].append(
                _Op(emit, waits if i == 0 else [], (q, 16), label))
        self._commit(eng, allreads, allwrites, tkt, waits)
        kn = self.known[eng]
        if kn.get(q, 0) < prev:
            kn[q] = prev
        return tkt

    def barrier(self, label="bar"):
        targets = {}
        for e in ENGINES:
            sem = "m_" + e
            cur = self.sem_counts[sem]
            kn = self.known[e]
            waits = []
            if cur and kn.get(sem, 0) < cur:
                waits.append((sem, cur))
                kn[sem] = cur
            self.sem_counts[sem] += 1
            targets[sem] = self.sem_counts[sem]
            self.ops[e].append(
                _Op(lambda eng: eng.nop(), waits, (sem, 1), label + "-i"))
        for q in QUEUES:
            targets[q] = self.sem_counts[q]
        for e in ENGINES:
            waits = []
            kn = self.known[e]
            for sem, val in sorted(targets.items()):
                if val and kn.get(sem, 0) < val:
                    waits.append((sem, val))
                    kn[sem] = val
            self.ops[e].append(
                _Op(lambda eng: eng.nop(), waits, None, label + "-w"))

    def finalize_and_emit(self, block):
        waits = []
        kn = self.known["sync"]
        for sem, tot in sorted(self.sem_counts.items()):
            if sem == "m_sync":
                continue
            if tot and kn.get(sem, 0) < tot:
                waits.append((sem, tot))
        self.ops["sync"].append(
            _Op(lambda eng: eng.nop(), waits, None, "final-drain"))
        handles = self.sem_handles

        def run(eng_name):
            def go(engine):
                for op in self.ops[eng_name]:
                    for sem, val in op.waits:
                        engine.wait_ge(handles[sem], val)
                    ins = op.emit(engine)
                    if op.inc is not None:
                        ins.then_inc(handles[op.inc[0]], op.inc[1])
            return go

        block.tensor(run("tensor"))
        block.vector(run("vector"))
        block.scalar(run("scalar"))
        block.gpsimd(run("gpsimd"))
        block.sync(run("sync"))

    def stats(self):
        return {e: len(self.ops[e]) for e in ENGINES}





LN_EPS = 1e-5


class Env:

    def __init__(self, prog, stack):
        self.prog = prog
        self.stack = stack
        self.bufs = {}

    def sbuf(self, name, shape, dtype):
        t = self.stack.enter_context(self.prog.nc.sbuf_tensor(name, list(shape), dtype))
        b = Buf(name, t, shape[1])
        self.bufs[name] = b
        return b

    def psum(self, name, shape, dtype):
        t = self.stack.enter_context(self.prog.nc.psum_tensor(name, list(shape), dtype))
        b = Buf(name, t, shape[1], is_psum=True)
        self.bufs[name] = b
        return b

    def dram(self, name, shape, dtype, kind):
        t = self.prog.nc.dram_tensor(name, list(shape), dtype, kind=kind)
        b = Buf(name, t, int(np.prod(shape)))
        b.ap = t.ap()
        self.bufs[name] = b
        return b


def host_consts():
    c = {}
    c["c_ident"] = np.eye(128, dtype=np.float32)
    c["c_lncol"] = np.stack(
        [np.full(128, -1.0 / D, np.float32), np.full(128, 1.0 / D, np.float32)],
        axis=1)
    c["c_ones_col"] = np.ones((128, 1), np.float32)
    c["c_ones_row"] = np.ones((1, 512), np.float32)
    misc = np.zeros((128, 4), np.float32)
    misc[:, 0] = 1e-5
    misc[:, 1] = 1e-6
    misc[:, 2] = 1e-8
    c["c_misc"] = misc
    return c


def load_consts(prog, env):
    nc = prog.nc
    d_id = env.dram("c_ident", (128, 128), F32, "ExternalInput")
    d_ln = env.dram("c_lncol", (128, 2), F32, "ExternalInput")
    d_oc = env.dram("c_ones_col", (128, 1), F32, "ExternalInput")
    d_or = env.dram("c_ones_row", (1, 512), F32, "ExternalInput")
    d_mi = env.dram("c_misc", (128, 4), F32, "ExternalInput")
    ident = env.sbuf("identb", (128, 128), BF16)
    identf = env.sbuf("identf", (128, 128), F32)
    lncol = env.sbuf("lncol", (128, 2), BF16)
    ones_col = env.sbuf("ones_col", (128, 1), BF16)
    ones_row = env.sbuf("ones_row", (1, 512), BF16)
    misc = env.sbuf("misc", (128, 4), F32)
    prog.dma_batch("qg", [
        (ident.t[:], d_id.ap, [], [(ident, 0, 128)]),
        (identf.t[:], d_id.ap, [], [(identf, 0, 128)]),
        (lncol.t[:], d_ln.ap, [], [(lncol, 0, 2)]),
        (ones_col.t[:], d_oc.ap, [], [(ones_col, 0, 1)]),
        (ones_row.t[:], d_or.ap, [], [(ones_row, 0, 512)]),
        (misc.t[:], d_mi.ap, [], [(misc, 0, 4)]),
    ], label="consts")
    return dict(ident=ident, identf=identf, lncol=lncol, ones_col=ones_col,
                ones_row=ones_row, misc=misc)


def x_tile_ap(xdram_buf, t0, nt=NT, f0=0, nf=D):
    ap = xdram_buf.t.ap()[f0:f0 + nf, t0:t0 + nt]
    return ap.rearrange("(c p) t -> p c t", p=128)


def sb3(buf, nchunks, nt=NT, col0=0):
    return buf.t[:, col0:col0 + nchunks * nt].rearrange(
        "p (c t) -> p c t", t=nt)


def emit_ln_tile(prog, cst, xdram, t0, xbuf, xcol, xcol2, xln, lnscr, ps, label="ln"):
    nc = prog.nc
    prog.dma_batch("qs", [
        (sb3(xbuf, 8), x_tile_ap(xdram, t0), [(xdram, t0, t0 + NT)],
         [(xbuf, 0, 8 * NT)]),
    ], label=label + "-ld")
    prog.op("vector",
            lambda e: e.tensor_copy(xln.t[:, 0:8 * NT], xbuf.t[:, 0:8 * NT]),
            reads=[(xbuf, 0, 8 * NT)], writes=[(xln, 0, 8 * NT)],
            label=label + "-cast")
    prog.mm_group(ps["stat1"], [
        (lambda e, fc=fc: e.matmul(ps["stat1"].t[0:1, 0:NT],
                                   cst["lncol"].t[:, 0:1],
                                   xln.t[:, fc * NT:(fc + 1) * NT],
                                   start=(fc == 0), stop=(fc == 7)),
         [(xln, fc * NT, (fc + 1) * NT), (cst["lncol"], 0, 1)])
        for fc in range(8)], label=label + "-mmu")
    prog.op("vector",
            lambda e: e.tensor_mul(lnscr.t[:, 0:8 * NT],
                                   xln.t[:, 0:8 * NT], xln.t[:, 0:8 * NT]),
            reads=[(xln, 0, 8 * NT)], writes=[(lnscr, 0, 8 * NT)],
            label=label + "-sq")
    prog.mm_group(ps["stat2"], [
        (lambda e, fc=fc: e.matmul(ps["stat2"].t[0:1, 0:NT],
                                   cst["lncol"].t[:, 1:2],
                                   lnscr.t[:, fc * NT:(fc + 1) * NT],
                                   start=(fc == 0), stop=(fc == 7)),
         [(lnscr, fc * NT, (fc + 1) * NT), (cst["lncol"], 0, 2)])
        for fc in range(8)], label=label + "-mmsq")
    prog.op("vector", lambda e: e.tensor_copy(xcol.t[0:1, 0:NT],
                                              ps["stat1"].t[0:1, :]),
            reads=[(ps["stat1"], 0, NT)], writes=[(xcol, 0, NT)],
            label=label + "-negmu")
    prog.op("vector",
            lambda e: e.tensor_mul(xcol.t[0:1, NT:2 * NT],
                                   xcol.t[0:1, 0:NT], xcol.t[0:1, 0:NT]),
            reads=[(xcol, 0, NT)], writes=[(xcol, NT, 2 * NT)],
            label=label + "-mu2")
    prog.op("vector",
            lambda e: e.tensor_tensor(xcol.t[0:1, NT:2 * NT],
                                      ps["stat2"].t[0:1, :],
                                      xcol.t[0:1, NT:2 * NT],
                                      op=ALU.subtract),
            reads=[(ps["stat2"], 0, NT), (xcol, NT, 2 * NT)],
            writes=[(xcol, NT, 2 * NT)], label=label + "-var")
    prog.op("scalar",
            lambda e: e.activation(out=xcol.t[0:1, NT:2 * NT],
                                   in_=xcol.t[0:1, NT:2 * NT],
                                   func=AF.Ln, bias=cst["misc"].t[0:1, 0:1]),
            reads=[(xcol, NT, 2 * NT), (cst["misc"], 0, 1)],
            writes=[(xcol, NT, 2 * NT)],
            label=label + "-lnv")
    prog.op("scalar",
            lambda e: e.activation(out=xcol.t[0:1, NT:2 * NT],
                                   in_=xcol.t[0:1, NT:2 * NT],
                                   func=AF.Exp, scale=-0.5),
            reads=[(xcol, NT, 2 * NT)], writes=[(xcol, NT, 2 * NT)],
            label=label + "-rsq")
    prog.op("vector",
            lambda e: e.tensor_mul(xcol.t[0:1, 0:NT], xcol.t[0:1, 0:NT],
                                   xcol.t[0:1, NT:2 * NT]),
            reads=[(xcol, 0, 2 * NT)], writes=[(xcol, 0, NT)],
            label=label + "-nmr")
    prog.op("vector", lambda e: e.tensor_copy(xcol2.t[0:1, 0:NT],
                                              xcol.t[0:1, NT:2 * NT]),
            reads=[(xcol, NT, 2 * NT)], writes=[(xcol2, 0, NT)],
            label=label + "-rb16")
    prog.op("vector", lambda e: e.tensor_copy(xcol2.t[0:1, NT:2 * NT],
                                              xcol.t[0:1, 0:NT]),
            reads=[(xcol, 0, NT)], writes=[(xcol2, NT, 2 * NT)],
            label=label + "-nm16")
    prog.mm_group(ps["bc1"], [
        (lambda e: e.matmul(ps["bc1"].t[:], cst["ones_row"].t[0:1, 0:128],
                            xcol2.t[0:1, 0:NT], start=True, stop=True),
         [(xcol2, 0, NT), (cst["ones_row"], 0, 128)])], label=label + "-bcr")
    prog.mm_group(ps["bc2"], [
        (lambda e: e.matmul(ps["bc2"].t[:], cst["ones_row"].t[0:1, 0:128],
                            xcol2.t[0:1, NT:2 * NT], start=True, stop=True),
         [(xcol2, NT, 2 * NT), (cst["ones_row"], 0, 128)])], label=label + "-bcm")
    for fc in range(8):
        prog.op("vector",
                lambda e, fc=fc: e.tensor_mul(
                    lnscr.t[:, fc * NT:(fc + 1) * NT],
                    xbuf.t[:, fc * NT:(fc + 1) * NT], ps["bc1"].t[:]),
                reads=[(xbuf, fc * NT, (fc + 1) * NT), (ps["bc1"], 0, NT)],
                writes=[(lnscr, fc * NT, (fc + 1) * NT)],
                label=label + "-ap1")
        prog.op("vector",
                lambda e, fc=fc: e.tensor_tensor(
                    xln.t[:, fc * NT:(fc + 1) * NT],
                    lnscr.t[:, fc * NT:(fc + 1) * NT], ps["bc2"].t[:],
                    op=ALU.add),
                reads=[(lnscr, fc * NT, (fc + 1) * NT), (ps["bc2"], 0, NT)],
                writes=[(xln, fc * NT, (fc + 1) * NT)],
                label=label + "-ap2")



def declare_l1ffn_drams(env):
    return dict(
        d_u1=env.dram("w_U1", (1024, 256), F32, "ExternalInput"),
        d_v1=env.dram("w_V1", (256, 4096), F32, "ExternalInput"),
        d_u2=env.dram("w_U2", (4096, 256), F32, "ExternalInput"),
        d_v2=env.dram("w_V2", (256, 1024), F32, "ExternalInput"),
        d_b1=env.dram("w_l1b1", (128, 32), F32, "ExternalInput"),
        d_b2=env.dram("w_l1b2", (1, 1024), F32, "ExternalInput"),
    )


def stage_l1ffn_weights(prog, env, dr=None, tag=""):
    if dr is None:
        dr = declare_l1ffn_drams(env)
    d_u1, d_v1, d_u2, d_v2 = dr["d_u1"], dr["d_v1"], dr["d_u2"], dr["d_v2"]
    d_b1, d_b2 = dr["d_b1"], dr["d_b2"]

    u1 = env.sbuf("u1" + tag, (128, 8 * 256), BF16)
    v1 = env.sbuf("v1" + tag, (128, 2 * 4096), BF16)
    u2 = env.sbuf("u2" + tag, (128, 32 * 256), BF16)
    v2 = env.sbuf("v2" + tag, (128, 2 * 1024), BF16)
    b1c = env.sbuf("l1b1c" + tag, (128, 32), F32)
    b2r = env.sbuf("l1b2r" + tag, (1, 1024), BF16)

    prog.dma_batch("qg", [
        (sb3(u1, 8, 256), d_u1.ap.rearrange("(c p) r -> p c r", p=128),
         [], [(u1, 0, 8 * 256)]),
        (sb3(v1, 2, 4096), d_v1.ap.rearrange("(c p) f -> p c f", p=128),
         [], [(v1, 0, 2 * 4096)]),
        (sb3(u2, 32, 256), d_u2.ap.rearrange("(c p) r -> p c r", p=128),
         [], [(u2, 0, 32 * 256)]),
        (sb3(v2, 2, 1024), d_v2.ap.rearrange("(c p) d -> p c d", p=128),
         [], [(v2, 0, 2 * 1024)]),
        (b1c.t[:], d_b1.ap, [], [(b1c, 0, 32)]),
        (b2r.t[:], d_b2.ap, [], [(b2r, 0, 1024)]),
    ], label="l1ffn-w")
    return dict(u1=u1, v1=v1, u2=u2, v2=v2, b1c=b1c, b2r=b2r)


def emit_l1ffn_sublayer(prog, cst, w, bufs, x_in, x_out, label="l1f"):
    ps = bufs["ps"]

    def B(name, par):
        return bufs[name + ("B" if par else "")]

    def stage_P(lb, par, xln):
        rot = (ps["pa"], ps["pb"]) if par == 0 else (ps["pc"], ps["pd"])
        pbuf = B("pbuf", par)
        for rc in range(2):
            bk = rot[rc % 2]
            prog.mm_group(bk, [
                (lambda e, fc=fc, rc=rc, bk=bk, xln=xln: e.matmul(
                    bk.t[:], w["u1"].t[:, fc * 256 + rc * 128:fc * 256 + rc * 128 + 128],
                    xln.t[:, fc * NT:(fc + 1) * NT],
                    start=(fc == 0), stop=(fc == 7)),
                 [(w["u1"], fc * 256, (fc + 1) * 256), (xln, fc * NT, (fc + 1) * NT)])
                for fc in range(8)], label=lb + "-P")
            prog.op("vector",
                    lambda e, rc=rc, bk=bk, pbuf=pbuf: e.tensor_copy(
                        pbuf.t[:, rc * NT:(rc + 1) * NT], bk.t[:]),
                    reads=[(bk, 0, NT)], writes=[(pbuf, rc * NT, (rc + 1) * NT)],
                    label=lb + "-Pev")

    def stage_H(lb, par):
        rot = (ps["pa"], ps["pb"]) if par == 0 else (ps["pc"], ps["pd"])
        pbuf, hbuf = B("pbuf", par), B("hbuf", par)
        for hc in range(32):
            bk = rot[hc % 2]
            prog.mm_group(bk, [
                (lambda e, rc=rc, hc=hc, bk=bk, pbuf=pbuf: e.matmul(
                    bk.t[:], w["v1"].t[:, rc * 4096 + hc * 128:rc * 4096 + hc * 128 + 128],
                    pbuf.t[:, rc * NT:(rc + 1) * NT],
                    start=(rc == 0), stop=(rc == 1)),
                 [(w["v1"], rc * 4096, (rc + 1) * 4096), (pbuf, rc * NT, (rc + 1) * NT)])
                for rc in range(2)], label=lb + "-H")
            prog.op("scalar",
                    lambda e, hc=hc, bk=bk, hbuf=hbuf: e.activation(
                        out=hbuf.t[:, hc * NT:(hc + 1) * NT], in_=bk.t[:],
                        func=AF.Gelu_apprx_tanh, bias=w["b1c"].t[:, hc:hc + 1]),
                    reads=[(bk, 0, NT), (w["b1c"], hc, hc + 1)],
                    writes=[(hbuf, hc * NT, (hc + 1) * NT)], label=lb + "-g")

    def stage_Q(lb, par):
        rot = (ps["pa"], ps["pb"]) if par == 0 else (ps["pc"], ps["pd"])
        hbuf, qbuf = B("hbuf", par), B("qbuf", par)
        for rc in range(2):
            bk = rot[rc % 2]
            prog.mm_group(bk, [
                (lambda e, hc=hc, rc=rc, bk=bk, hbuf=hbuf: e.matmul(
                    bk.t[:], w["u2"].t[:, hc * 256 + rc * 128:hc * 256 + rc * 128 + 128],
                    hbuf.t[:, hc * NT:(hc + 1) * NT],
                    start=(hc == 0), stop=(hc == 31)),
                 [(w["u2"], hc * 256, (hc + 1) * 256), (hbuf, hc * NT, (hc + 1) * NT)])
                for hc in range(32)], label=lb + "-Q")
            prog.op("vector",
                    lambda e, rc=rc, bk=bk, qbuf=qbuf: e.tensor_copy(
                        qbuf.t[:, rc * NT:(rc + 1) * NT], bk.t[:]),
                    reads=[(bk, 0, NT)], writes=[(qbuf, rc * NT, (rc + 1) * NT)],
                    label=lb + "-Qev")

    def stage_O(lb, par, t0, xbuf):
        rot = (ps["pa"], ps["pb"]) if par == 0 else (ps["pc"], ps["pd"])
        qbuf = B("qbuf", par)
        for dc in range(8):
            bk = rot[dc % 2]
            parts = [
                (lambda e, rc=rc, dc=dc, bk=bk, qbuf=qbuf: e.matmul(
                    bk.t[:], w["v2"].t[:, rc * 1024 + dc * 128:rc * 1024 + dc * 128 + 128],
                    qbuf.t[:, rc * NT:(rc + 1) * NT],
                    start=(rc == 0), stop=False),
                 [(w["v2"], rc * 1024, (rc + 1) * 1024), (qbuf, rc * NT, (rc + 1) * NT)])
                for rc in range(2)]
            parts.append(
                (lambda e, dc=dc, bk=bk: e.matmul(
                    bk.t[:], w["b2r"].t[0:1, dc * 128:(dc + 1) * 128],
                    cst["ones_row"].t[0:1, 0:NT], start=False, stop=True),
                 [(w["b2r"], dc * 128, (dc + 1) * 128), (cst["ones_row"], 0, NT)]))
            prog.mm_group(bk, parts, label=lb + "-O")
            prog.op("vector",
                    lambda e, dc=dc, bk=bk, xbuf=xbuf: e.scalar_tensor_tensor(
                        xbuf.t[:, dc * NT:(dc + 1) * NT], bk.t[:], 0.5,
                        xbuf.t[:, dc * NT:(dc + 1) * NT],
                        op0=ALU.mult, op1=ALU.add),
                    reads=[(bk, 0, NT), (xbuf, dc * NT, (dc + 1) * NT)],
                    writes=[(xbuf, dc * NT, (dc + 1) * NT)], label=lb + "-res")
        prog.dma_batch("qa", [
            (x_tile_ap(x_out, t0), sb3(xbuf, 8), [(xbuf, 0, 8 * NT)],
             [(x_out, t0, t0 + NT)]),
        ], label=lb + "-st")

    lnps = dict(stat1=ps["stat1"], stat2=ps["stat2"], bc1=ps["bc1"],
                bc2=ps["bc2"])
    for tp_ in range(NTILES // 2):
        tiles = (2 * tp_, 2 * tp_ + 1)
        lbs = [f"{label}{ti}" for ti in tiles]
        for j, ti in enumerate(tiles):
            emit_ln_tile(prog, cst, x_in, ti * NT, B("xbuf", j), bufs["xcol"],
                         bufs["xcol2"], B("xln", j), B("hbuf", j), lnps,
                         label=lbs[j])
        for j in range(2):
            stage_P(lbs[j], j, B("xln", j))
        for j in range(2):
            stage_H(lbs[j], j)
        for j in range(2):
            stage_Q(lbs[j], j)
        for j in range(2):
            stage_O(lbs[j], j, tiles[j] * NT, B("xbuf", j))


def make_l1ffn_bufs(env, tag=""):
    ps = dict(
        stat1=env.psum("ps_stat1" + tag, (1, NT), F32),
        stat2=env.psum("ps_stat2" + tag, (1, NT), F32),
        bc1=env.psum("ps_bc1" + tag, (128, NT), F32),
        bc2=env.psum("ps_bc2" + tag, (128, NT), F32),
        pa=env.psum("ps_pa" + tag, (128, NT), F32),
        pb=env.psum("ps_pb" + tag, (128, NT), F32),
        pc=env.psum("ps_pc" + tag, (128, NT), F32),
        pd=env.psum("ps_pd" + tag, (128, NT), F32),
    )
    return dict(
        xbuf=env.sbuf("xbuf" + tag, (128, 8 * NT), F32),
        xbufB=env.sbuf("xbufB" + tag, (128, 8 * NT), F32),
        xcol=env.sbuf("xcol" + tag, (1, 2 * NT), F32),
        xcol2=env.sbuf("xcol2" + tag, (1, 2 * NT), BF16),
        xln=env.sbuf("xln" + tag, (128, 8 * NT), BF16),
        xlnB=env.sbuf("xlnB" + tag, (128, 8 * NT), BF16),
        pbuf=env.sbuf("pbuf" + tag, (128, 2 * NT), BF16),
        pbufB=env.sbuf("pbufB" + tag, (128, 2 * NT), BF16),
        qbuf=env.sbuf("qbuf" + tag, (128, 2 * NT), BF16),
        qbufB=env.sbuf("qbufB" + tag, (128, 2 * NT), BF16),
        hbuf=env.sbuf("hbuf" + tag, (128, 32 * NT), BF16),
        hbufB=env.sbuf("hbufB" + tag, (128, 32 * NT), BF16),
        ps=ps,
    )



def host_l1ffn_weights(inputs):
    return {
        "w_U1": np.ascontiguousarray(inputs["L1_ffn_U1"]),
        "w_V1": np.ascontiguousarray(inputs["L1_ffn_V1"]),
        "w_U2": np.ascontiguousarray(inputs["L1_ffn_U2"]),
        "w_V2": np.ascontiguousarray(inputs["L1_ffn_V2"]),
        "w_l1b1": np.ascontiguousarray(
            inputs["L1_ffn_b1"].reshape(32, 128).T),
        "w_l1b2": np.ascontiguousarray(
            inputs["L1_ffn_b2"].reshape(1, 1024)),
    }


def host_x_shard(x_full, core):
    sh = x_full[core * B_CORE:(core + 1) * B_CORE]
    return np.ascontiguousarray(
        sh.transpose(2, 0, 1).reshape(D, NTOK).astype(np.float32))


def host_x_unshard(res_list):
    out = np.empty((B_FULL, T_SEQ, D), np.float32)
    for core, r in enumerate(res_list):
        xm = r.reshape(D, B_CORE, T_SEQ).transpose(1, 2, 0)
        out[core * B_CORE:(core + 1) * B_CORE] = xm
    return out



def declare_moe_drams(env):
    return dict(
        w1=env.dram("w_moe_w1", (8, 1024, 4096), F32, "ExternalInput"),
        w2=env.dram("w_moe_w2", (8, 4096, 1024), F32, "ExternalInput"),
        d_wg=env.dram("w_moe_wg", (128, 64), F32, "ExternalInput"),
        d_bg=env.dram("w_moe_bg", (1, 8), F32, "ExternalInput"),
        d_b1=env.dram("w_moe_b1", (128, 256), F32, "ExternalInput"),
        d_b2=env.dram("w_moe_b2", (128, 64), F32, "ExternalInput"),
    )


def stage_moe_weights(prog, env, dr=None, tag=""):
    if dr is None:
        dr = declare_moe_drams(env)
    d = dict(dr)
    wg = env.sbuf("wgsb" + tag, (128, 64), BF16)
    bg = env.sbuf("bgrow" + tag, (1, 8), BF16)
    b1 = env.sbuf("b1c" + tag, (128, 256), F32)
    b2 = env.sbuf("b2c" + tag, (128, 64), F32)
    prog.dma_batch("qg", [
        (wg.t[:], dr["d_wg"].ap, [], [(wg, 0, 64)]),
        (bg.t[:], dr["d_bg"].ap, [], [(bg, 0, 8)]),
        (b1.t[:], dr["d_b1"].ap, [], [(b1, 0, 256)]),
        (b2.t[:], dr["d_b2"].ap, [], [(b2, 0, 64)]),
    ], label="moe-smallw")
    d.update(wg=wg, bg=bg, b1=b1, b2=b2)
    return d


class Buf2View:

    def __init__(self, buf, off):
        self.buf = buf
        self.off = off
        self.t = _OffT(buf.t, off)
        self.is_psum = buf.is_psum

    @property
    def writes(self):
        return self.buf.writes

    @property
    def reads(self):
        return self.buf.reads


class _OffT:
    def __init__(self, t, off):
        self._t = t
        self._off = off

    def __getitem__(self, sl):
        p, f = sl
        return self._t[p, f.start + self._off:f.stop + self._off]


def make_moe_bufs(env, tag=""):
    ps = {f"p{i}": env.psum(f"{tag}mps_p{i}", (128, NT), F32) for i in range(8)}
    return dict(
        w1sb=env.sbuf("w1sb" + tag, (128, 8 * 4096), BF16),
        w2sb=env.sbuf("w2sb" + tag, (128, 32 * 1024), BF16),
        hbuf=env.sbuf("mhbuf" + tag, (128, 32 * NT), BF16),
        lnbuf=env.sbuf("mlnbuf" + tag, (128, 8 * NT), BF16),
        xresid=env.sbuf("mxresid" + tag, (128, 8 * NT), F32),
        wbsb=env.sbuf("wbsb" + tag, (128, NT), F32),
        tmpsb=env.sbuf("mtmp" + tag, (128, NT), F32),
        tmp2=env.sbuf("mtmp2" + tag, (128, NT), F32),
        tmp3=env.sbuf("mtmp3" + tag, (128, NT), F32),
        wstore=env.sbuf("wstore" + tag, (128, 32 * NTILES), F32),
        xcol=env.sbuf("mxcol" + tag, (1, 2 * NT), F32),
        xcol2=env.sbuf("mxcol2" + tag, (1, 2 * NT), BF16),
        rsc=env.sbuf("mrsc" + tag, (128, 64), F32),
        ps=ps,
    )


def emit_moe_cast_prepass(prog, w, wbf_scr, label="mc"):
    for e in range(8):
        prog.dma_batch("qg", [
            (wbf_scr.t.ap()[e, :, 0:32768].rearrange("p (c h) -> p c h", h=4096),
             w["w1"].t.ap()[e].rearrange("(c p) h -> p c h", p=128),
             [], [(wbf_scr, e * 65536, e * 65536 + 32768)]),
        ], label=label + f"-c1e{e}")
        prog.dma_batch("qg", [
            (wbf_scr.t.ap()[e, :, 32768:65536].rearrange("p (c d) -> p c d", d=1024),
             w["w2"].t.ap()[e].rearrange("(c p) d -> p c d", p=128),
             [], [(wbf_scr, e * 65536 + 32768, (e + 1) * 65536)]),
        ], label=label + f"-c2e{e}")


def emit_moe_router_tile(prog, cst, w, bufs, x_in, xs_ln, ti, label="mr"):
    ps = bufs["ps"]
    lnps = dict(stat1=ps["p0"], stat2=ps["p1"], bc1=ps["p2"], bc2=ps["p3"])
    lgbanks = (ps["p4"], ps["p5"])
    lnb = bufs["lnbuf"]
    rsc0 = 0
    rsc = bufs["rsc"]
    if True:
        t0 = ti * NT
        lb = f"{label}{ti}"
        emit_ln_tile(prog, cst, x_in, t0, bufs["xresid"], bufs["xcol"],
                     bufs["xcol2"], lnb, bufs["hbuf"], lnps, label=lb)
        prog.dma_batch("qa", [
            (x_tile_ap(xs_ln, t0), sb3(lnb, 8),
             [(lnb, 0, 8 * NT)], [(xs_ln, t0, t0 + NT)]),
        ], label=lb + "-lnst")
        for g in range(4):
            gt = g * 128
            bk = lgbanks[g % 2]
            parts = [
                (lambda e, fc=fc, gt=gt, bk=bk, lnb=lnb: e.matmul(
                    bk.t[0:128, 0:8],
                    lnb.t[:, fc * NT + gt:fc * NT + gt + 128],
                    w["wg"].t[:, fc * 8:(fc + 1) * 8],
                    start=(fc == 0), stop=False),
                 [(lnb, fc * NT + gt, fc * NT + gt + 128),
                  (w["wg"], fc * 8, (fc + 1) * 8)])
                for fc in range(8)]
            parts.append(
                (lambda e, bk=bk: e.matmul(
                    bk.t[0:128, 0:8], cst["ones_row"].t[0:1, 0:128],
                    w["bg"].t[0:1, 0:8], start=False, stop=True),
                 [(cst["ones_row"], 0, 128), (w["bg"], 0, 8)]))
            prog.mm_group(bk, parts, label=lb + f"-lg{g}")
            c0 = 0
            prog.op("vector",
                    lambda e, bk=bk: e.reduce_max(rsc.t[:, rsc0 + 0:rsc0 + 1], bk.t[0:128, 0:8],
                                                  axis=AX.X, negate=True),
                    reads=[(bk, 0, NT)], writes=[(rsc, rsc0 + 0, rsc0 + 1)], label=lb + "-mx")
            prog.op("scalar",
                    lambda e, bk=bk: e.activation(
                        out=rsc.t[:, rsc0 + 8:rsc0 + 16], in_=bk.t[0:128, 0:8], func=AF.Exp,
                        bias=rsc.t[:, rsc0 + 0:rsc0 + 1]),
                    reads=[(bk, 0, NT), (rsc, rsc0 + 0, rsc0 + 1)], writes=[(rsc, rsc0 + 8, rsc0 + 16)],
                    label=lb + "-exp")
            prog.op("vector",
                    lambda e: e.reduce_sum(rsc.t[:, rsc0 + 1:rsc0 + 2], rsc.t[:, rsc0 + 8:rsc0 + 16], axis=AX.X),
                    reads=[(rsc, rsc0 + 8, rsc0 + 16)], writes=[(rsc, rsc0 + 1, rsc0 + 2)], label=lb + "-s")
            prog.op("vector",
                    lambda e: e.reduce_max(rsc.t[:, rsc0 + 2:rsc0 + 3], rsc.t[:, rsc0 + 8:rsc0 + 16], axis=AX.X),
                    reads=[(rsc, rsc0 + 8, rsc0 + 16)], writes=[(rsc, rsc0 + 2, rsc0 + 3)], label=lb + "-m1")
            prog.op("vector",
                    lambda e: e.tensor_scalar(
                        out=rsc.t[:, rsc0 + 16:rsc0 + 24], in0=rsc.t[:, rsc0 + 8:rsc0 + 16],
                        scalar1=rsc.t[:, rsc0 + 2:rsc0 + 3], scalar2=-1.0,
                        op0=ALU.is_equal, op1=ALU.mult),
                    reads=[(rsc, rsc0 + 8, rsc0 + 16), (rsc, rsc0 + 2, rsc0 + 3)], writes=[(rsc, rsc0 + 16, rsc0 + 24)],
                    label=lb + "-negeq")
            prog.op("vector",
                    lambda e: e.tensor_scalar(
                        out=rsc.t[:, rsc0 + 16:rsc0 + 24], in0=rsc.t[:, rsc0 + 16:rsc0 + 24],
                        scalar1=1.0, scalar2=None, op0=ALU.add),
                    reads=[(rsc, rsc0 + 16, rsc0 + 24)], writes=[(rsc, rsc0 + 16, rsc0 + 24)],
                    label=lb + "-inv")
            prog.op("vector",
                    lambda e: e.tensor_mul(rsc.t[:, rsc0 + 16:rsc0 + 24], rsc.t[:, rsc0 + 16:rsc0 + 24],
                                           rsc.t[:, rsc0 + 8:rsc0 + 16]),
                    reads=[(rsc, rsc0 + 8, rsc0 + 24)], writes=[(rsc, rsc0 + 16, rsc0 + 24)],
                    label=lb + "-msk")
            prog.op("vector",
                    lambda e: e.reduce_max(rsc.t[:, rsc0 + 3:rsc0 + 4], rsc.t[:, rsc0 + 16:rsc0 + 24], axis=AX.X),
                    reads=[(rsc, rsc0 + 16, rsc0 + 24)], writes=[(rsc, rsc0 + 3, rsc0 + 4)], label=lb + "-m2")
            prog.op("vector",
                    lambda e: e.tensor_scalar(
                        out=rsc.t[:, rsc0 + 24:rsc0 + 32], in0=rsc.t[:, rsc0 + 8:rsc0 + 16],
                        scalar1=rsc.t[:, rsc0 + 3:rsc0 + 4], scalar2=None, op0=ALU.is_ge),
                    reads=[(rsc, rsc0 + 8, rsc0 + 16), (rsc, rsc0 + 3, rsc0 + 4)], writes=[(rsc, rsc0 + 24, rsc0 + 32)],
                    label=lb + "-ge")
            prog.op("vector",
                    lambda e: e.tensor_mul(rsc.t[:, rsc0 + 24:rsc0 + 32], rsc.t[:, rsc0 + 24:rsc0 + 32],
                                           rsc.t[:, rsc0 + 8:rsc0 + 16]),
                    reads=[(rsc, rsc0 + 8, rsc0 + 32)], writes=[(rsc, rsc0 + 24, rsc0 + 32)],
                    label=lb + "-wsel")
            prog.op("vector",
                    lambda e: e.reciprocal(rsc.t[:, rsc0 + 4:rsc0 + 5], rsc.t[:, rsc0 + 1:rsc0 + 2]),
                    reads=[(rsc, rsc0 + 1, rsc0 + 2)], writes=[(rsc, rsc0 + 4, rsc0 + 5)], label=lb + "-rs")
            wc = (ti * 4 + g) * 8
            prog.op("vector",
                    lambda e, wc=wc: e.tensor_scalar(
                        out=bufs["wstore"].t[:, wc:wc + 8], in0=rsc.t[:, rsc0 + 24:rsc0 + 32],
                        scalar1=rsc.t[:, rsc0 + 4:rsc0 + 5], scalar2=None, op0=ALU.mult),
                    reads=[(rsc, rsc0 + 24, rsc0 + 32), (rsc, rsc0 + 4, rsc0 + 5)],
                    writes=[(bufs["wstore"], wc, wc + 8)], label=lb + "-w")


def emit_moe_experts(prog, cst, w, bufs, x_in, x_out, xs_ln, xs_acc,
                     wbf_scr=None, scratch_mode="write", label="me",
                     router_cb=None):
    ps = bufs["ps"]
    w1sb, w2sb = bufs["w1sb"], bufs["w2sb"]
    hbuf, xlnt = bufs["hbuf"], bufs["lnbuf"]
    xresid, wbsb, tmpsb, tmp2 = bufs["xresid"], bufs["wbsb"], bufs["tmpsb"], bufs["tmp2"]
    for e in range(8):
        lbE = f"{label}e{e}"
        if scratch_mode == "read":
            for q4 in range(4):
                c0, c1 = q4 * 8192, (q4 + 1) * 8192
                prog.dma_batch("qs", [
                    (w1sb.t[:, c0:c1], wbf_scr.t.ap()[e, :, c0:c1],
                     [(wbf_scr, e * 65536 + c0, e * 65536 + c1)],
                     [(w1sb, c0, c1)]),
                ], label=lbE + f"-w1{q4}")
            for q4 in range(4):
                c0, c1 = q4 * 8192, (q4 + 1) * 8192
                prog.dma_batch("qs", [
                    (w2sb.t[:, c0:c1], wbf_scr.t.ap()[e, :, 32768 + c0:32768 + c1],
                     [(wbf_scr, e * 65536 + 32768 + c0, e * 65536 + 32768 + c1)],
                     [(w2sb, c0, c1)]),
                ], label=lbE + f"-w2{q4}")
        else:
            prog.dma_batch("qg", [
                (sb3(w1sb, 8, 4096),
                 w["w1"].t.ap()[e].rearrange("(c p) h -> p c h", p=128),
                 [], [(w1sb, 0, 32768)]),
            ], label=lbE + "-w1")
            prog.dma_batch("qg", [
                (sb3(w2sb, 32, 1024),
                 w["w2"].t.ap()[e].rearrange("(c p) d -> p c d", p=128),
                 [], [(w2sb, 0, 32768)]),
            ], label=lbE + "-w2")
            if scratch_mode == "write":
                prog.dma_batch("qa", [
                    (wbf_scr.t.ap()[e, :, 0:32768], w1sb.t[:],
                     [(w1sb, 0, 32768)],
                     [(wbf_scr, e * 65536, e * 65536 + 32768)]),
                ], label=lbE + "-sv1")
                prog.dma_batch("qa", [
                    (wbf_scr.t.ap()[e, :, 32768:65536], w2sb.t[:],
                     [(w2sb, 0, 32768)],
                     [(wbf_scr, e * 65536 + 32768, (e + 1) * 65536)]),
                ], label=lbE + "-sv2")
        for ti in range(NTILES):
            t0 = ti * NT
            lb = f"{lbE}t{ti}"
            if router_cb is not None and e == 0 and ti + 2 < NTILES:
                router_cb(ti + 2)
            prog.dma_batch("qs", [
                (sb3(xlnt, 8), x_tile_ap(xs_ln, t0), [(xs_ln, t0, t0 + NT)],
                 [(xlnt, 0, 8 * NT)]),
            ], label=lb + "-xln")
            if e == 7:
                prog.dma_batch("qs", [
                    (sb3(xresid, 8), x_tile_ap(x_in, t0),
                     [(x_in, t0, t0 + NT)], [(xresid, 0, 8 * NT)]),
                ], label=lb + "-xres")
            pw = ps["p4"]
            for g in range(4):
                wc = (ti * 4 + g) * 8 + e
                prog.pe_op(
                    lambda en, wc=wc, g=g, pw=pw: en.transpose(
                        pw.t[0:1, g * 128:(g + 1) * 128],
                        bufs["wstore"].t[:, wc:wc + 1], cst["identf"].t[:]),
                    reads=[(bufs["wstore"], wc, wc + 1), (cst["identf"], 0, 128)],
                    writes=[(pw, 0, NT)], label=lb + "-wt")
            prog.op("vector",
                    lambda e_, pw=pw: e_.tensor_copy(bufs["xcol2"].t[0:1, 0:NT],
                                                     pw.t[0:1, :]),
                    reads=[(pw, 0, NT)], writes=[(bufs["xcol2"], 0, NT)],
                    label=lb + "-wr16")
            prog.mm_group(ps["p5"], [
                (lambda e_: e_.matmul(ps["p5"].t[:], cst["ones_row"].t[0:1, 0:128],
                                      bufs["xcol2"].t[0:1, 0:NT],
                                      start=True, stop=True),
                 [(bufs["xcol2"], 0, NT), (cst["ones_row"], 0, 128)])],
                label=lb + "-wbc")
            prog.op("vector",
                    lambda e_: e_.tensor_copy(wbsb.t[:], ps["p5"].t[:]),
                    reads=[(ps["p5"], 0, NT)], writes=[(wbsb, 0, NT)],
                    label=lb + "-wbs")
            for hc in range(32):
                bk = ps["p0" if hc % 2 == 0 else "p1"]
                prog.mm_group(bk, [
                    (lambda e_, fc=fc, hc=hc, bk=bk: e_.matmul(
                        bk.t[:],
                        w1sb.t[:, fc * 4096 + hc * 128:fc * 4096 + hc * 128 + 128],
                        xlnt.t[:, fc * NT:(fc + 1) * NT],
                        start=(fc == 0), stop=(fc == 7)),
                     [(w1sb, fc * 4096 + hc * 128, fc * 4096 + hc * 128 + 128),
                      (xlnt, fc * NT, (fc + 1) * NT)])
                    for fc in range(8)], label=lb + "-h")
                prog.op("scalar",
                        lambda e_, hc=hc, bk=bk, e=e: e_.activation(
                            out=hbuf.t[:, hc * NT:(hc + 1) * NT], in_=bk.t[:],
                            func=AF.Gelu,
                            bias=w["b1"].t[:, e * 32 + hc:e * 32 + hc + 1]),
                        reads=[(bk, 0, NT), (w["b1"], e * 32 + hc, e * 32 + hc + 1)],
                        writes=[(hbuf, hc * NT, (hc + 1) * NT)], label=lb + "-g")
            for dc in range(8):
                bk = ps["p2" if dc % 2 == 0 else "p3"]
                tps_ = tmpsb if dc % 2 == 0 else bufs["tmp3"]
                if e == 7:
                    prog.dma_batch("qs", [
                        (tmp2.t[:],
                         x_tile_ap(xs_acc, t0, nt=NT, f0=dc * 128, nf=128)
                         .rearrange("p c t -> p (c t)"),
                         [(xs_acc, t0, t0 + NT)], [(tmp2, 0, NT)]),
                    ], label=lb + f"-ald{dc}")
                prog.mm_group(bk, [
                    (lambda e_, hc=hc, dc=dc, bk=bk: e_.matmul(
                        bk.t[:],
                        w2sb.t[:, hc * 1024 + dc * 128:hc * 1024 + dc * 128 + 128],
                        hbuf.t[:, hc * NT:(hc + 1) * NT],
                        start=(hc == 0), stop=(hc == 31)),
                     [(w2sb, hc * 1024 + dc * 128, hc * 1024 + dc * 128 + 128),
                      (hbuf, hc * NT, (hc + 1) * NT)])
                    for hc in range(32)], label=lb + "-eo")
                prog.op("vector",
                        lambda e_, dc=dc, bk=bk, e=e, tps_=tps_: e_.scalar_tensor_tensor(
                            tps_.t[:], bk.t[:],
                            w["b2"].t[:, e * 8 + dc:e * 8 + dc + 1], wbsb.t[:],
                            op0=ALU.add, op1=ALU.mult),
                        reads=[(bk, 0, NT), (w["b2"], e * 8 + dc, e * 8 + dc + 1),
                               (wbsb, 0, NT)],
                        writes=[(tps_, 0, NT)], label=lb + "-wt2")
                if e < 7:
                    kw = {} if e == 0 else {"accum_op": ALU.add}
                    prog.dma_batch("qg", [
                        (x_tile_ap(xs_acc, t0, nt=NT, f0=dc * 128, nf=128)
                         .rearrange("p c t -> p (c t)"), tps_.t[:],
                         [(tps_, 0, NT)], [(xs_acc, t0, t0 + NT)], kw),
                    ], label=lb + f"-acc{dc}")
                else:
                    prog.op("vector",
                            lambda e_, dc=dc, tps_=tps_: e_.tensor_tensor(
                                tps_.t[:], tps_.t[:], tmp2.t[:], op=ALU.add),
                            reads=[(tmp2, 0, NT), (tps_, 0, NT)],
                            writes=[(tps_, 0, NT)], label=lb + "-cmb")
                    prog.op("vector",
                            lambda e_, dc=dc, tps_=tps_: e_.scalar_tensor_tensor(
                                xresid.t[:, dc * NT:(dc + 1) * NT],
                                tps_.t[:], 0.5,
                                xresid.t[:, dc * NT:(dc + 1) * NT],
                                op0=ALU.mult, op1=ALU.add),
                            reads=[(tps_, 0, NT),
                                   (xresid, dc * NT, (dc + 1) * NT)],
                            writes=[(xresid, dc * NT, (dc + 1) * NT)],
                            label=lb + "-res")
            if e == 7:
                prog.dma_batch("qa", [
                    (x_tile_ap(x_out, t0), sb3(xresid, 8), [(xresid, 0, 8 * NT)],
                     [(x_out, t0, t0 + NT)]),
                ], label=lb + "-xst")


def host_moe_weights(inputs):
    w1 = np.ascontiguousarray(inputs["L0_ffn_W1"].astype(np.float32))
    w2 = np.ascontiguousarray(inputs["L0_ffn_W2"].astype(np.float32))
    wg = np.ascontiguousarray(
        inputs["L0_ffn_Wg"].reshape(8, 128, 8).transpose(1, 0, 2).reshape(128, 64))
    bg = np.ascontiguousarray(inputs["L0_ffn_bg"].reshape(1, 8))
    b1 = np.ascontiguousarray(
        inputs["L0_ffn_b1"].reshape(8, 32, 128).transpose(2, 0, 1).reshape(128, 256))
    b2 = np.ascontiguousarray(
        inputs["L0_ffn_b2"].reshape(8, 8, 128).transpose(2, 0, 1).reshape(128, 64))
    return {"w_moe_w1": w1, "w_moe_w2": w2, "w_moe_wg": wg, "w_moe_bg": bg,
            "w_moe_b1": b1, "w_moe_b2": b2}



WDN = 512
CDN = 128
NH = 8
DK = 128


def declare_dn_drams(env):
    return dict(
        d_qkv=env.dram("w_qkvT", (1024, 3072), F32, "ExternalInput"),
        d_zw=env.dram("w_zwT", (1024, 1024), F32, "ExternalInput"),
        d_ab=env.dram("w_abT", (1024, 16), F32, "ExternalInput"),
        d_ow=env.dram("w_owT", (1024, 1024), F32, "ExternalInput"),
        d_cv=env.dram("w_conv", (128, 96), F32, "ExternalInput"),
        d_dtb=env.dram("w_dtbrow", (1, 16), F32, "ExternalInput"),
        d_alog=env.dram("w_alog", (8, 1), F32, "ExternalInput"),
        d_nw=env.dram("w_normw", (128, 1), F32, "ExternalInput"),
        d_msT=env.dram("c_masksT", (128, 128), F32, "ExternalInput"),
        d_miT=env.dram("c_maskiT", (128, 128), F32, "ExternalInput"),
        d_onesf=env.dram("c_onesf", (1, 128), F32, "ExternalInput"),
    )


def stage_dn_weights(prog, env, dr=None):
    if dr is None:
        dr = declare_dn_drams(env)
    d_qkv, d_zw, d_ab, d_ow = dr["d_qkv"], dr["d_zw"], dr["d_ab"], dr["d_ow"]
    d_cv, d_dtb, d_alog, d_nw = dr["d_cv"], dr["d_dtb"], dr["d_alog"], dr["d_nw"]
    d_msT, d_miT, d_onesf = dr["d_msT"], dr["d_miT"], dr["d_onesf"]

    qkvT = env.sbuf("qkvT", (128, 8 * 3072), BF16)
    zwT = env.sbuf("zwT", (128, 8 * 1024), BF16)
    owT = env.sbuf("owT", (128, 8 * 1024), BF16)
    convw = env.sbuf("convw", (128, 96), F32)
    dtbrow = env.sbuf("dtbrow", (1, 16), BF16)
    alog = env.sbuf("alog", (8, 1), F32)
    normw = env.sbuf("normw", (128, 1), F32)
    msT = env.sbuf("msT", (128, 128), F32)
    miT = env.sbuf("miT", (128, 128), F32)
    onesf = env.sbuf("onesf", (1, 128), F32)
    nega = env.sbuf("nega", (8, 1), F32)

    prog.dma_batch("qg", [
        (sb3(qkvT, 8, 3072), d_qkv.ap.rearrange("(c p) o -> p c o", p=128),
         [], [(qkvT, 0, 8 * 3072)]),
        (sb3(zwT, 8, 1024), d_zw.ap.rearrange("(c p) o -> p c o", p=128),
         [], [(zwT, 0, 8 * 1024)]),
        (sb3(owT, 8, 1024), d_ow.ap.rearrange("(c p) o -> p c o", p=128),
         [], [(owT, 0, 8 * 1024)]),
        (convw.t[:], d_cv.ap, [], [(convw, 0, 96)]),
        (dtbrow.t[:], d_dtb.ap, [], [(dtbrow, 0, 16)]),
        (alog.t[:], d_alog.ap, [], [(alog, 0, 1)]),
        (normw.t[:], d_nw.ap, [], [(normw, 0, 1)]),
        (msT.t[:], d_msT.ap, [], [(msT, 0, 128)]),
        (miT.t[:], d_miT.ap, [], [(miT, 0, 128)]),
        (onesf.t[:], d_onesf.ap, [], [(onesf, 0, 128)]),
    ], label="dn-w")
    abTc = env.sbuf("abTc", (128, 8 * 16), BF16)
    prog.dma_batch("qg", [
        (sb3(abTc, 8, 16), d_ab.ap.rearrange("(c p) o -> p c o", p=128),
         [], [(abTc, 0, 8 * 16)]),
    ], label="dn-w2")
    prog.op("scalar", lambda e: e.activation(out=nega.t[:], in_=alog.t[:],
                                             func=AF.Exp),
            reads=[(alog, 0, 1)], writes=[(nega, 0, 1)], label="dn-nega")
    prog.op("vector", lambda e: e.tensor_scalar(
        out=nega.t[:], in0=nega.t[:], scalar1=-1.0, scalar2=None, op0=ALU.mult),
        reads=[(nega, 0, 1)], writes=[(nega, 0, 1)], label="dn-nega2")
    return dict(qkvT=qkvT, zwT=zwT, abT=abTc, owT=owT, convw=convw,
                dtbrow=dtbrow, nega=nega, normw=normw, msT=msT,
                miT=miT, onesf=onesf)


def make_dn_bufs(env):
    ps = {f"p{i}": env.psum(f"dps_p{i}", (128, NT), F32) for i in range(6)}
    ps["b0"] = env.psum("dps_b0", (128, 1024), BF16)
    ps["b1"] = env.psum("dps_b1", (128, 1024), BF16)
    return dict(
        mpre=env.sbuf("mpre", (128, 24 * 516), BF16),
        mpost=env.sbuf("mpost", (128, 2 * 24 * 128), BF16),
        lnbuf=env.sbuf("dlnbuf", (128, 8 * NT), BF16),
        xresid=env.sbuf("dxresid", (128, 8 * NT), F32),
        obuf=env.sbuf("obuf", (128, 2 * 1024), BF16),
        zsil=env.sbuf("zsil", (128, 2 * 1024), BF16),
        omf=env.sbuf("omf", (128, 8 * NT), BF16),
        S=env.sbuf("Sst", (128, 8 * 128), F32),
        Sb=env.sbuf("Sbst", (128, 8 * 128), BF16),
        brow=env.sbuf("brow", (8, WDN), F32),
        grow=env.sbuf("grow", (8, WDN), F32),
        zrow=env.sbuf("zrow", (8, CDN), F32),
        crow=env.sbuf("crow", (8, WDN), F32),
        crows=env.sbuf("crows", (1, 1024), F32),
        cols=env.sbuf("dncols", (128, 52), F32),
        powb=env.sbuf("powb", (128, 26 * 128), F32),
        wk=env.sbuf("dnwk", (128, 26 * 128), F32),
        xcol2=env.sbuf("dxcol2", (1, 2 * NT), BF16),
        ps=ps,
    )


def emit_dn_sublayer(prog, cst, w, bufs, x_in, x_out, label="dn"):
    ps = bufs["ps"]
    mpre, mpost, lnbuf = bufs["mpre"], bufs["mpost"], bufs["lnbuf"]
    xresid, obuf, zsil, omf = bufs["xresid"], bufs["obuf"], bufs["zsil"], bufs["omf"]
    S, Sb = bufs["S"], bufs["Sb"]
    brow, grow, zrow, crow = bufs["brow"], bufs["grow"], bufs["zrow"], bufs["crow"]
    crows, cols, powb, wk = bufs["crows"], bufs["cols"], bufs["powb"], bufs["wk"]
    lnps = dict(stat1=ps["p0"], stat2=ps["p1"], bc1=ps["p2"], bc2=ps["p3"])
    srow_ap = lambda: wk.t[0:8, 0:WDN]
    trow_ap = lambda: wk.t[0:8, WDN:2 * WDN]
    SR = (wk, 0, WDN)
    TR = (wk, WDN, 2 * WDN)

    prog.op("vector", lambda e: e.memset(zrow.t[:], 0.0), writes=[(zrow, 0, CDN)],
            label=label + "-z0")
    for vc in range(8):
        prog.op("vector",
                lambda e, vc=vc: e.tensor_scalar(
                    out=w["owT"].t[:, vc * 1024:(vc + 1) * 1024],
                    in0=w["owT"].t[:, vc * 1024:(vc + 1) * 1024],
                    scalar1=w["normw"].t[:, 0:1], scalar2=None, op0=ALU.mult),
                reads=[(w["owT"], vc * 1024, (vc + 1) * 1024), (w["normw"], 0, 1)],
                writes=[(w["owT"], vc * 1024, (vc + 1) * 1024)],
                label=label + "-nwfold")

    def head_chain(h, ch, hpar, cpar, lc):
        lh = lc + f"h{h}"
        par = hpar
        mb = cpar * 24
        pmain = ps["p3"] if par == 0 else ps["p5"]
        paux = ps["p2"] if par == 0 else ps["p4"]
        bnb = ps["b0"] if par == 0 else ps["b1"]
        wkb = 8 + par * 9
        pwb = par * 13

        def wa(i):
            return wk.t[:, (wkb + i) * 128:(wkb + i + 1) * 128]

        def wi(i):
            return (wk, (wkb + i) * 128, (wkb + i + 1) * 128)

        def pa(i):
            return powb.t[:, (pwb + i) * 128:(pwb + i + 1) * 128]

        def pi(i):
            return (powb, (pwb + i) * 128, (pwb + i + 1) * 128)

        qap = mpost.t[:, (mb + h) * 128:(mb + h + 1) * 128]
        qiv = (mpost, (mb + h) * 128, (mb + h + 1) * 128)
        kap = mpost.t[:, (mb + 8 + h) * 128:(mb + 8 + h + 1) * 128]
        kiv = (mpost, (mb + 8 + h) * 128, (mb + 8 + h + 1) * 128)
        vap = mpost.t[:, (mb + 16 + h) * 128:(mb + 16 + h + 1) * 128]
        viv = (mpost, (mb + 16 + h) * 128, (mb + 16 + h + 1) * 128)
        sap = Sb.t[:, h * 128:(h + 1) * 128]
        siv = (Sb, h * 128, (h + 1) * 128)
        prog.mm_group(paux, [
            (lambda e, h=h, paux=paux: e.matmul(
                paux.t[0:128, 0:128], w["onesf"].t[0:1, 0:128],
                crows.t[0:1, h * 128:(h + 1) * 128], start=True, stop=True),
             [(crows, h * 128, (h + 1) * 128), (w["onesf"], 0, 128)])],
            label=lh + "-bc")
        yield
        prog.op("vector",
                lambda e, h=h, paux=paux: e.tensor_scalar(
                    out=wa(0), in0=paux.t[0:128, 0:128],
                    scalar1=cols.t[:, 24 + h:25 + h], scalar2=0.0,
                    op0=ALU.add, op1=ALU.min),
                reads=[(paux, 0, NT), (cols, 24 + h, 25 + h)],
                writes=[wi(0)], label=lh + "-gd")
        yield
        prog.op("scalar",
                lambda e: e.activation(out=wa(0), in_=wa(0), func=AF.Exp),
                reads=[wi(0)], writes=[wi(0)], label=lh + "-gexp")
        yield
        prog.op("vector",
                lambda e: e.tensor_mul(wa(1), wa(0), w["miT"].t[:]),
                reads=[wi(0), (w["miT"], 0, 128)], writes=[wi(1)],
                label=lh + "-gi")
        yield
        prog.op("vector",
                lambda e: e.tensor_mul(wa(2), wa(0), w["msT"].t[:]),
                reads=[wi(0), (w["msT"], 0, 128)], writes=[wi(2)],
                label=lh + "-gs")
        yield
        prog.mm_group(pmain, [
            (lambda e, kap=kap, pmain=pmain: e.matmul(
                pmain.t[0:128, 0:128], kap, kap, start=True, stop=True),
             [kiv])], label=lh + "-akk")
        yield
        prog.op("vector",
                lambda e, pmain=pmain: e.tensor_tensor(
                    wa(2), wa(2), pmain.t[0:128, 0:128], op=ALU.mult),
                reads=[wi(2), (pmain, 0, NT)], writes=[wi(2)], label=lh + "-mt")
        yield
        prog.op("vector",
                lambda e, h=h: e.tensor_scalar(
                    out=pa(0), in0=wa(2), scalar1=cols.t[:, 8 + h:9 + h],
                    scalar2=None, op0=ALU.mult),
                reads=[wi(2), (cols, 8 + h, 9 + h)], writes=[pi(0)],
                label=lh + "-pt")
        yield
        prog.mm_group(pmain, [
            (lambda e, kap=kap, qap=qap, pmain=pmain: e.matmul(
                pmain.t[0:128, 0:128], kap, qap, start=True, stop=True),
             [kiv, qiv])], label=lh + "-aqk")
        yield
        prog.op("vector",
                lambda e, pmain=pmain: e.tensor_tensor(
                    wa(3), wa(1), pmain.t[0:128, 0:128], op=ALU.mult),
                reads=[wi(1), (pmain, 0, NT)], writes=[wi(3)], label=lh + "-wt")
        yield
        prog.pe_op(lambda e, paux=paux: e.transpose(
            paux.t[0:128, 0:128], pa(0), cst["identf"].t[:]),
            reads=[pi(0), (cst["identf"], 0, 128)], writes=[(paux, 0, NT)],
            label=lh + "-trp")
        yield
        prog.op("scalar",
                lambda e, paux=paux: e.activation(
                    out=pa(1), in_=paux.t[0:128, 0:128], func=AF.Identity),
                reads=[(paux, 0, NT)], writes=[pi(1)], label=lh + "-pcp")
        yield
        for (it, inn, ot, on) in [(0, 1, 2, 3), (2, 3, 4, 5), (4, 5, 6, 7),
                                  (6, 7, 8, 9), (8, 9, 10, 11)]:
            prog.mm_group(pmain, [
                (lambda e, it=it, inn=inn, pmain=pmain: e.matmul(
                    pmain.t[0:128, 0:128], pa(it), pa(inn),
                    start=True, stop=True), [pi(it), pi(inn)])],
                label=lh + "-sq")
            yield
            prog.op("scalar",
                    lambda e, on=on, pmain=pmain: e.activation(
                        out=pa(on), in_=pmain.t[0:128, 0:128],
                        func=AF.Identity),
                    reads=[(pmain, 0, NT)], writes=[pi(on)], label=lh + "-sqe")
            yield
            prog.mm_group(paux, [
                (lambda e, it=it, inn=inn, paux=paux: e.matmul(
                    paux.t[0:128, 0:128], pa(inn), pa(it),
                    start=True, stop=True), [pi(it), pi(inn)])],
                label=lh + "-sqt")
            yield
            prog.op("scalar",
                    lambda e, ot=ot, paux=paux: e.activation(
                        out=pa(ot), in_=paux.t[0:128, 0:128],
                        func=AF.Identity),
                    reads=[(paux, 0, NT)], writes=[pi(ot)], label=lh + "-sqte")
            yield
        prog.mm_group(pmain, [
            (lambda e, pmain=pmain: e.matmul(pmain.t[0:128, 0:128], pa(11),
                                             pa(10), start=True, stop=True),
             [pi(10), pi(11)])], label=lh + "-s64")
        yield
        prog.op("scalar",
                lambda e, pmain=pmain: e.activation(
                    out=pa(12), in_=pmain.t[0:128, 0:128], func=AF.Identity),
                reads=[(pmain, 0, NT)], writes=[pi(12)], label=lh + "-s64e")
        yield
        prog.mm_group(pmain, [
            (lambda e, kap=kap, sap=sap, pmain=pmain: e.matmul(
                pmain.t[0:128, 0:128], kap, sap, start=True, stop=True),
             [kiv, siv])], label=lh + "-ks")
        yield
        prog.pe_op(lambda e, vap=vap, bnb=bnb: e.transpose(
            bnb.t[:, 128:256], vap, cst["ident"].t[:]),
            reads=[viv, (cst["ident"], 0, 128)], writes=[(bnb, 0, 1024)],
            label=lh + "-vt")
        yield
        prog.op("scalar",
                lambda e, bnb=bnb: e.activation(
                    out=wa(4), in_=bnb.t[:, 128:256], func=AF.Identity),
                reads=[(bnb, 0, 1024)], writes=[wi(4)], label=lh + "-vte")
        yield
        prog.op("vector",
                lambda e, h=h, pmain=pmain: e.scalar_tensor_tensor(
                    wa(5), pmain.t[0:128, 0:128], cols.t[:, 32 + h:33 + h],
                    wa(4), op0=ALU.mult, op1=ALU.add),
                reads=[(pmain, 0, NT), (cols, 32 + h, 33 + h), wi(4)],
                writes=[wi(5)], label=lh + "-bt")
        yield
        tsrc = 5
        for pix in (12, 10, 8, 6, 4, 2):
            tdst = 6 if tsrc != 6 else 7
            prog.mm_group(pmain, [
                (lambda e, pix=pix, tsrc=tsrc, pmain=pmain: e.matmul(
                    pmain.t[0:128, 0:128], pa(pix), wa(tsrc),
                    start=True, stop=True), [pi(pix), wi(tsrc)])],
                label=lh + "-ap")
            yield
            prog.op("vector",
                    lambda e, tsrc=tsrc, tdst=tdst, pmain=pmain: e.tensor_tensor(
                        wa(tdst), wa(tsrc), pmain.t[0:128, 0:128], op=ALU.add),
                    reads=[wi(tsrc), (pmain, 0, NT)], writes=[wi(tdst)],
                    label=lh + "-ape")
            yield
            tsrc = tdst
        prog.mm_group(pmain, [
            (lambda e, tsrc=tsrc, pmain=pmain: e.matmul(
                pmain.t[0:128, 0:128], pa(0), wa(tsrc),
                start=True, stop=True), [pi(0), wi(tsrc)])], label=lh + "-apl")
        yield
        prog.op("vector",
                lambda e, tsrc=tsrc, pmain=pmain: e.tensor_tensor(
                    wa(8), wa(tsrc), pmain.t[0:128, 0:128], op=ALU.subtract),
                reads=[wi(tsrc), (pmain, 0, NT)], writes=[wi(8)],
                label=lh + "-ub")
        yield
        prog.op("vector",
                lambda e, h=h: e.tensor_scalar(
                    out=wa(8), in0=wa(8), scalar1=cols.t[:, 8 + h:9 + h],
                    scalar2=None, op0=ALU.mult),
                reads=[wi(8), (cols, 8 + h, 9 + h)], writes=[wi(8)],
                label=lh + "-u")
        yield
        prog.mm_group(paux, [
            (lambda e, qap=qap, sap=sap, paux=paux: e.matmul(
                paux.t[0:128, 0:128], qap, sap, start=True, stop=True),
             [qiv, siv])], label=lh + "-oint")
        yield
        prog.mm_group(pmain, [
            (lambda e, pmain=pmain: e.matmul(pmain.t[0:128, 0:128], wa(3),
                                             wa(8), start=True, stop=True),
             [wi(3), wi(8)])], label=lh + "-ointr")
        yield
        prog.op("vector",
                lambda e, h=h, paux=paux: e.tensor_scalar(
                    out=wa(2), in0=paux.t[0:128, 0:128],
                    scalar1=cols.t[:, 16 + h:17 + h], scalar2=None,
                    op0=ALU.mult),
                reads=[(paux, 0, NT), (cols, 16 + h, 17 + h)],
                writes=[wi(2)], label=lh + "-osc")
        yield
        prog.op("vector",
                lambda e, h=h, cpar=cpar, pmain=pmain: e.tensor_tensor(
                    obuf.t[:, cpar * 1024 + h * 128:cpar * 1024 + (h + 1) * 128],
                    wa(2), pmain.t[0:128, 0:128], op=ALU.add),
                reads=[wi(2), (pmain, 0, NT)],
                writes=[(obuf, cpar * 1024 + h * 128,
                         cpar * 1024 + (h + 1) * 128)],
                label=lh + "-o")
        yield
        prog.pe_op(lambda e, kap=kap, bnb=bnb: e.transpose(
            bnb.t[:, 256:384], kap, cst["ident"].t[:]),
            reads=[kiv, (cst["ident"], 0, 128)], writes=[(bnb, 0, 1024)],
            label=lh + "-kt")
        yield
        prog.op("vector",
                lambda e: e.tensor_copy(wa(0)[:, 0:1],
                                        wa(1)[:, 127:128]),
                reads=[wi(1)], writes=[wi(0)], label=lh + "-ghf")
        yield
        prog.op("vector",
                lambda e, bnb=bnb: e.tensor_scalar(
                    out=wa(4), in0=bnb.t[:, 256:384],
                    scalar1=wa(0)[:, 0:1], scalar2=None, op0=ALU.mult),
                reads=[(bnb, 0, 1024), wi(0)], writes=[wi(4)],
                label=lh + "-kh")
        yield
        prog.mm_group(pmain, [
            (lambda e, pmain=pmain: e.matmul(pmain.t[0:128, 0:128], wa(4),
                                             wa(8), start=True, stop=True),
             [wi(4), wi(8)])], label=lh + "-snew")
        yield
        prog.op("vector",
                lambda e, h=h, pmain=pmain: e.scalar_tensor_tensor(
                    S.t[:, h * 128:(h + 1) * 128],
                    S.t[:, h * 128:(h + 1) * 128],
                    cols.t[:, 40 + h:41 + h], pmain.t[0:128, 0:128],
                    op0=ALU.mult, op1=ALU.add),
                reads=[(S, h * 128, (h + 1) * 128), (cols, 40 + h, 41 + h),
                       (pmain, 0, NT)],
                writes=[(S, h * 128, (h + 1) * 128)], label=lh + "-S")
        yield
        prog.op("vector",
                lambda e, h=h: e.tensor_copy(
                    Sb.t[:, h * 128:(h + 1) * 128],
                    S.t[:, h * 128:(h + 1) * 128]),
                reads=[(S, h * 128, (h + 1) * 128)],
                writes=[(Sb, h * 128, (h + 1) * 128)], label=lh + "-Sb")
        yield
        prog.op("gpsimd",
                lambda e, h=h, cpar=cpar: e.tensor_tensor(
                    wa(0),
                    obuf.t[:, cpar * 1024 + h * 128:cpar * 1024 + (h + 1) * 128],
                    obuf.t[:, cpar * 1024 + h * 128:cpar * 1024 + (h + 1) * 128],
                    op=ALU.mult),
                reads=[(obuf, cpar * 1024 + h * 128,
                        cpar * 1024 + (h + 1) * 128)],
                writes=[wi(0)], label=lh + "-osq")
        yield
        prog.op("vector",
                lambda e, h=h, par=par: e.reduce_sum(
                    cols.t[:, 48 + par:49 + par], wa(0), axis=AX.X),
                reads=[wi(0)], writes=[(cols, 48 + par, 49 + par)],
                label=lh + "-ms")
        yield
        prog.op("scalar",
                lambda e, par=par: e.activation(
                    out=cols.t[:, 48 + par:49 + par],
                    in_=cols.t[:, 48 + par:49 + par], func=AF.Ln,
                    scale=1.0 / 128, bias=cst["misc"].t[:, 1:2]),
                reads=[(cols, 48 + par, 49 + par), (cst["misc"], 1, 2)],
                writes=[(cols, 48 + par, 49 + par)], label=lh + "-lnm")
        yield
        prog.op("scalar",
                lambda e, par=par: e.activation(
                    out=cols.t[:, 48 + par:49 + par],
                    in_=cols.t[:, 48 + par:49 + par], func=AF.Exp, scale=-0.5),
                reads=[(cols, 48 + par, 49 + par)],
                writes=[(cols, 48 + par, 49 + par)], label=lh + "-rsq")
        yield
        prog.op("vector",
                lambda e, h=h, par=par, cpar=cpar: e.scalar_tensor_tensor(
                    obuf.t[:, cpar * 1024 + h * 128:cpar * 1024 + (h + 1) * 128],
                    obuf.t[:, cpar * 1024 + h * 128:cpar * 1024 + (h + 1) * 128],
                    cols.t[:, 48 + par:49 + par],
                    zsil.t[:, cpar * 1024 + h * 128:cpar * 1024 + (h + 1) * 128],
                    op0=ALU.mult, op1=ALU.mult),
                reads=[(obuf, cpar * 1024 + h * 128,
                        cpar * 1024 + (h + 1) * 128),
                       (cols, 48 + par, 49 + par),
                       (zsil, cpar * 1024 + h * 128,
                        cpar * 1024 + (h + 1) * 128)],
                writes=[(obuf, cpar * 1024 + h * 128,
                         cpar * 1024 + (h + 1) * 128)],
                label=lh + "-om")
        yield

    for s in range(B_CORE):
        prog.op("vector", lambda e: e.memset(S.t[:], 0.0),
                writes=[(S, 0, 1024)], label=label + "-S0")
        prog.op("vector", lambda e: e.memset(Sb.t[:], 0.0),
                writes=[(Sb, 0, 1024)], label=label + "-Sb0")
        prog.op("vector", lambda e: e.memset(mpre.t[:, 0:24 * 516], 0.0),
                writes=[(mpre, 0, 24 * 516)], label=label + "-halo0")
        for wi_ in range(T_SEQ // WDN):
            t0 = s * T_SEQ + wi_ * WDN
            lb = f"{label}s{s}w{wi_}"
            emit_ln_tile(prog, cst, x_in, t0, xresid, crows,
                         bufs["xcol2"], lnbuf, omf, lnps, label=lb)
            if wi_ > 0:
                for c in range(24):
                    prog.op("vector",
                            lambda e, c=c: e.tensor_copy(
                                mpre.t[:, c * 516:c * 516 + 3],
                                mpre.t[:, c * 516 + 512:c * 516 + 515]),
                            reads=[(mpre, c * 516 + 512, c * 516 + 515)],
                            writes=[(mpre, c * 516, c * 516 + 3)],
                            label=lb + "-halo")
            for c in range(24):
                bk = ps["p0" if c % 2 == 0 else "p1"]
                prog.mm_group(bk, [
                    (lambda e, fc=fc, c=c, bk=bk: e.matmul(
                        bk.t[:],
                        w["qkvT"].t[:, fc * 3072 + c * 128:fc * 3072 + c * 128 + 128],
                        lnbuf.t[:, fc * NT:(fc + 1) * NT],
                        start=(fc == 0), stop=(fc == 7)),
                     [(w["qkvT"], fc * 3072 + c * 128, fc * 3072 + c * 128 + 128),
                      (lnbuf, fc * NT, (fc + 1) * NT)])
                    for fc in range(8)], label=lb + "-qkv")
                prog.op("scalar",
                        lambda e, c=c, bk=bk: e.activation(
                            out=mpre.t[:, c * 516 + 3:c * 516 + 3 + 512],
                            in_=bk.t[:], func=AF.Identity),
                        reads=[(bk, 0, NT)],
                        writes=[(mpre, c * 516 + 3, c * 516 + 516)],
                        label=lb + "-qkve")
            for which, rows0 in (("b", 0), ("a", 8)):
                bk = ps["p4"] if which == "b" else ps["p5"]
                parts = [
                    (lambda e, fc=fc, rows0=rows0, bk=bk: e.matmul(
                        bk.t[0:8, 0:WDN],
                        w["abT"].t[:, fc * 16 + rows0:fc * 16 + rows0 + 8],
                        lnbuf.t[:, fc * NT:(fc + 1) * NT],
                        start=(fc == 0), stop=(fc == 7 and rows0 == 0)),
                     [(w["abT"], fc * 16 + rows0, fc * 16 + rows0 + 8),
                      (lnbuf, fc * NT, (fc + 1) * NT)])
                    for fc in range(8)]
                if rows0 == 8:
                    parts.append(
                        (lambda e, bk=bk: e.matmul(
                            bk.t[0:8, 0:WDN], w["dtbrow"].t[0:1, 8:16],
                            cst["ones_row"].t[0:1, 0:WDN],
                            start=False, stop=True),
                         [(w["dtbrow"], 8, 16), (cst["ones_row"], 0, WDN)]))
                prog.mm_group(bk, parts, label=lb + "-ab" + which)
            prog.op("scalar",
                    lambda e: e.activation(out=brow.t[:],
                                           in_=ps["p4"].t[0:8, 0:WDN],
                                           func=AF.Sigmoid),
                    reads=[(ps["p4"], 0, NT)], writes=[(brow, 0, WDN)],
                    label=lb + "-sig")
            prog.op("scalar",
                    lambda e: e.activation(out=srow_ap(),
                                           in_=ps["p5"].t[0:8, 0:WDN],
                                           func=AF.Abs),
                    reads=[(ps["p5"], 0, NT)], writes=[SR], label=lb + "-abs")
            prog.op("scalar",
                    lambda e: e.activation(out=srow_ap(), in_=srow_ap(),
                                           func=AF.Exp, scale=-1.0),
                    reads=[SR], writes=[SR], label=lb + "-en")
            prog.op("scalar",
                    lambda e: e.activation(out=srow_ap(), in_=srow_ap(),
                                           func=AF.Ln, bias=1.0),
                    reads=[SR], writes=[SR], label=lb + "-l1p")
            prog.op("vector",
                    lambda e: e.tensor_scalar(out=trow_ap(),
                                              in0=ps["p5"].t[0:8, 0:WDN],
                                              scalar1=0.0, scalar2=None,
                                              op0=ALU.max),
                    reads=[(ps["p5"], 0, NT)], writes=[TR], label=lb + "-relu")
            prog.op("vector",
                    lambda e: e.tensor_add(grow.t[:], srow_ap(), trow_ap()),
                    reads=[SR, TR], writes=[(grow, 0, WDN)], label=lb + "-sp")
            prog.op("vector",
                    lambda e: e.tensor_scalar(out=grow.t[:], in0=grow.t[:],
                                              scalar1=w["nega"].t[:, 0:1],
                                              scalar2=None, op0=ALU.mult),
                    reads=[(grow, 0, WDN), (w["nega"], 0, 1)],
                    writes=[(grow, 0, WDN)], label=lb + "-g")
            for ch in range(4):
                prog.op("vector",
                        lambda e, ch=ch: e.tensor_tensor_scan(
                            out=crow.t[:, ch * CDN:(ch + 1) * CDN],
                            data0=grow.t[:, ch * CDN:(ch + 1) * CDN],
                            data1=zrow.t[:, 0:CDN],
                            initial=0.0, op0=ALU.add, op1=ALU.add),
                        reads=[(grow, ch * CDN, (ch + 1) * CDN), (zrow, 0, CDN)],
                        writes=[(crow, ch * CDN, (ch + 1) * CDN)],
                        label=lb + "-cs")
            for ch in range(4):
                lc = lb + f"c{ch}"
                par = ch % 2
                mb = par * 24
                for c in range(24):
                    prog.op("vector",
                            lambda e, c=c, ch=ch, mb=mb: e.tensor_scalar(
                                out=mpost.t[:, (mb + c) * 128:(mb + c + 1) * 128],
                                in0=mpre.t[:, c * 516 + ch * 128:c * 516 + ch * 128 + 128],
                                scalar1=w["convw"].t[:, c * 4:c * 4 + 1],
                                scalar2=None, op0=ALU.mult),
                            reads=[(mpre, c * 516 + ch * 128,
                                    c * 516 + ch * 128 + 128),
                                   (w["convw"], c * 4, c * 4 + 1)],
                            writes=[(mpost, (mb + c) * 128, (mb + c + 1) * 128)],
                            label=lc + "-cv0")
                    for j in (1, 2, 3):
                        prog.op("vector",
                                lambda e, c=c, j=j, ch=ch, mb=mb:
                                e.scalar_tensor_tensor(
                                    mpost.t[:, (mb + c) * 128:(mb + c + 1) * 128],
                                    mpre.t[:, c * 516 + ch * 128 + j:c * 516 + ch * 128 + j + 128],
                                    w["convw"].t[:, c * 4 + j:c * 4 + j + 1],
                                    mpost.t[:, (mb + c) * 128:(mb + c + 1) * 128],
                                    op0=ALU.mult, op1=ALU.add),
                                reads=[(mpre, c * 516 + ch * 128,
                                        c * 516 + ch * 128 + 131),
                                       (w["convw"], c * 4 + j, c * 4 + j + 1),
                                       (mpost, (mb + c) * 128, (mb + c + 1) * 128)],
                                writes=[(mpost, (mb + c) * 128,
                                         (mb + c + 1) * 128)],
                                label=lc + f"-cv{j}")
                prog.pe_op(lambda e, ch=ch: e.transpose(
                    ps["p0"].t[0:128, 0:8], crow.t[:, ch * CDN:(ch + 1) * CDN],
                    cst["identf"].t[0:8, 0:8]),
                    reads=[(crow, ch * CDN, (ch + 1) * CDN),
                           (cst["identf"], 0, 128)],
                    writes=[(ps["p0"], 0, NT)], label=lc + "-tc")
                prog.op("vector", lambda e: e.tensor_copy(cols.t[:, 0:8],
                                                          ps["p0"].t[0:128, 0:8]),
                        reads=[(ps["p0"], 0, NT)], writes=[(cols, 0, 8)],
                        label=lc + "-ccol")
                prog.pe_op(lambda e, ch=ch: e.transpose(
                    ps["p0"].t[0:128, 0:8], brow.t[:, ch * CDN:(ch + 1) * CDN],
                    cst["identf"].t[0:8, 0:8]),
                    reads=[(brow, ch * CDN, (ch + 1) * CDN),
                           (cst["identf"], 0, 128)],
                    writes=[(ps["p0"], 0, NT)], label=lc + "-tb")
                prog.op("vector", lambda e: e.tensor_copy(cols.t[:, 8:16],
                                                          ps["p0"].t[0:128, 0:8]),
                        reads=[(ps["p0"], 0, NT)], writes=[(cols, 8, 16)],
                        label=lc + "-bcol")
                for h in range(NH):
                    bko = ps["p0"] if h < 4 else ps["p1"]
                    prog.pe_op(lambda e, h=h, bko=bko: e.transpose(
                        bko.t[0:1, (h % 4) * 128:(h % 4 + 1) * 128],
                        cols.t[:, h:h + 1], cst["identf"].t[:]),
                        reads=[(cols, h, h + 1), (cst["identf"], 0, 128)],
                        writes=[(bko, 0, NT)], label=lc + "-tr")
                prog.op("vector", lambda e: e.tensor_copy(crows.t[0:1, 0:512],
                                                          ps["p0"].t[0:1, 0:512]),
                        reads=[(ps["p0"], 0, NT)], writes=[(crows, 0, 512)],
                        label=lc + "-crA")
                prog.op("vector", lambda e: e.tensor_copy(crows.t[0:1, 512:1024],
                                                          ps["p1"].t[0:1, 0:512]),
                        reads=[(ps["p1"], 0, NT)], writes=[(crows, 512, 1024)],
                        label=lc + "-crB")
                prog.op("scalar", lambda e: e.activation(
                    out=cols.t[:, 16:24], in_=cols.t[:, 0:8], func=AF.Exp),
                    reads=[(cols, 0, 8)], writes=[(cols, 16, 24)],
                    label=lc + "-ecol")
                prog.op("vector", lambda e: e.tensor_scalar(
                    out=cols.t[:, 24:32], in0=cols.t[:, 0:8], scalar1=-1.0,
                    scalar2=None, op0=ALU.mult),
                    reads=[(cols, 0, 8)], writes=[(cols, 24, 32)],
                    label=lc + "-ncc")
                prog.op("vector", lambda e: e.tensor_scalar(
                    out=cols.t[:, 32:40], in0=cols.t[:, 16:24], scalar1=-1.0,
                    scalar2=None, op0=ALU.mult),
                    reads=[(cols, 16, 24)], writes=[(cols, 32, 40)],
                    label=lc + "-nec")
                prog.mm_group(ps["p0"], [
                    (lambda e: e.matmul(
                        ps["p0"].t[0:128, 0:8], w["onesf"].t[0:1, 0:128],
                        crows.t[0:1, 127:1024:128], start=True, stop=True),
                     [(crows, 0, 1024), (w["onesf"], 0, 128)])], label=lc + "-ccb")
                prog.op("scalar", lambda e: e.activation(
                    out=cols.t[:, 40:48], in_=ps["p0"].t[0:128, 0:8],
                    func=AF.Exp),
                    reads=[(ps["p0"], 0, NT)], writes=[(cols, 40, 48)],
                    label=lc + "-eCC")
                for half in range(2):
                    bk = ps["p0"] if half == 0 else ps["p1"]
                    prog.mm_group(bk, [
                        (lambda e, fc=fc, ch=ch, half=half, bk=bk: e.matmul(
                            bk.t[:],
                            lnbuf.t[:, fc * NT + ch * CDN:fc * NT + ch * CDN + 128],
                            w["zwT"].t[:, fc * 1024 + half * 512:fc * 1024 + (half + 1) * 512],
                            start=(fc == 0), stop=(fc == 7)),
                         [(lnbuf, fc * NT + ch * CDN, fc * NT + ch * CDN + 128),
                          (w["zwT"], fc * 1024 + half * 512,
                           fc * 1024 + (half + 1) * 512)])
                        for fc in range(8)], label=lc + f"-z{half}")
                    prog.op("scalar",
                            lambda e, par=par, half=half, bk=bk: e.activation(
                                out=zsil.t[:, par * 1024 + half * 512:par * 1024 + (half + 1) * 512],
                                in_=bk.t[:], func=AF.Silu),
                            reads=[(bk, 0, NT)],
                            writes=[(zsil, par * 1024 + half * 512,
                                     par * 1024 + (half + 1) * 512)],
                            label=lc + f"-zsil{half}")
                for hp in range(NH // 2):
                    g0 = head_chain(2 * hp, ch, 0, par, lc)
                    g1 = head_chain(2 * hp + 1, ch, 1, par, lc)
                    a_done = b_done = False
                    while not (a_done and b_done):
                        if not a_done:
                            try:
                                next(g0)
                            except StopIteration:
                                a_done = True
                        if not b_done:
                            try:
                                next(g1)
                            except StopIteration:
                                b_done = True
                for vc in range(8):
                    prog.pe_op(lambda e, vc=vc, par=par: e.transpose(
                        ps["b0"].t[:, 384:512],
                        obuf.t[:, par * 1024 + vc * 128:par * 1024 + (vc + 1) * 128],
                        cst["ident"].t[:]),
                        reads=[(obuf, par * 1024 + vc * 128,
                                par * 1024 + (vc + 1) * 128),
                               (cst["ident"], 0, 128)],
                        writes=[(ps["b0"], 0, 1024)], label=lc + "-ot")
                    prog.op("vector",
                            lambda e, vc=vc, ch=ch: e.tensor_copy(
                                omf.t[:, vc * NT + ch * CDN:vc * NT + ch * CDN + 128],
                                ps["b0"].t[:, 384:512]),
                            reads=[(ps["b0"], 0, 1024)],
                            writes=[(omf, vc * NT + ch * CDN,
                                     vc * NT + ch * CDN + 128)],
                            label=lc + "-ote")
            for fc in range(8):
                bk = ps["p0" if fc % 2 == 0 else "p1"]
                prog.mm_group(bk, [
                    (lambda e, vc=vc, fc=fc, bk=bk: e.matmul(
                        bk.t[:],
                        w["owT"].t[:, vc * 1024 + fc * 128:vc * 1024 + fc * 128 + 128],
                        omf.t[:, vc * NT:(vc + 1) * NT],
                        start=(vc == 0), stop=(vc == 7)),
                     [(w["owT"], vc * 1024 + fc * 128, vc * 1024 + fc * 128 + 128),
                      (omf, vc * NT, (vc + 1) * NT)])
                    for vc in range(8)], label=lb + "-ow")
                prog.op("vector",
                        lambda e, fc=fc, bk=bk: e.tensor_tensor(
                            xresid.t[:, fc * NT:(fc + 1) * NT],
                            xresid.t[:, fc * NT:(fc + 1) * NT], bk.t[:],
                            op=ALU.add),
                        reads=[(bk, 0, NT), (xresid, fc * NT, (fc + 1) * NT)],
                        writes=[(xresid, fc * NT, (fc + 1) * NT)],
                        label=lb + "-res")
            prog.dma_batch("qa", [
                (x_tile_ap(x_out, t0), sb3(xresid, 8), [(xresid, 0, 8 * NT)],
                 [(x_out, t0, t0 + NT)]),
            ], label=lb + "-xst")


def host_dn_weights(inputs):
    qkv = inputs["L0_attn_qkv_w"]
    zw = inputs["L0_attn_z_w"]
    bw = inputs["L0_attn_b_w"]
    aw = inputs["L0_attn_a_w"]
    cw = inputs["L0_attn_conv_w"]
    dtb = inputs["L0_attn_dt_bias"]
    alog = inputs["L0_attn_A_log"]
    nw = inputs["L0_attn_norm_w"]
    ow = inputs["L0_attn_out_w"]
    i128 = np.arange(128)
    dtrow = np.zeros((1, 16), np.float32)
    dtrow[0, 8:16] = dtb
    return {
        "w_qkvT": np.ascontiguousarray(qkv.T.astype(np.float32)),
        "w_zwT": np.ascontiguousarray(zw.T.astype(np.float32)),
        "w_abT": np.ascontiguousarray(
            np.concatenate([bw, aw], 0).T.astype(np.float32)),
        "w_owT": np.ascontiguousarray(ow.T.astype(np.float32)),
        "w_conv": np.ascontiguousarray(
            cw.reshape(24, 128, 4).transpose(1, 0, 2).reshape(128, 96)
            .astype(np.float32)),
        "w_dtbrow": dtrow,
        "w_alog": np.ascontiguousarray(alog.reshape(8, 1).astype(np.float32)),
        "w_normw": np.ascontiguousarray(nw.reshape(128, 1).astype(np.float32)),
        "c_masksT": np.ascontiguousarray(
            (i128[:, None] < i128[None, :]).astype(np.float32)),
        "c_maskiT": np.ascontiguousarray(
            (i128[:, None] <= i128[None, :]).astype(np.float32)),
        "c_onesf": np.ones((1, 128), np.float32),
    }



ATT_SCALE = float(1.0 / np.sqrt(64 * 2.0))
ATT_TEMP = 1.5


def declare_att_drams(env):
    d = {}
    for nm in ("WqT", "WkT", "WvT", "WoT"):
        d[nm] = env.dram("w_" + nm, (1024, 1024), F32, "ExternalInput")
    d["d_bq"] = env.dram("w_bqc", (128, 8), F32, "ExternalInput")
    d["d_bk"] = env.dram("w_bkc", (128, 8), F32, "ExternalInput")
    d["d_bv"] = env.dram("w_bvr", (1, 1024), F32, "ExternalInput")
    d["d_bo"] = env.dram("w_bor", (1, 1024), F32, "ExternalInput")
    d["d_msk"] = env.dram("c_att", (128, 5 * 512), F32, "ExternalInput")
    return d


def stage_att_weights(prog, env, dr=None):
    if dr is None:
        dr = declare_att_drams(env)
    d = dict(dr)
    d_bq, d_bk, d_bv, d_bo, d_msk = (dr["d_bq"], dr["d_bk"], dr["d_bv"],
                                     dr["d_bo"], dr["d_msk"])
    bq = env.sbuf("bqc", (128, 8), F32)
    bk = env.sbuf("bkc", (128, 8), F32)
    bv = env.sbuf("bvr", (1, 1024), BF16)
    bo = env.sbuf("bor", (1, 1024), BF16)
    msk = env.sbuf("attmsk", (128, 5 * 512), F32)
    prog.dma_batch("qg", [
        (bq.t[:], d_bq.ap, [], [(bq, 0, 8)]),
        (bk.t[:], d_bk.ap, [], [(bk, 0, 8)]),
        (bv.t[:], d_bv.ap, [], [(bv, 0, 1024)]),
        (bo.t[:], d_bo.ap, [], [(bo, 0, 1024)]),
        (msk.t[:], d_msk.ap, [], [(msk, 0, 2560)]),
    ], label="att-w")
    d.update(bq=bq, bk=bk, bv=bv, bo=bo, msk=msk)
    return d


def make_att_bufs(env):
    ps = {f"p{i}": env.psum(f"aps_p{i}", (128, NT), F32) for i in range(8)}
    return dict(
        wA=env.sbuf("awA", (128, 8 * 1024), BF16),
        wB=env.sbuf("awB", (128, 8 * 1024), BF16),
        xlnS=env.sbuf("axln", (128, 8 * 2048), BF16),
        kbuf=env.sbuf("akbuf", (128, 8 * 2048), BF16),
        vbuf=env.sbuf("avbuf", (128, 16 * 1024), BF16),
        qbuf=env.sbuf("aqbuf", (128, 8 * NT), BF16),
        obuf=env.sbuf("aobuf", (128, 8 * NT), BF16),
        pbuf=env.sbuf("apbuf", (128, 4 * NT), BF16),
        sqb=env.sbuf("asqb", (128, NT), F32),
        bcb=env.sbuf("abcb", (128, NT), F32),
        bcb2=env.sbuf("abcb2", (128, NT), F32),
        bcb3=env.sbuf("abcb3", (128, NT), F32),
        bcb4=env.sbuf("abcb4", (128, NT), F32),
        xresid=env.sbuf("axresid", (128, 8 * NT), F32),
        xcol=env.sbuf("axcol", (1, 2 * NT), F32),
        xcol2=env.sbuf("axcol2", (1, 2 * NT), BF16),
        ps=ps,
    )


def _load_att_w(prog, wbuf, dram, label):
    prog.dma_batch("qg", [
        (sb3(wbuf, 8, 1024), dram.ap.rearrange("(c p) o -> p c o", p=128),
         [], [(wbuf, 0, 8192)]),
    ], label=label)


def emit_att_sublayer(prog, cst, w, bufs, x_in, x_out, label="at"):
    ps = bufs["ps"]
    wA, wB = bufs["wA"], bufs["wB"]
    xlnS, kbuf, vbuf, qbuf = bufs["xlnS"], bufs["kbuf"], bufs["vbuf"], bufs["qbuf"]
    obuf, pbuf, sqb, bcb = bufs["obuf"], bufs["pbuf"], bufs["sqb"], bufs["bcb"]
    xresid, xcol, xcol2 = bufs["xresid"], bufs["xcol"], bufs["xcol2"]
    lnps = dict(stat1=ps["p0"], stat2=ps["p1"], bc1=ps["p2"], bc2=ps["p3"])
    NTQ = T_SEQ // NT

    def proj_norm(lb, Wsb, bcol, src_off, dst_buf, dst_cols):
        for rc in range(8):
            bk_ = ps["p4" if rc % 2 == 0 else "p5"]
            prog.mm_group(bk_, [
                (lambda e, fc=fc, rc=rc, bk_=bk_: e.matmul(
                    bk_.t[:],
                    Wsb.t[:, fc * 1024 + rc * 128:fc * 1024 + rc * 128 + 128],
                    xlnS.t[:, fc * 2048 + src_off:fc * 2048 + src_off + NT],
                    start=(fc == 0), stop=(fc == 7)),
                 [(Wsb, fc * 1024 + rc * 128, fc * 1024 + rc * 128 + 128),
                  (xlnS, fc * 2048 + src_off, fc * 2048 + src_off + NT)])
                for fc in range(8)], label=lb + "-pj")
            prog.op("scalar",
                    lambda e, rc=rc, bk_=bk_: e.activation(
                        out=sqb.t[:], in_=bk_.t[:], func=AF.Identity,
                        bias=bcol.t[:, rc:rc + 1]),
                    reads=[(bk_, 0, NT), (bcol, rc, rc + 1)],
                    writes=[(sqb, 0, NT)], label=lb + "-ev")
            prog.op("vector",
                    lambda e: e.tensor_mul(pbuf.t[:, 0:NT], sqb.t[:], sqb.t[:]),
                    reads=[(sqb, 0, NT)], writes=[(pbuf, 0, NT)],
                    label=lb + "-sq")
            for sub in range(2):
                b0 = sub * 64
                bnk = ps["p6"] if sub == 0 else ps["p7"]
                prog.mm_group(bnk, [
                    (lambda e, b0=b0, bnk=bnk: e.matmul(
                        bnk.t[0:1, 0:NT], cst["ones_col"].t[b0:b0 + 64, 0:1],
                        pbuf.t[b0:b0 + 64, 0:NT], start=True, stop=True),
                     [(pbuf, 0, NT), (cst["ones_col"], 0, 1)])],
                    label=lb + "-ss")
                prog.op("scalar",
                        lambda e, bnk=bnk: e.activation(
                            out=xcol.t[0:1, 0:NT], in_=bnk.t[0:1, 0:NT],
                            func=AF.Ln, bias=cst["misc"].t[0:1, 3:4]),
                        reads=[(bnk, 0, NT), (cst["misc"], 3, 4)],
                        writes=[(xcol, 0, NT)], label=lb + "-ln")
                prog.op("scalar",
                        lambda e: e.activation(out=xcol.t[0:1, 0:NT],
                                               in_=xcol.t[0:1, 0:NT],
                                               func=AF.Exp, scale=-0.5),
                        reads=[(xcol, 0, NT)], writes=[(xcol, 0, NT)],
                        label=lb + "-ex")
                prog.op("vector",
                        lambda e: e.tensor_scalar(
                            out=xcol2.t[0:1, 0:NT], in0=xcol.t[0:1, 0:NT],
                            scalar1=1e8, scalar2=None, op0=ALU.min),
                        reads=[(xcol, 0, NT)], writes=[(xcol2, 0, NT)],
                        label=lb + "-mn")
                prog.mm_group(bnk, [
                    (lambda e, bnk=bnk: e.matmul(
                        bnk.t[0:64, 0:NT], cst["ones_row"].t[0:1, 0:64],
                        xcol2.t[0:1, 0:NT], start=True, stop=True),
                     [(xcol2, 0, NT), (cst["ones_row"], 0, 64)])],
                    label=lb + "-bc")
                prog.op("vector",
                        lambda e, rc=rc, b0=b0, bnk=bnk,
                        dc=dst_cols[rc]: e.tensor_tensor(
                            dst_buf.t[b0:b0 + 64, dc:dc + NT],
                            sqb.t[b0:b0 + 64, 0:NT], bnk.t[0:64, 0:NT],
                            op=ALU.mult),
                        reads=[(sqb, 0, NT), (bnk, 0, NT)],
                        writes=[(dst_buf, dst_cols[rc], dst_cols[rc] + NT)],
                        label=lb + "-nm")

    for s in range(B_CORE):
        sb_ = s * T_SEQ
        lbS = f"{label}s{s}"
        _load_att_w(prog, wA, w["WkT"], lbS + "-wk")
        _load_att_w(prog, wB, w["WvT"], lbS + "-wv")
        for ti in range(NTQ):
            t0 = sb_ + ti * NT
            lb = f"{lbS}l{ti}"
            emit_ln_tile(prog, cst, x_in, t0, xresid, xcol, xcol2, qbuf,
                         obuf, lnps, label=lb)
            for fc in range(8):
                prog.op("vector",
                        lambda e, fc=fc, ti=ti: e.tensor_copy(
                            xlnS.t[:, fc * 2048 + ti * NT:fc * 2048 + (ti + 1) * NT],
                            qbuf.t[:, fc * NT:(fc + 1) * NT]),
                        reads=[(qbuf, fc * NT, (fc + 1) * NT)],
                        writes=[(xlnS, fc * 2048 + ti * NT,
                                 fc * 2048 + (ti + 1) * NT)],
                        label=lb + "-plc")
        for ti in range(NTQ):
            proj_norm(f"{lbS}k{ti}", wA, w["bk"], ti * NT, kbuf,
                      [rc * 2048 + ti * NT for rc in range(8)])
        for ti in range(NTQ):
            lb = f"{lbS}v{ti}"
            for tc4 in range(4):
                tc = ti * 4 + tc4
                for half in range(2):
                    bk_ = ps["p4"] if half == 0 else ps["p5"]
                    parts = [
                        (lambda e, fc=fc, tc=tc, half=half, bk_=bk_: e.matmul(
                            bk_.t[:],
                            xlnS.t[:, fc * 2048 + tc * 128:fc * 2048 + tc * 128 + 128],
                            wB.t[:, fc * 1024 + half * 512:fc * 1024 + (half + 1) * 512],
                            start=(fc == 0), stop=False),
                         [(xlnS, fc * 2048 + tc * 128, fc * 2048 + tc * 128 + 128),
                          (wB, fc * 1024 + half * 512,
                           fc * 1024 + (half + 1) * 512)])
                        for fc in range(8)]
                    parts.append(
                        (lambda e, half=half, bk_=bk_: e.matmul(
                            bk_.t[:], cst["ones_row"].t[0:1, 0:128],
                            w["bv"].t[0:1, half * 512:(half + 1) * 512],
                            start=False, stop=True),
                         [(cst["ones_row"], 0, 128),
                          (w["bv"], half * 512, (half + 1) * 512)]))
                    prog.mm_group(bk_, parts, label=lb + f"-v{half}")
                    prog.op("vector",
                            lambda e, tc=tc, half=half, bk_=bk_: e.tensor_copy(
                                vbuf.t[:, tc * 1024 + half * 512:tc * 1024 + (half + 1) * 512],
                                bk_.t[:]),
                            reads=[(bk_, 0, NT)],
                            writes=[(vbuf, tc * 1024 + half * 512,
                                     tc * 1024 + (half + 1) * 512)],
                            label=lb + f"-ve{half}")
        _load_att_w(prog, wA, w["WqT"], lbS + "-wq")
        _load_att_w(prog, wB, w["WoT"], lbS + "-wo")
        for qt in range(NTQ):
            lbq = f"{lbS}q{qt}"
            proj_norm(lbq, wA, w["bq"], qt * NT, qbuf,
                      [rc * NT for rc in range(8)])
            nkc = (qt + 1) * 4

            def head_att(h, lbq=lbq, qt=qt, nkc=nkc):
                lbh = lbq + f"h{h}"
                rc, b0 = h // 2, (h % 2) * 64
                par = h % 2
                scs = (ps["p0"], ps["p3"]) if par == 0 else (ps["p4"], ps["p7"])
                av = ps["p1"] if par == 0 else ps["p5"]
                lsum = ps["p2"] if par == 0 else ps["p6"]
                pep = ps["p3"] if par == 0 else ps["p7"]
                cb = bufs["bcb"] if par == 0 else bufs["bcb2"]
                cb2 = bufs["bcb3"] if par == 0 else bufs["bcb4"]
                xo0 = NT if par == 0 else 0
                def emit_qk(kc):
                    sc = scs[kc % 2]
                    prog.mm_group(sc, [
                        (lambda e, rc=rc, b0=b0, kc=kc, sc=sc: e.matmul(
                            sc.t[:],
                            kbuf.t[b0:b0 + 64,
                                   rc * 2048 + kc * 128:rc * 2048 + kc * 128 + 128],
                            qbuf.t[b0:b0 + 64, rc * NT:(rc + 1) * NT],
                            start=True, stop=True),
                         [(kbuf, rc * 2048 + kc * 128,
                           rc * 2048 + kc * 128 + 128),
                          (qbuf, rc * NT, (rc + 1) * NT)])],
                        label=lbh + f"k{kc}-qk")

                def emit_clmk(kc):
                    sc = scs[kc % 2]
                    cb = (bufs["bcb"], bufs["bcb3"])[kc % 2] if par == 0 else \
                         (bufs["bcb2"], bufs["bcb4"])[kc % 2]
                    rel = kc * 128 - qt * NT
                    mi = rel // 128 if 0 <= rel < NT else 4
                    prog.op("vector",
                            lambda e, sc=sc, cb=cb: e.tensor_scalar(
                                out=cb.t[:], in0=sc.t[:], scalar1=ATT_SCALE,
                                scalar2=-5.0, op0=ALU.mult, op1=ALU.max),
                            reads=[(sc, 0, NT)], writes=[(cb, 0, NT)],
                            label=lbh + f"k{kc}-cl")
                    prog.op("vector",
                            lambda e, mi=mi, cb=cb: e.tensor_tensor(
                                cb.t[:], cb.t[:],
                                w["msk"].t[:, mi * 512:(mi + 1) * 512],
                                op=ALU.min),
                            reads=[(cb, 0, NT),
                                   (w["msk"], mi * 512, (mi + 1) * 512)],
                            writes=[(cb, 0, NT)], label=lbh + f"k{kc}-mk")

                def emit_ex(kc):
                    cb = (bufs["bcb"], bufs["bcb3"])[kc % 2] if par == 0 else \
                         (bufs["bcb2"], bufs["bcb4"])[kc % 2]
                    pslot = par * 2 + (kc % 2)
                    prog.op("scalar",
                            lambda e, pslot=pslot, cb=cb: e.activation(
                                out=pbuf.t[:, pslot * NT:(pslot + 1) * NT],
                                in_=cb.t[:], func=AF.Exp,
                                scale=1.0 / ATT_TEMP),
                            reads=[(cb, 0, NT)],
                            writes=[(pbuf, pslot * NT, (pslot + 1) * NT)],
                            label=lbh + f"k{kc}-ex")

                def emit_lsav(kc, first):
                    pslot = par * 2 + (kc % 2)
                    prog.mm_group(lsum, [
                        (lambda e, pslot=pslot, lsum=lsum, first=first,
                         last=(kc == nkc - 1): e.matmul(
                            lsum.t[0:1, 0:NT], cst["ones_col"].t[:, 0:1],
                            pbuf.t[:, pslot * NT:(pslot + 1) * NT],
                            start=first, stop=last),
                         [(pbuf, pslot * NT, (pslot + 1) * NT),
                          (cst["ones_col"], 0, 1)])], label=lbh + f"k{kc}-ls")
                    prog.mm_group(av, [
                        (lambda e, kc=kc, h=h, pslot=pslot, av=av, first=first,
                         last=(kc == nkc - 1): e.matmul(
                            av.t[0:64, 0:NT],
                            vbuf.t[:, kc * 1024 + h * 64:kc * 1024 + h * 64 + 64],
                            pbuf.t[:, pslot * NT:(pslot + 1) * NT],
                            start=first, stop=last),
                         [(vbuf, kc * 1024 + h * 64, kc * 1024 + h * 64 + 64),
                          (pbuf, pslot * NT, (pslot + 1) * NT)])],
                        label=lbh + f"k{kc}-av")

                emit_qk(0)
                yield
                emit_clmk(0)
                yield
                emit_ex(0)
                yield
                for kc in range(nkc):
                    if kc + 1 < nkc:
                        emit_qk(kc + 1)
                        yield
                        emit_clmk(kc + 1)
                        yield
                    emit_lsav(kc, kc == 0)
                    yield
                    if kc + 1 < nkc:
                        emit_ex(kc + 1)
                        yield
                prog.op("scalar",
                        lambda e, lsum=lsum, xo0=xo0: e.activation(
                            out=xcol.t[0:1, xo0:xo0 + NT], in_=lsum.t[0:1, 0:NT],
                            func=AF.Ln),
                        reads=[(lsum, 0, NT)], writes=[(xcol, xo0, xo0 + NT)],
                        label=lbh + "-rcl")
                yield
                prog.op("scalar",
                        lambda e, xo0=xo0: e.activation(
                            out=xcol.t[0:1, xo0:xo0 + NT],
                            in_=xcol.t[0:1, xo0:xo0 + NT],
                            func=AF.Exp, scale=-1.0),
                        reads=[(xcol, xo0, xo0 + NT)],
                        writes=[(xcol, xo0, xo0 + NT)],
                        label=lbh + "-rce")
                yield
                prog.op("vector",
                        lambda e, xo0=xo0: e.tensor_copy(
                            xcol2.t[0:1, xo0:xo0 + NT],
                            xcol.t[0:1, xo0:xo0 + NT]),
                        reads=[(xcol, xo0, xo0 + NT)],
                        writes=[(xcol2, xo0, xo0 + NT)], label=lbh + "-r16")
                yield
                prog.mm_group(pep, [
                    (lambda e, pep=pep, xo0=xo0: e.matmul(
                        pep.t[0:64, 0:NT], cst["ones_row"].t[0:1, 0:64],
                        xcol2.t[0:1, xo0:xo0 + NT], start=True, stop=True),
                     [(xcol2, xo0, xo0 + NT), (cst["ones_row"], 0, 64)])],
                    label=lbh + "-bc")
                yield
                prog.op("vector",
                        lambda e, pep=pep, cb=cb: e.tensor_copy(
                            cb.t[0:64, 0:NT], pep.t[0:64, 0:NT]),
                        reads=[(pep, 0, NT)], writes=[(cb, 0, NT)],
                        label=lbh + "-bce")
                yield
                prog.op("vector",
                        lambda e, rc=rc, b0=b0, av=av, cb=cb: e.tensor_tensor(
                            obuf.t[b0:b0 + 64, rc * NT:(rc + 1) * NT],
                            cb.t[0:64, 0:NT], av.t[0:64, 0:NT], op=ALU.mult),
                        reads=[(cb, 0, NT), (av, 0, NT)],
                        writes=[(obuf, rc * NT, (rc + 1) * NT)],
                        label=lbh + "-o")
                yield

            for hp in range(8):
                g0 = head_att(2 * hp)
                g1 = head_att(2 * hp + 1)
                d0 = d1 = False
                while not (d0 and d1):
                    if not d0:
                        try:
                            next(g0)
                        except StopIteration:
                            d0 = True
                    if not d1:
                        try:
                            next(g1)
                        except StopIteration:
                            d1 = True
            t0 = sb_ + qt * NT
            prog.dma_batch("qs", [
                (sb3(xresid, 8), x_tile_ap(x_in, t0), [(x_in, t0, t0 + NT)],
                 [(xresid, 0, 8 * NT)]),
            ], label=lbq + "-xr")
            for fc in range(8):
                bk_ = ps["p4" if fc % 2 == 0 else "p5"]
                parts = [
                    (lambda e, rc=rc, fc=fc, bk_=bk_: e.matmul(
                        bk_.t[:],
                        wB.t[:, rc * 1024 + fc * 128:rc * 1024 + fc * 128 + 128],
                        obuf.t[:, rc * NT:(rc + 1) * NT],
                        start=(rc == 0), stop=False),
                     [(wB, rc * 1024 + fc * 128, rc * 1024 + fc * 128 + 128),
                      (obuf, rc * NT, (rc + 1) * NT)])
                    for rc in range(8)]
                parts.append(
                    (lambda e, fc=fc, bk_=bk_: e.matmul(
                        bk_.t[:], w["bo"].t[0:1, fc * 128:(fc + 1) * 128],
                        cst["ones_row"].t[0:1, 0:NT], start=False, stop=True),
                     [(w["bo"], fc * 128, (fc + 1) * 128),
                      (cst["ones_row"], 0, NT)]))
                prog.mm_group(bk_, parts, label=lbq + "-ow")
                prog.op("vector",
                        lambda e, fc=fc, bk_=bk_: e.scalar_tensor_tensor(
                            xresid.t[:, fc * NT:(fc + 1) * NT], bk_.t[:], 0.5,
                            xresid.t[:, fc * NT:(fc + 1) * NT],
                            op0=ALU.mult, op1=ALU.add),
                        reads=[(bk_, 0, NT), (xresid, fc * NT, (fc + 1) * NT)],
                        writes=[(xresid, fc * NT, (fc + 1) * NT)],
                        label=lbq + "-rs")
            prog.dma_batch("qa", [
                (x_tile_ap(x_out, t0), sb3(xresid, 8), [(xresid, 0, 8 * NT)],
                 [(x_out, t0, t0 + NT)]),
            ], label=lbq + "-st")


def host_att_weights(inputs):
    return {
        "w_WqT": np.ascontiguousarray(inputs["L1_attn_Wq"].T.astype(np.float32)),
        "w_WkT": np.ascontiguousarray(inputs["L1_attn_Wk"].T.astype(np.float32)),
        "w_WvT": np.ascontiguousarray(inputs["L1_attn_Wv"].T.astype(np.float32)),
        "w_WoT": np.ascontiguousarray(inputs["L1_attn_Wo"].T.astype(np.float32)),
        "w_bqc": np.ascontiguousarray(
            inputs["L1_attn_bq"].reshape(8, 128).T.astype(np.float32)),
        "w_bkc": np.ascontiguousarray(
            inputs["L1_attn_bk"].reshape(8, 128).T.astype(np.float32)),
        "w_bvr": np.ascontiguousarray(
            inputs["L1_attn_bv"].reshape(1, 1024).astype(np.float32)),
        "w_bor": np.ascontiguousarray(
            inputs["L1_attn_bo"].reshape(1, 1024).astype(np.float32)),
        "c_att": _att_masks(),
    }


def _att_masks():
    m = np.empty((128, 5 * 512), np.float32)
    pk = np.arange(128)[:, None]
    fq = np.arange(512)[None, :]
    for r in range(4):
        rel = r * 128
        m[:, r * 512:(r + 1) * 512] = np.where(pk + rel > fq, -10000.0, 5.0)
    m[:, 4 * 512:] = 5.0
    return m



from contextlib import ExitStack

_PHASE_CACHE = {}


def build_phase(phase):
    if phase in _PHASE_CACHE:
        return _PHASE_CACHE[phase]
    nc = bass.Bass("TRN2", target_bir_lowering=False, debug=False)
    prog = Prog(nc)
    stack = ExitStack()
    with stack:
        env = Env(prog, stack)
        prog.alloc_sems(stack)
        x_in = env.dram("x_in", (D, NTOK), F32, "ExternalInput")
        x_out = env.dram("x_out", (D, NTOK), F32, "ExternalOutput")
        cst = load_consts(prog, env)
        if phase == "moe":
            xs_ln = env.dram("xs_ln", (D, NTOK), BF16, "Internal")
            xs_acc = env.dram("xs_acc", (D, NTOK), F32, "Internal")
            w = stage_moe_weights(prog, env)
            bufs = make_moe_bufs(env)
            for ti in range(NTILES):
                emit_moe_router_tile(prog, cst, w, bufs, x_in, xs_ln, ti)
            emit_moe_experts(prog, cst, w, bufs, x_in, x_out, xs_ln, xs_acc,
                             wbf_scr=None, scratch_mode="none")
        elif phase == "dn":
            w = stage_dn_weights(prog, env)
            bufs = make_dn_bufs(env)
            emit_dn_sublayer(prog, cst, w, bufs, x_in, x_out)
        elif phase == "att":
            w = stage_att_weights(prog, env)
            bufs = make_att_bufs(env)
            emit_att_sublayer(prog, cst, w, bufs, x_in, x_out)
        elif phase == "l1f":
            w = stage_l1ffn_weights(prog, env)
            bufs = make_l1ffn_bufs(env)
            emit_l1ffn_sublayer(prog, cst, w, bufs, x_in, x_out)
        block = stack.enter_context(nc.Block())
        prog.finalize_and_emit(block)
    _PHASE_CACHE[phase] = nc
    return nc


_FUSED_CACHE = []


def build_fused():
    if _FUSED_CACHE:
        return _FUSED_CACHE[0]
    nc = bass.Bass("TRN2", target_bir_lowering=False, debug=False)
    prog = Prog(nc)
    stack = ExitStack()
    with stack:
        env = Env(prog, stack)
        prog.alloc_sems(stack)
        x0 = env.dram("x_in", (D, NTOK), F32, "ExternalInput")
        xs_mid = [env.dram(f"xmid{i}", (D, NTOK), F32, "Internal")
                  for i in range(5)]
        x6 = env.dram("x_out", (D, NTOK), F32, "ExternalOutput")
        chain = [x0] + xs_mid + [x6]
        xs_ln = env.dram("xs_ln", (D, NTOK), BF16, "Internal")
        xs_acc = env.dram("xs_acc", (D, NTOK), F32, "Internal")
        wbf = env.dram("wbf", (8, 128, 65536), BF16, "Internal")
        cst = load_consts(prog, env)
        dr_moe = declare_moe_drams(env)
        dr_dn = declare_dn_drams(env)
        dr_att = declare_att_drams(env)
        dr_l1f = declare_l1ffn_drams(env)

        def ph_moe(xi, xo, mode, tag):
            with ExitStack() as ps_:
                penv = Env(prog, ps_)
                w = stage_moe_weights(prog, penv, dr_moe, tag=tag)
                bufs = make_moe_bufs(penv, tag=tag)

                for ti in range(NTILES):
                    emit_moe_router_tile(prog, cst, w, bufs, xi, xs_ln, ti,
                                         label="mr" + tag)
                emit_moe_experts(prog, cst, w, bufs, xi, xo, xs_ln, xs_acc,
                                 wbf_scr=wbf, scratch_mode=mode,
                                 label="me" + tag)

        def ph_dn(xi, xo):
            with ExitStack() as ps_:
                penv = Env(prog, ps_)
                w = stage_dn_weights(prog, penv, dr_dn)
                bufs = make_dn_bufs(penv)
                emit_dn_sublayer(prog, cst, w, bufs, xi, xo)

        def ph_att(xi, xo):
            with ExitStack() as ps_:
                penv = Env(prog, ps_)
                w = stage_att_weights(prog, penv, dr_att)
                bufs = make_att_bufs(penv)
                emit_att_sublayer(prog, cst, w, bufs, xi, xo)

        def ph_l1f(xi, xo, tag):
            with ExitStack() as ps_:
                penv = Env(prog, ps_)
                w = stage_l1ffn_weights(prog, penv, dr_l1f, tag=tag)
                bufs = make_l1ffn_bufs(penv, tag=tag)
                emit_l1ffn_sublayer(prog, cst, w, bufs, xi, xo,
                                    label="l1f" + tag)

        with ExitStack() as ps0:
            penv0 = Env(prog, ps0)
            w0 = dict(dr_moe)
            emit_moe_cast_prepass(prog, w0, wbf)
        ph_moe(chain[0], chain[1], "read", "A")
        prog.barrier("b1")
        ph_dn(chain[1], chain[2])
        prog.barrier("b2")
        ph_moe(chain[2], chain[3], "read", "B")
        prog.barrier("b3")
        ph_l1f(chain[3], chain[4], "A")
        prog.barrier("b4")
        ph_att(chain[4], chain[5])
        prog.barrier("b5")
        ph_l1f(chain[5], chain[6], "B")
        block = stack.enter_context(nc.Block())
        prog.finalize_and_emit(block)
    _FUSED_CACHE.append(nc)
    return nc


def kernel(**inputs):
    consts = host_consts()
    base = dict(consts)
    base.update(host_moe_weights(inputs))
    base.update(host_dn_weights(inputs))
    base.update(host_att_weights(inputs))
    base.update(host_l1ffn_weights(inputs))
    xs = [host_x_shard(np.asarray(inputs["L0_attn_hidden_states"]), c)
          for c in range(N_CORES)]
    nc = build_fused()
    in_maps = [dict(base, x_in=xs[c]) for c in range(N_CORES)]
    res = run_bass_kernel_spmd(nc, in_maps, core_ids=list(range(N_CORES)))
    xs = [np.asarray(res.results[c]["x_out"]) for c in range(N_CORES)]
    return host_x_unshard(xs)


def kernel_unfused(**inputs):
    consts = host_consts()
    wsets = {
        "moe": host_moe_weights(inputs),
        "dn": host_dn_weights(inputs),
        "att": host_att_weights(inputs),
        "l1f": host_l1ffn_weights(inputs),
    }
    xs = [host_x_shard(np.asarray(inputs["L0_attn_hidden_states"]), c)
          for c in range(N_CORES)]
    for phase in ("moe", "dn", "moe", "l1f", "att", "l1f"):
        nc = build_phase(phase)
        base = dict(consts)
        base.update(wsets[phase])
        in_maps = [dict(base, x_in=xs[c]) for c in range(N_CORES)]
        res = run_bass_kernel_spmd(nc, in_maps, core_ids=list(range(N_CORES)))
        xs = [np.asarray(res.results[c]["x_out"]) for c in range(N_CORES)]
    return host_x_unshard(xs)
